# version 17
# baseline (speedup 1.0000x reference)
import os
import numpy as np
import ml_dtypes
from contextlib import ExitStack
import concourse.bass as bass
import concourse.mybir as mybir
from concourse import dve_ops
from concourse.bass_utils import run_bass_kernel_spmd

F32 = mybir.dt.float32
BF16 = mybir.dt.bfloat16
AF = mybir.ActivationFunctionType
ALU = mybir.AluOpType

B_LOC = 2
T = 2048
E = 1024
H = 16
D = 64
LAT = 512
FF = 4096
QC = 512
NQC = T // QC
EPS = 1e-5
SET_LNEXP = 6
SET_GELU = 11
N_CORES = 8


class Buf:
    def __init__(self):
        self.ready = []
        self.users = []

    def begin_write(self):
        w = self.users + self.ready
        self.ready = []
        self.users = []
        return w


class Sched:
    def __init__(self, nc, semnames):
        self.nc = nc
        self.sems = {n: nc.alloc_semaphore(n) for n in semnames}
        self.cnt = {n: 0 for n in semnames}
        self.ops = None

    def new_phase(self):
        self.ops = []

    def add(self, engine, fn, waits=(), incs=()):
        norm = []
        for inc in incs:
            sem, n = inc if isinstance(inc, tuple) else (inc, 1)
            self.cnt[sem] += n
            norm.append((sem, n))
        wm = {}
        for w in waits:
            if w is None:
                continue
            sem, val = w
            if val > 0:
                wm[sem] = max(wm.get(sem, 0), val)
        self.ops.append((engine, tuple(sorted(wm.items())), tuple(norm), fn))
        return (norm[0][0], self.cnt[norm[0][0]]) if norm else None

    def emit_phase(self):
        ops = self.ops
        self.ops = None
        with self.nc.Block() as blk:
            decos = {"te": blk.tensor, "se": blk.scalar, "ve": blk.vector,
                     "gp": blk.gpsimd, "sp": blk.sync}
            for name in ("te", "se", "ve", "gp", "sp"):
                myops = [(w, i, f) for (e, w, i, f) in ops if e == name]
                if not myops:
                    continue

                def make_body(myops=myops):
                    def body(eng):
                        seen = {}
                        for waits, incs, fn in myops:
                            for sem, val in waits:
                                if seen.get(sem, 0) < val:
                                    eng.wait_ge(self.sems[sem], val)
                                    seen[sem] = val
                            if fn is None:
                                continue
                            inst = fn(eng)
                            for sem, n in incs:
                                inst.then_inc(self.sems[sem], n)
                    return body

                decos[name](make_body())


def build_program():
    nc = bass.Bass("TRN2", target_bir_lowering=False)

    x_in = nc.dram_tensor("x", [B_LOC, T, E], F32, kind="ExternalInput")
    wq_in = nc.dram_tensor("wq", [E, E], F32, kind="ExternalInput")
    wd_in = nc.dram_tensor("wd", [E, LAT], F32, kind="ExternalInput")
    wk_in = nc.dram_tensor("wk", [LAT, E], F32, kind="ExternalInput")
    wv_in = nc.dram_tensor("wv", [LAT, E], F32, kind="ExternalInput")
    wo_in = nc.dram_tensor("wo", [E, E], F32, kind="ExternalInput")
    wr_in = nc.dram_tensor("wr", [E, 8], F32, kind="ExternalInput")
    w1_in = nc.dram_tensor("w1", [E, FF], F32, kind="ExternalInput")
    w2_in = nc.dram_tensor("w2", [FF, E], F32, kind="ExternalInput")
    b1_in = nc.dram_tensor("b1l", [128, FF // 128], F32, kind="ExternalInput")
    b2_in = nc.dram_tensor("b2b", [128, E], F32, kind="ExternalInput")
    ct_in = nc.dram_tensor("ctab", [128, T], F32, kind="ExternalInput")
    st_in = nc.dram_tensor("stab", [128, T], F32, kind="ExternalInput")
    psw_in = nc.dram_tensor("pswap", [128, 128], F32, kind="ExternalInput")
    out_d = nc.dram_tensor("out", [B_LOC, T, E], F32, kind="ExternalOutput")

    xb_scr = [nc.dram_tensor(f"xb_scr{b}", [T, E], BF16, kind="Internal") for b in range(B_LOC)]
    q_scr = [nc.dram_tensor(f"q_scr{b}", [NQC, 8, 128, QC], BF16, kind="Internal") for b in range(B_LOC)]
    x2f_scr = [nc.dram_tensor(f"x2f_scr{b}", [T, E], F32, kind="Internal") for b in range(B_LOC)]
    x2b_scr = [nc.dram_tensor(f"x2b_scr{b}", [T, E], BF16, kind="Internal") for b in range(B_LOC)]

    psS = nc.alloc_psum_tensor("psS", [128, 2048], F32)
    psC = [nc.alloc_psum_tensor(f"psC{i}", [128, 512], F32) for i in range(4)]

    S = Sched(nc, ["dma_w", "dma_t", "dma_xg", "dma_xt", "dma_ct0", "dma_ct1",
                   "dma_st0", "dma_st1", "dma_qs0", "dma_qs1", "dma_qch0",
                   "dma_xr0", "dma_xr1", "dma_xf0", "dma_xf1", "dma_xb0",
                   "dma_xb1", "dma_w10", "dma_w11", "dma_x2t", "dma_fr0",
                   "dma_fr1", "dma_b1", "dma_sel", "dma_qch1", "dma_o0", "dma_o1", "te", "se", "ve", "gp"])
    bufs = {}

    def B(name):
        if name not in bufs:
            bufs[name] = Buf()
        return bufs[name]

    slots8 = [("psS0", psS, 0, 512), ("psS1", psS, 512, 512),
              ("psS2", psS, 1024, 512), ("psS3", psS, 1536, 512),
              ("psC0", psC[0], 0, 512), ("psC1", psC[1], 0, 512),
              ("psC2", psC[2], 0, 512), ("psC3", psC[3], 0, 512)]

    class Rot:
        def __init__(self, slots):
            self.slots = slots
            self.i = 0

        def next(self):
            s = self.slots[self.i % len(self.slots)]
            self.i += 1
            return s

    def mm_group(slot, pieces, extra_waits=(), reads=()):
        name, tensor, coloff, width = slot
        buf = B(name)
        war = buf.begin_write()
        ap = tensor[:, coloff:coloff + width]
        n = len(pieces)
        tok = None
        for j, (lhs, rhs, ws) in enumerate(pieces):
            waits = list(ws)
            if j == 0:
                waits += war + list(extra_waits)
            tok = S.add(
                "te",
                lambda e, ap=ap, lhs=lhs, rhs=rhs, st=(j == 0), sp=(j == n - 1):
                    e.matmul(ap, lhs, rhs, start=st, stop=sp),
                waits=waits,
                incs=("te",) if j == n - 1 else ())
        buf.ready = [tok]
        for rb in reads:
            rb.users.append(tok)
        return ap, tok, buf

    def dma(queue, out_ap, in_ap, waits, sem, buf=None):
        waits = list(waits)
        if queue == "gp":
            waits.append((sem, S.cnt[sem]))
        tok = S.add(queue, lambda e, o=out_ap, i=in_ap: e.dma_start(out=o, in_=i),
                    waits=waits, incs=[(sem, 16)])
        if buf is not None:
            buf.ready = [tok]
        return tok

    def load_set(set_id):
        if os.environ.get("KDBG_NOACT"):
            return
        S.add("se", lambda e, s=set_id: e.add_instruction(
            mybir.InstLoadActFuncSet(
                name=e.bass.get_next_instruction_name(),
                act_func_set_id=s, ins=[], outs=[])))

    def drain_dmas():
        S.add("sp", None, waits=[(n, S.cnt[n]) for n in S.sems if n.startswith("dma_")])

    xt_prefetch = set()

    def proj_phase(b, tens):
        (wq_sb, wd_sb, wk_sb, wv_sb, ct_sb, st_sb, kr_sb, vaug_sb,
         xt_buf, lat_buf, qsp_buf, pswap_sb, aux, qbf_tmp, t1_tmp,
         t2_tmp) = tens
        rot = Rot(slots8)

        for sbuf, dram, nkc in ((wq_sb, wq_in, 8),
                                (wk_sb, wk_in, 4), (wv_sb, wv_in, 4)):
            bb = B(sbuf.name)
            dma("gp", sbuf[:].rearrange("p (kc m) -> p kc m", kc=nkc),
                dram.rearrange("(kc p) m -> p kc m", p=128),
                bb.begin_write(), "dma_w", bb)
        load_set(SET_LNEXP)

        xbb = B(f"xb{b}")
        if not xbb.ready:
            dma("gp", xb_scr[b][:], x_in[b], xbb.begin_write(), "dma_xg", xbb)

        kr = B(kr_sb.name)
        kr_war = kr.begin_write()
        va = B(vaug_sb.name)
        va_war = va.begin_write()
        ones_tok = S.add("gp", lambda e: e.memset(
            vaug_sb[:].rearrange("p (tt h d) -> p tt h d", tt=16, h=H)[:, :, :, 64:128], 1.0),
            waits=va_war, incs=("gp",))
        va.ready.append(ones_tok)
        qsc = B(f"qsc{b}")
        qsc_war = qsc.begin_write()
        if os.environ.get("KDBG_STAGE") == "setup":
            drain_dmas()
            return

        for tc in range(NQC):
            skip_xt = (tc == 0 and b in xt_prefetch)
            ctb = B(f"ct{tc % 2}")
            toff = (tc % 2) * 512
            cttok = dma("sp", ct_sb[:, toff:toff + 512], ct_in[:, tc * 512:(tc + 1) * 512],
                        ctb.begin_write(), f"dma_ct{tc % 2}", ctb)
            stb = B(f"st{tc % 2}")
            sttok = dma("sp", st_sb[:, toff:toff + 512], st_in[:, tc * 512:(tc + 1) * 512],
                        stb.begin_write(), f"dma_st{tc % 2}", stb)
            xtb = B("xt")
            xoff = 0
            if not skip_xt:
                xt_war = xtb.begin_write()
                xt_toks = []
                for et in range(8):
                    tok = S.add("sp", lambda e, b=b, tc=tc, et=et, xoff=xoff: e.dma_start(
                        out=xt_buf[:, xoff + et * 512: xoff + et * 512 + 512],
                        in_=xb_scr[b][tc * 512:(tc + 1) * 512, et * 128:(et + 1) * 128],
                        transpose=True),
                        waits=(xt_war if et == 0 else []) + xbb.ready,
                        incs=[("dma_xt", 16)])
                    xt_toks.append(tok)
                xtb.ready = xt_toks

            def xt_rhs(kc, xoff=xoff):
                return xt_buf[:, xoff + kc * 512: xoff + kc * 512 + 512]

            latb = B(f"lat{tc % 2}")
            lat_war = latb.begin_write()
            lat_toks = []
            loff = (tc % 2) * 2048
            for lt in range(4):
                pieces = [(wd_sb[:, kc * LAT + lt * 128: kc * LAT + lt * 128 + 128],
                           xt_rhs(kc),
                           B(wd_sb.name).ready + xtb.ready) for kc in range(8)]
                ap, mtok, pbuf = mm_group(rot.next(), pieces, extra_waits=lat_war if lt == 0 else (),
                                          reads=[B(wd_sb.name), xtb])
                etok = S.add("se", lambda e, ap=ap, lt=lt, loff=loff: e.activation(
                    lat_buf[:, loff + lt * 512: loff + lt * 512 + 512], ap, AF.Copy),
                    waits=[mtok], incs=("se",))
                pbuf.users.append(etok)
                lat_toks.append(etok)
            latb.ready = lat_toks

            if os.environ.get("KDBG_STAGE") == "proj1":
                continue
            for which in ("q", "k"):
                for mc in range(8):
                    if which == "q":
                        pieces = [(wq_sb[:, kc * E + mc * 128: kc * E + mc * 128 + 128],
                                   xt_rhs(kc),
                                   B(wq_sb.name).ready + xtb.ready) for kc in range(8)]
                    else:
                        pieces = [(wk_sb[:, kc * E + mc * 128: kc * E + mc * 128 + 128],
                                   lat_buf[:, loff + kc * 512: loff + kc * 512 + 512],
                                   B(wk_sb.name).ready + latb.ready) for kc in range(4)]
                    ap, mtok, pbuf = mm_group(rot.next(), pieces,
                                              reads=[B(wq_sb.name), xtb] if which == "q" else [B(wk_sb.name), latb])
                    sl = mc % 2
                    qb = B(f"qbf{sl}")
                    qb_war = qb.begin_write()
                    qtok = S.add("se", lambda e, ap=ap, sl=sl: e.activation(
                        qbf_tmp[:, sl * 512:sl * 512 + 512], ap, AF.Copy),
                        waits=[mtok] + qb_war, incs=("se",))
                    qb.ready = [qtok]
                    pbuf.users.append(qtok)
                    if os.environ.get("KDBG_STAGE") == "proj2a":
                        continue

                    swslot = rot.next()
                    swname, swt, swoff, _ = swslot
                    swbuf = B(swname)
                    sw_war = swbuf.begin_write()
                    swap = swt[:, swoff:swoff + 512]
                    swtok = S.add("te", lambda e, swap=swap, sl=sl: e.matmul(
                        swap, pswap_sb[:], qbf_tmp[:, sl * 512:sl * 512 + 512],
                        start=True, stop=True),
                        waits=[qtok] + sw_war + B(pswap_sb.name).ready, incs=("te",))
                    swbuf.ready = [swtok]
                    qb.users.append(swtok)

                    if os.environ.get("KDBG_STAGE") == "proj2b":
                        continue
                    t1b = B(f"t1{sl}")
                    t1_war = t1b.begin_write()
                    t1tok = S.add("ve", lambda e, ap=ap, sl=sl, toff=toff: e.tensor_tensor(
                        t1_tmp[:, sl * 512:sl * 512 + 512], ap,
                        ct_sb[:, toff:toff + 512], ALU.mult),
                        waits=[mtok, qtok, cttok] + t1_war, incs=("ve",))
                    ctb.users.append(t1tok)
                    t1b.ready = [t1tok]
                    pbuf.users.append(t1tok)
                    if os.environ.get("KDBG_STAGE") == "proj2d":
                        t1b.users.append(t1tok)
                        continue

                    t2b = B(f"t2{sl}")
                    t2_war = t2b.begin_write()
                    t2tok = S.add("ve", lambda e, swap=swap, sl=sl, toff=toff: e.tensor_tensor(
                        t2_tmp[:, sl * 512:sl * 512 + 512], swap,
                        st_sb[:, toff:toff + 512], ALU.mult),
                        waits=[swtok, sttok] + t2_war, incs=("ve",))
                    stb.users.append(t2tok)
                    t2b.ready = [t2tok]
                    swbuf.users.append(t2tok)

                    if which == "k":
                        ktok = S.add("gp", lambda e, sl=sl, mc=mc, tc=tc: e.tensor_tensor(
                            kr_sb[:, mc * T + tc * 512: mc * T + tc * 512 + 512],
                            t1_tmp[:, sl * 512:sl * 512 + 512],
                            t2_tmp[:, sl * 512:sl * 512 + 512], ALU.add),
                            waits=[t1tok, t2tok] + (kr_war if (mc == 0 and tc == 0) else []),
                            incs=("gp",))
                        kr.ready.append(ktok)
                        t1b.users.append(ktok)
                        t2b.users.append(ktok)
                    else:
                        qsb = B(f"qsp{sl}")
                        qs_war = qsb.begin_write()
                        qrtok = S.add("gp", lambda e, sl=sl: e.tensor_tensor(
                            qsp_buf[:, sl * 512:sl * 512 + 512],
                            t1_tmp[:, sl * 512:sl * 512 + 512],
                            t2_tmp[:, sl * 512:sl * 512 + 512], ALU.add),
                            waits=[t1tok, t2tok] + qs_war, incs=("gp",))
                        t1b.users.append(qrtok)
                        t2b.users.append(qrtok)
                        qsb.ready = [qrtok]
                        if os.environ.get("KDBG_STAGE") == "proj2c":
                            qsb.users.append(qrtok)
                            continue
                        sptok = S.add("sp", lambda e, b=b, tc=tc, mc=mc, sl=sl: e.dma_start(
                            out=q_scr[b][tc, mc], in_=qsp_buf[:, sl * 512:sl * 512 + 512]),
                            waits=[qrtok] + (qsc_war if (mc == 0 and tc == 0) else []),
                            incs=[(f"dma_qs{sl}", 16)])
                        qsb.users.append(sptok)
                        qsc.ready.append(sptok)

            if os.environ.get("KDBG_STAGE") == "proj2":
                continue
            for tt4 in range(4):
                tt = tc * 4 + tt4
                for ec in range(2):
                    pieces = [(lat_buf[:, loff + lc * 512 + tt4 * 128: loff + lc * 512 + tt4 * 128 + 128],
                               wv_sb[:, lc * E + ec * 512: lc * E + ec * 512 + 512],
                               latb.ready + B(wv_sb.name).ready) for lc in range(4)]
                    ap, mtok, pbuf = mm_group(rot.next(), pieces, reads=[latb, B(wv_sb.name)])
                    vtok = S.add("ve", lambda e, ap=ap, tt=tt, ec=ec: e.tensor_copy(
                        vaug_sb[:].rearrange("p (tt h d) -> p tt h d", tt=16, h=H)
                        [:, tt, 8 * ec:8 * ec + 8, 0:64],
                        ap.rearrange("p (h d) -> p h d", h=8)),
                        waits=[mtok], incs=("ve",))
                    pbuf.users.append(vtok)
                    va.ready.append(vtok)

        drain_dmas()

    def emit_x2t(b, hf, x2t_sb):
        x2t = B("x2t")
        x2t_war = x2t.begin_write()
        toks = []
        for et in range(8):
            tok = S.add("sp", lambda e, b=b, hf=hf, et=et, x2t_sb=x2t_sb: e.dma_start(
                out=x2t_sb[:, et * 1024:(et + 1) * 1024],
                in_=x2b_scr[b][hf * 1024:(hf + 1) * 1024, et * 128:(et + 1) * 128],
                transpose=True),
                waits=(x2t_war if et == 0 else []) + B(f"x2bs{b}_h{hf}").ready,
                incs=[("dma_x2t", 16)])
            toks.append(tok)
        x2t.ready = toks
        return x2t

    def attn_phase(b, tens):
        (wo_sb, kr_sb, vaug_sb, qch_buf, pt_buf, rb_sb, rb_scr, ao_sb,
         xres_buf, rs1_buf, x2t_sb, aux) = tens
        if os.environ.get("KDBG_STAGE", "").startswith("proj"):
            drain_dmas()
            return
        load_set(SET_LNEXP)
        bb = B(wo_sb.name)
        dma("gp", wo_sb[:].rearrange("p (kc m) -> p kc m", kc=8),
            wo_in.rearrange("(kc p) m -> p kc m", p=128),
            bb.begin_write(), "dma_w", bb)
        kr = B(kr_sb.name)
        va = B(vaug_sb.name)
        qsc = B(f"qsc{b}")
        def load_qch(c):
            qch = B(f"qch{c % 2}")
            q_war = qch.begin_write()
            qoff = (c % 2) * 8 * QC
            dma("sp", qch_buf[:, qoff:qoff + 8 * QC].rearrange("p (kc n) -> p kc n", kc=8),
                q_scr[b][c].rearrange("kc p n -> p kc n"),
                q_war + qsc.ready, f"dma_qch{c % 2}", qch)
            return qch

        load_qch(0)
        for c in range(NQC):
            qch = B(f"qch{c % 2}")
            qoff = (c % 2) * 8 * QC
            if c + 1 < NQC:
                load_qch(c + 1)
            nkc = 4 * (c + 1)
            ao = B(ao_sb.name)
            ao_war = ao.begin_write()
            for hp in range(8):
                par = hp % 2
                avA_ap = psC[0 + 2 * par]
                avB_ap = psC[1 + 2 * par]
                avA = B(f"psC{0 + 2 * par}")
                avB = B(f"psC{1 + 2 * par}")
                avA_war = avA.begin_write()
                avB_war = avB.begin_write()
                av_last = None
                pt_toks = {}

                def emit_S(kc, hp=hp, c=c, qoff=qoff):
                    snames = ("psS0", "psS1") if (kc % 2 == 0) else ("psS2", "psS3")
                    ao_base = (kc % 2) * 1024
                    sb0, sb1 = B(snames[0]), B(snames[1])
                    s_war = sb0.begin_write() + sb1.begin_write()
                    mts = []
                    for half in range(2):
                        base = half * 64
                        colo = ao_base + half * 512
                        tok = S.add("te", lambda e, colo=colo, base=base, hp=hp, kc=kc, qoff=qoff:
                                    e.matmul(
                                        psS[:, colo:colo + 512],
                                        kr_sb[base:base + 64, hp * T + kc * 128: hp * T + kc * 128 + 128],
                                        qch_buf[base:base + 64, qoff + hp * QC: qoff + hp * QC + QC],
                                        start=True, stop=True, tile_position=(base, 0)),
                                    waits=(s_war + kr.ready + qch.ready) if half == 0 else [],
                                    incs=("te",))
                        mts.append(tok)
                    sb0.ready = [mts[0]]
                    sb1.ready = [mts[1]]
                    kr.users.append(mts[1])
                    qch.users.append(mts[1])
                    masked = (kc >= 4 * c + 2)
                    ptb = B(f"pt{kc % 4}")
                    pt_war = ptb.begin_write()
                    po = (kc % 4) * 1024
                    if not masked:
                        etok = S.add("se", lambda e, po=po, ao_base=ao_base: e.activation(
                            pt_buf[:, po:po + 1024], psS[:, ao_base:ao_base + 1024],
                            AF.Exp, scale=0.125),
                            waits=mts + pt_war, incs=("se",))
                        ptoks = [etok]
                    else:
                        ztok = S.add("ve", lambda e, po=po: e.memset(
                            pt_buf[:, po:po + 1024].rearrange("p (h n) -> p h n", h=2)[:, :, 0:256], 0.0),
                            waits=pt_war, incs=("ve",))
                        etok = S.add("se", lambda e, po=po, ao_base=ao_base: e.activation(
                            pt_buf[:, po:po + 1024].rearrange("p (h n) -> p h n", h=2)[:, :, 256:512],
                            psS[:, ao_base:ao_base + 1024].rearrange("p (h n) -> p h n", h=2)[:, :, 256:512],
                            AF.Exp, scale=0.125),
                            waits=mts + [ztok], incs=("se",))
                        ptoks = [etok, ztok]
                    ptb.ready = ptoks
                    sb0.users.append(etok)
                    sb1.users.append(etok)
                    pt_toks[kc] = ptoks

                def emit_AV(kc, hp=hp):
                    nonlocal av_last
                    po = (kc % 4) * 1024
                    ptb = B(f"pt{kc % 4}")
                    tok = None
                    for half in range(2):
                        av_ap = avA_ap if half == 0 else avB_ap
                        tok = S.add("te", lambda e, av_ap=av_ap, hp=hp, half=half, kc=kc, po=po, nkc=nkc:
                                    e.matmul(
                                        av_ap[:, :],
                                        vaug_sb[:, kc * (H * 128) + (2 * hp + half) * 128:
                                                kc * (H * 128) + (2 * hp + half) * 128 + 128],
                                        pt_buf[:, po + half * 512: po + half * 512 + 512],
                                        start=(kc == 0), stop=(kc == nkc - 1)),
                                    waits=pt_toks[kc] + va.ready +
                                          ((avA_war if half == 0 else avB_war) if kc == 0 else []),
                                    incs=("te",) if half == 1 else ())
                    ptb.users.append(tok)
                    va.users.append(tok)
                    if kc == nkc - 1:
                        av_last = tok

                emit_S(0)
                for kc in range(nkc):
                    if kc + 1 < nkc:
                        emit_S(kc + 1)
                    emit_AV(kc)
                avA.ready = [av_last]
                avB.ready = [av_last]

                rbs = B(f"rb{par}")
                rb_war = rbs.begin_write()
                rtoks = []
                for half in range(2):
                    av_ap = avA_ap if half == 0 else avB_ap
                    ro = (2 * par + half) * 512
                    if half == 0:
                        rtok = S.add("ve", lambda e, av_ap=av_ap, ro=ro:
                                     e.reciprocal(rb_sb[0:64, ro:ro + 512], av_ap[64:128, :]),
                                     waits=[av_last] + rb_war, incs=("ve",))
                    else:
                        scr = rb_scr[0:64, par * 512:par * 512 + 512]
                        scb = B(f"rbscr{par}")
                        scr_war = scb.begin_write()
                        ltok = S.add("se", lambda e, av_ap=av_ap, scr=scr:
                                     e.activation(scr, av_ap[64:128, :], AF.Ln),
                                     waits=[av_last] + scr_war, incs=("se",))
                        scb.ready = [ltok]
                        rtok = S.add("se", lambda e, ro=ro, scr=scr:
                                     e.activation(rb_sb[0:64, ro:ro + 512], scr, AF.Exp, scale=-1.0),
                                     waits=[ltok], incs=("se",))
                        scb.users.append(rtok)
                    rtoks.append(rtok)
                rbs.ready = rtoks
                avA.users.append(rtoks[0])
                avB.users.append(rtoks[1])
                for half in range(2):
                    avbuf = avA if half == 0 else avB
                    av_ap = avA_ap if half == 0 else avB_ap
                    atok = S.add("ve", lambda e, hp=hp, half=half, par=par, av_ap=av_ap: e.tensor_tensor(
                        ao_sb[half * 64:half * 64 + 64, hp * 512:hp * 512 + 512],
                        av_ap[0:64, :],
                        rb_sb[0:64, (2 * par + half) * 512:(2 * par + half) * 512 + 512], ALU.mult),
                        waits=[rtoks[half]] + (ao_war if (hp == 0 and half == 0) else []),
                        incs=("ve",))
                    avbuf.users.append(atok)
                    rbs.users.append(atok)
                    ao.ready.append(atok)
            for tb in range(4):
                trow = c * 512 + tb * 128
                sl = tb % 2
                xr = B(f"xres{sl}")
                xr_war = xr.begin_write()
                xtok = dma("sp", xres_buf[:, sl * E:(sl + 1) * E],
                           x_in[b, trow:trow + 128, :], xr_war, f"dma_xr{sl}", xr)
                rs = B(f"rs1{sl}")
                rs_war = rs.begin_write()
                rs.ready = []
                for ec in range(2):
                    slot = ("psC3", psC[3], 0, 512) if ec == 0 else ("psC2", psC[2], 0, 512)
                    pieces = [(ao_sb[:, kc * 512 + tb * 128: kc * 512 + tb * 128 + 128],
                               wo_sb[:, kc * E + ec * 512: kc * E + ec * 512 + 512],
                               ao.ready + B(wo_sb.name).ready) for kc in range(8)]
                    ap, mtok, pbuf = mm_group(slot, pieces, reads=[ao, B(wo_sb.name)])
                    rtok = S.add("ve", lambda e, ap=ap, sl=sl, ec=ec: e.tensor_tensor(
                        rs1_buf[:, sl * E + ec * 512: sl * E + ec * 512 + 512],
                        ap, xres_buf[:, sl * E + ec * 512: sl * E + ec * 512 + 512], ALU.add),
                        waits=[mtok, xtok] + (rs_war if ec == 0 else []), incs=("ve",))
                    pbuf.users.append(rtok)
                    xr.users.append(rtok)
                    rs.ready.append(rtok)
                rstd_ap, nmr_ap, st_toks = ln_stats(rs.ready, rs1_buf[:, sl * E:(sl + 1) * E], sl, aux)
                xr2_war = xr.begin_write()
                a1 = S.add("ve", lambda e, sl=sl, r=rstd_ap, nm=nmr_ap: e.tensor_scalar(
                    xres_buf[:, sl * E:(sl + 1) * E], rs1_buf[:, sl * E:(sl + 1) * E],
                    r, nm, ALU.mult, op1=ALU.add),
                    waits=st_toks + rs.ready + xr2_war, incs=("ve",))
                xr.ready = [a1]
                rs.users.append(a1)
                sp1 = S.add("sp", lambda e, b=b, trow=trow, sl=sl: e.dma_start(
                    out=x2f_scr[b][trow:trow + 128, :], in_=xres_buf[:, sl * E:(sl + 1) * E]),
                    waits=[a1], incs=[(f"dma_xf{sl}", 16)])
                xr.users.append(sp1)
                B(f"x2fs{b}").ready.append(sp1)
                B(f"x2fs{b}_h{trow // 1024}").ready.append(sp1)
            if c in (1, 3):
                hf_ = (c - 1) // 2
                xc = B(f"x2bs{b}_h{hf_}")
                ctok = S.add("gp", lambda e, b=b, hf_=hf_: e.dma_start(
                    out=x2b_scr[b][hf_ * 1024:(hf_ + 1) * 1024, :],
                    in_=x2f_scr[b][hf_ * 1024:(hf_ + 1) * 1024, :]),
                    waits=xc.begin_write() + B(f"x2fs{b}_h{hf_}").ready +
                          [(f"dma_xb{hf_}", S.cnt[f"dma_xb{hf_}"])],
                    incs=[(f"dma_xb{hf_}", 16)])
                xc.ready = [ctok]
            if c == 1:
                emit_x2t(b, 0, x2t_sb)
        drain_dmas()

    def ln_stats(src_toks, src_ap, sl, aux):
        so = sl * 24
        st = B(f"lnst{sl}")
        st_war = st.begin_write()
        t_b0 = S.add("ve", lambda e, src=src_ap, so=so: e.bn_stats(
            aux[:, so:so + 6], src[:, 0:512]),
            waits=src_toks + st_war, incs=("ve",))
        t_b1 = S.add("ve", lambda e, src=src_ap, so=so: e.bn_stats(
            aux[:, so + 6:so + 12], src[:, 512:1024]),
            waits=src_toks, incs=("ve",))
        t_ag = S.add("ve", lambda e, so=so: e.bn_aggr(aux[:, so + 12:so + 14], aux[:, so:so + 12]),
                     waits=[t_b0, t_b1], incs=("ve",))
        t_ln = S.add("se", lambda e, so=so: e.activation(
            aux[:, so + 14:so + 15], aux[:, so + 13:so + 14], AF.Ln, bias=aux[:, 255:256]),
            waits=[t_ag] + B("epsap").ready, incs=("se",))
        t_rs = S.add("se", lambda e, so=so: e.activation(
            aux[:, so + 15:so + 16], aux[:, so + 14:so + 15], AF.Exp, scale=-0.5),
            waits=[t_ln], incs=("se",))
        t_mr = S.add("ve", lambda e, so=so: e.tensor_tensor(
            aux[:, so + 16:so + 17], aux[:, so + 12:so + 13], aux[:, so + 15:so + 16], ALU.mult),
            waits=[t_ag, t_rs], incs=("ve",))
        t_nm = S.add("ve", lambda e, so=so: e.tensor_scalar(
            aux[:, so + 17:so + 18], aux[:, so + 16:so + 17], -1.0, None, ALU.mult),
            waits=[t_mr], incs=("ve",))
        st.ready = [t_nm, t_rs]
        return aux[:, so + 15:so + 16], aux[:, so + 17:so + 18], [t_nm, t_rs]

    def ffn_phase(b, tens):
        if os.environ.get("KDBG_STAGE") in ("proj", "attn"):
            drain_dmas()
            return
        (w2_sb, x2t_sb, h_sb, w1_buf, b2b_sb, x2fr_buf, t1y_buf, t2y_buf,
         outb_buf, aux) = tens
        w2b = B(w2_sb.name)
        b2b = B(b2b_sb.name)
        dma("sp", b2b_sb[:], b2_in[:], b2b.begin_write(), "dma_t", b2b)
        w2_pending = [True]
        xb_pending = [b + 1 < B_LOC]

        for hf in range(2):
            x2t = B("x2t")

            load_set(SET_GELU)
            hb = B(h_sb.name)
            h_war = hb.begin_write()
            hb.ready = []
            hslots = [("psS0", psS, 0, 512), ("psS1", psS, 512, 512),
                      ("psS2", psS, 1024, 512), ("psS3", psS, 1536, 512),
                      ("psC0", psC[0], 0, 512), ("psC1", psC[1], 0, 512),
                      ("psC2", psC[2], 0, 512), ("psC3", psC[3], 0, 512)]
            hrot = Rot(hslots)
            for ft in range(FF // 128):
                wsl = ft % 2
                w1b = B(f"w1{wsl}")
                w1_war = w1b.begin_write()
                wtok = S.add("gp", lambda e, ft=ft, wsl=wsl: e.dma_start(
                    out=w1_buf[:, wsl * 1024:(wsl + 1) * 1024]
                    .rearrange("p (kc m) -> p kc m", kc=8),
                    in_=w1_in[:, ft * 128:(ft + 1) * 128]
                    .rearrange("(kc p) m -> p kc m", p=128)),
                    waits=list(w1_war) + [(f"dma_w1{wsl}", S.cnt[f"dma_w1{wsl}"])],
                    incs=[(f"dma_w1{wsl}", 16)])
                w1b.ready = [wtok]
                if ft >= 4 and w2_pending[0]:
                    w2_pending[0] = False
                    dma("gp", w2_sb[:].rearrange("p (fc m) -> p fc m", fc=32),
                        w2_in.rearrange("(fc p) m -> p fc m", p=128),
                        w2b.begin_write(), "dma_w", w2b)
                if ft >= 8 and xb_pending[0]:
                    xb_pending[0] = False
                    xbn = B(f"xb{b + 1}")
                    dma("gp", xb_scr[b + 1][:], x_in[b + 1], xbn.begin_write(),
                        "dma_xg", xbn)
                    xt_prefetch.add(b + 1)
                slots2 = [hrot.next(), hrot.next()]
                bufs2 = [B(s[0]) for s in slots2]
                wars2 = [bb.begin_write() for bb in bufs2]
                aps2 = [s[1][:, s[2]:s[2] + s[3]] for s in slots2]
                mtoks2 = [None, None]
                for kc in range(8):
                    for tc2 in range(2):
                        w_ = list(([wtok] + x2t.ready))
                        if kc == 0:
                            w_ += wars2[tc2]
                            if ft == 0 and tc2 == 0:
                                w_ += list(h_war)
                        mtoks2[tc2] = S.add(
                            "te",
                            lambda e, ap=aps2[tc2], l=w1_buf[:, wsl * 1024 + kc * 128: wsl * 1024 + kc * 128 + 128],
                            r=x2t_sb[:, kc * 1024 + tc2 * 512: kc * 1024 + tc2 * 512 + 512],
                            st=(kc == 0), sp=(kc == 7): e.matmul(ap, l, r, start=st, stop=sp),
                            waits=w_, incs=("te",) if kc == 7 else ())
                for tc2 in range(2):
                    bufs2[tc2].ready = [mtoks2[tc2]]
                    w1b.users.append(mtoks2[tc2])
                    x2t.users.append(mtoks2[tc2])
                    gtok = S.add("se", lambda e, ap=aps2[tc2], ft=ft, tc2=tc2: e.activation(
                        h_sb[:, ft * 1024 + tc2 * 512: ft * 1024 + tc2 * 512 + 512], ap,
                        AF.Gelu_apprx_tanh, bias=b1_sb_global[:, ft:ft + 1]),
                        waits=[mtoks2[tc2]] + B(b1_sb_global.name).ready, incs=("se",))
                    bufs2[tc2].users.append(gtok)
                    hb.ready.append(gtok)

            load_set(SET_LNEXP)
            fac = B("fac")
            fac_war = fac.begin_write()
            fac_toks = []
            for tbl in range(8):
                rbslot = ("psC0", psC[0], 0, 512)
                pieces = [(x2t_sb[:, kc * 1024 + tbl * 128: kc * 1024 + tbl * 128 + 128],
                           wr_sb_global[:, kc * 8:(kc + 1) * 8],
                           x2t.ready + B(wr_sb_global.name).ready) for kc in range(8)]
                name, tensor, coloff, _ = rbslot
                pbufR = B(name)
                warR = pbufR.begin_write()
                apR = tensor[:, 0:8]
                mtokR = None
                for j, (lhs, rhs, ws) in enumerate(pieces):
                    mtokR = S.add("te", lambda e, apR=apR, lhs=lhs, rhs=rhs,
                                  st=(j == 0), sp=(j == 7): e.matmul(apR, lhs, rhs, start=st, stop=sp),
                                  waits=(list(ws) + warR) if j == 0 else list(ws),
                                  incs=("te",) if j == 7 else ())
                pbufR.ready = [mtokR]
                x2t.users.append(mtokR)
                B(wr_sb_global.name).users.append(mtokR)
                ro = 32 + tbl * 2
                lgc = 64 + (tbl % 2) * 32
                t0 = S.add("ve", lambda e, lgc=lgc: e.tensor_copy(aux[:, lgc:lgc + 8], psC[0][:, 0:8]),
                           waits=[mtokR] + B(f"rt{tbl % 2}").begin_write(), incs=("ve",))
                pbufR.users.append(t0)
                t1 = S.add("ve", lambda e, lgc=lgc: e.reduce_max(
                    aux[:, lgc + 8:lgc + 9], aux[:, lgc:lgc + 8], axis=mybir.AxisListType.X),
                    waits=[t0], incs=("ve",))
                t2 = S.add("ve", lambda e, lgc=lgc: e.tensor_scalar(
                    aux[:, lgc + 9:lgc + 17], aux[:, lgc:lgc + 8], aux[:, lgc + 8:lgc + 9],
                    -1e30, ALU.is_ge, op1=ALU.mult),
                    waits=[t1], incs=("ve",))
                t3 = S.add("ve", lambda e, lgc=lgc: e.tensor_tensor(
                    aux[:, lgc + 9:lgc + 17], aux[:, lgc:lgc + 8], aux[:, lgc + 9:lgc + 17], ALU.add),
                    waits=[t2], incs=("ve",))
                t4 = S.add("ve", lambda e, lgc=lgc: e.reduce_max(
                    aux[:, lgc + 17:lgc + 18], aux[:, lgc + 9:lgc + 17], axis=mybir.AxisListType.X),
                    waits=[t3], incs=("ve",))
                t5 = S.add("ve", lambda e, lgc=lgc: e.tensor_tensor(
                    aux[:, lgc + 18:lgc + 19], aux[:, lgc + 8:lgc + 9], aux[:, lgc + 17:lgc + 18],
                    ALU.subtract),
                    waits=[t4], incs=("ve",))
                t6 = S.add("se", lambda e, lgc=lgc: e.activation(
                    aux[:, lgc + 19:lgc + 20], aux[:, lgc + 18:lgc + 19], AF.Exp, scale=-1.0),
                    waits=[t5], incs=("se",))
                t7 = S.add("se", lambda e, lgc=lgc: e.activation(
                    aux[:, lgc + 20:lgc + 21], aux[:, lgc + 18:lgc + 19], AF.Exp),
                    waits=[t5], incs=("se",))
                t8 = S.add("ve", lambda e, lgc=lgc: e.tensor_scalar(
                    aux[:, lgc + 21:lgc + 22], aux[:, lgc + 19:lgc + 20], 1.0, None, ALU.add),
                    waits=[t6], incs=("ve",))
                t9 = S.add("ve", lambda e, lgc=lgc: e.reciprocal(
                    aux[:, lgc + 22:lgc + 23], aux[:, lgc + 21:lgc + 22]),
                    waits=[t8], incs=("ve",))
                t10 = S.add("ve", lambda e, lgc=lgc: e.tensor_scalar(
                    aux[:, lgc + 23:lgc + 24], aux[:, lgc + 20:lgc + 21], 1.0, None, ALU.add),
                    waits=[t7], incs=("ve",))
                t11 = S.add("ve", lambda e, lgc=lgc: e.reciprocal(
                    aux[:, lgc + 24:lgc + 25], aux[:, lgc + 23:lgc + 24]),
                    waits=[t10], incs=("ve",))
                t12 = S.add("ve", lambda e, lgc=lgc, tbl=tbl: e.tensor_tensor(
                    aux[:, 240 + tbl:241 + tbl], aux[:, lgc + 22:lgc + 23],
                    aux[:, lgc + 24:lgc + 25], ALU.add),
                    waits=[t9, t11] + (fac_war if tbl == 0 else []), incs=("ve",))
                B(f"rt{tbl % 2}").ready = [t12]
                fac_toks.append(t12)
            fac.ready = fac_toks
            if hf == 0:
                emit_x2t(b, 1, x2t_sb)
            elif b + 1 in xt_prefetch:
                xtb = B("xt")
                xt_war = xtb.begin_write()
                toks = []
                for et in range(8):
                    tok = S.add("sp", lambda e, nb=b + 1, et=et: e.dma_start(
                        out=xt_buf_g[:, et * 512: et * 512 + 512],
                        in_=xb_scr[nb][0:512, et * 128:(et + 1) * 128],
                        transpose=True),
                        waits=(xt_war if et == 0 else []) + B(f"xb{b + 1}").ready,
                        incs=[("dma_xt", 16)])
                    toks.append(tok)
                xtb.ready = toks

            for g in range(2):
                yslots = [[("psS0", psS, 0, 512), ("psS1", psS, 512, 512)],
                          [("psS2", psS, 1024, 512), ("psS3", psS, 1536, 512)],
                          [("psC0", psC[0], 0, 512), ("psC1", psC[1], 0, 512)],
                          [("psC2", psC[2], 0, 512), ("psC3", psC[3], 0, 512)]]
                ywar = []
                yaps = []
                for j in range(4):
                    aps = []
                    for ec in range(2):
                        nm, tsr, co, wd_ = yslots[j][ec]
                        bb = B(nm)
                        ywar.append((j, ec, bb, bb.begin_write()))
                        aps.append(tsr[:, co:co + wd_])
                    yaps.append(aps)
                ytoks = [[None, None] for _ in range(4)]
                nfc = FF // 128
                for fc in range(nfc):
                    for j in range(4):
                        tbl = g * 4 + j
                        for ec in range(2):
                            war = []
                            if fc == 0:
                                for (jj, eecc, bb, w) in ywar:
                                    if jj == j and eecc == ec:
                                        war = w
                            tok = S.add("te", lambda e, j=j, ec=ec, fc=fc, tbl=tbl:
                                        e.matmul(
                                            yaps_g[j][ec],
                                            h_sb[:, fc * 1024 + tbl * 128: fc * 1024 + tbl * 128 + 128],
                                            w2_sb[:, fc * E + ec * 512: fc * E + ec * 512 + 512],
                                            start=(fc == 0), stop=(fc == nfc - 1)),
                                        waits=(hb.ready + w2b.ready + war) if fc == 0 else [],
                                        incs=("te",) if fc == nfc - 1 else ())
                            if fc == nfc - 1:
                                ytoks[j][ec] = tok
                                hb.users.append(tok)
                                w2b.users.append(tok)
                for j in range(4):
                    tbl = g * 4 + j
                    trow = hf * 1024 + tbl * 128
                    sl = j % 2
                    xfr = B(f"x2fr{sl}")
                    xfr_war = xfr.begin_write()
                    xtok = dma("sp", x2fr_buf[:, sl * E:(sl + 1) * E],
                               x2f_scr[b][trow:trow + 128, :],
                               xfr_war + B(f"x2fs{b}").ready, f"dma_fr{sl}", xfr)
                    ty = B(f"t1y{sl}")
                    ty_war = ty.begin_write()
                    ty.ready = []
                    for ec in range(2):
                        ttok = S.add("ve", lambda e, j=j, ec=ec, sl=sl: e.tensor_tensor(
                            t1y_buf[:, sl * E + ec * 512: sl * E + ec * 512 + 512],
                            yaps_g[j][ec], b2b_sb[:, ec * 512:(ec + 1) * 512], ALU.add),
                            waits=[ytoks[j][ec]] + b2b.ready + (ty_war if ec == 0 else []),
                            incs=("ve",))
                        b2b.users.append(ttok)
                        for (jj, eecc, bb, w) in ywar:
                            if jj == j and eecc == ec:
                                bb.users.append(ttok)
                        ty.ready.append(ttok)
                    t2y = B(f"t2y{sl}")
                    t2y_war = t2y.begin_write()
                    ftok = S.add("ve", lambda e, sl=sl, tbl=tbl: e.tensor_scalar(
                        t2y_buf[:, sl * E:(sl + 1) * E], t1y_buf[:, sl * E:(sl + 1) * E],
                        aux[:, 240 + tbl:241 + tbl], None, ALU.mult),
                        waits=ty.ready + fac.ready + t2y_war, incs=("ve",))
                    ty.users.append(ftok)
                    fac.users.append(ftok)
                    t2y.ready = [ftok]
                    rs2 = B(f"rs2{sl}")
                    rs2_war = rs2.begin_write()
                    rtoks = []
                    for ec in range(2):
                        rtok = S.add("ve", lambda e, sl=sl, ec=ec: e.tensor_tensor(
                            t1y_buf[:, sl * E + ec * 512: sl * E + ec * 512 + 512],
                            t2y_buf[:, sl * E + ec * 512: sl * E + ec * 512 + 512],
                            x2fr_buf[:, sl * E + ec * 512: sl * E + ec * 512 + 512], ALU.add),
                            waits=[ftok, xtok] + (rs2_war + ty.begin_write() if ec == 0 else []),
                            incs=("ve",))
                        t2y.users.append(rtok)
                        xfr.users.append(rtok)
                        rtoks.append(rtok)
                    ty.ready = rtoks
                    rstd_ap, nmr_ap, st_toks = ln_stats(rtoks, t1y_buf[:, sl * E:(sl + 1) * E], sl, aux)
                    ob = B(f"outb{sl}")
                    ob_war = ob.begin_write()
                    otok = S.add("ve", lambda e, sl=sl, r=rstd_ap, nm=nmr_ap: e.tensor_scalar(
                        outb_buf[:, sl * E:(sl + 1) * E], t1y_buf[:, sl * E:(sl + 1) * E],
                        r, nm, ALU.mult, op1=ALU.add),
                        waits=st_toks + rtoks + ob_war, incs=("ve",))
                    ty.users.append(otok)
                    ob.ready = [otok]
                    stok = S.add("sp", lambda e, b=b, trow=trow, sl=sl: e.dma_start(
                        out=out_d[b, trow:trow + 128, :], in_=outb_buf[:, sl * E:(sl + 1) * E]),
                        waits=[otok], incs=[(f"dma_o{sl}", 16)])
                    ob.users.append(stok)
        drain_dmas()

    global b1_sb_global, wr_sb_global, yaps_g
    with nc.sbuf_tensor("b1_sb", [128, FF // 128], F32) as b1_sb, \
         nc.sbuf_tensor("wr_sb", [128, 64], BF16) as wr_sb, \
         nc.sbuf_tensor("aux", [128, 512], F32) as aux, \
         nc.sbuf_tensor("pswap_sb", [128, 128], BF16) as pswap_sb_g, \
         nc.sbuf_tensor("wd_sb", [128, 8 * LAT], BF16) as wd_sb_g, \
         nc.sbuf_tensor("xt_buf", [128, 8 * 512], BF16, is_transpose_dest=True) as xt_buf_g:
        b1_sb_global = b1_sb
        wr_sb_global = wr_sb
        S.new_phase()
        bb = B(b1_sb.name)
        dma("sp", b1_sb[:], b1_in[:], bb.begin_write(), "dma_b1", bb)
        bb = B(wr_sb.name)
        dma("gp", wr_sb[:].rearrange("p (kc m) -> p kc m", kc=8),
            wr_in.rearrange("(kc p) m -> p kc m", p=128),
            bb.begin_write(), "dma_w", bb)
        epsb = B("epsap")
        etok = S.add("ve", lambda e: e.memset(aux[:, 255:256], EPS), incs=("ve",))
        epsb.ready = [etok]
        bb = B(wd_sb_g.name)
        dma("gp", wd_sb_g[:].rearrange("p (kc m) -> p kc m", kc=8),
            wd_in.rearrange("(kc p) m -> p kc m", p=128), bb.begin_write(), "dma_w", bb)
        bb = B(pswap_sb_g.name)
        dma("gp", pswap_sb_g[:], psw_in[:], bb.begin_write(), "dma_w", bb)
        xbb = B("xb0")
        dma("gp", xb_scr[0][:], x_in[0], xbb.begin_write(), "dma_xg", xbb)
        S.emit_phase()

        for b in range(B_LOC):
            xstk = ExitStack()
            bstk = ExitStack()
            x2t_sb = xstk.enter_context(
                nc.sbuf_tensor(f"x2t_sb{b}", [128, 8 * 1024], BF16, is_transpose_dest=True))
            if True:
                ben = bstk.enter_context
                kr_sb = ben(nc.sbuf_tensor(f"kr_sb{b}", [128, 8 * T], BF16))
                vaug_sb = ben(nc.sbuf_tensor(f"vaug_sb{b}", [128, 16 * H * 128], BF16))
                with ExitStack() as stk:
                    en = stk.enter_context
                    wq_sb = en(nc.sbuf_tensor(f"wq_sb{b}", [128, 8 * E], BF16))
                    wd_sb = wd_sb_g
                    wk_sb = en(nc.sbuf_tensor(f"wk_sb{b}", [128, 4 * E], BF16))
                    wv_sb = en(nc.sbuf_tensor(f"wv_sb{b}", [128, 4 * E], BF16))
                    ct_sb = en(nc.sbuf_tensor(f"ct_sb{b}", [128, 2 * 512], F32))
                    st_sb = en(nc.sbuf_tensor(f"st_sb{b}", [128, 2 * 512], F32))
                    xt_buf = xt_buf_g
                    lat_buf = en(nc.sbuf_tensor(f"lat_buf{b}", [128, 2 * 4 * 512], BF16))
                    qsp_buf = en(nc.sbuf_tensor(f"qsp_buf{b}", [128, 2 * 512], BF16))
                    qbf_tmp = en(nc.sbuf_tensor(f"qbf_tmp{b}", [128, 2 * 512], BF16))
                    t1_tmp = en(nc.sbuf_tensor(f"t1_tmp{b}", [128, 2 * 512], F32))
                    t2_tmp = en(nc.sbuf_tensor(f"t2_tmp{b}", [128, 2 * 512], F32))
                    S.new_phase()
                    proj_phase(b, (wq_sb, wd_sb, wk_sb, wv_sb, ct_sb, st_sb,
                                   kr_sb, vaug_sb, xt_buf, lat_buf, qsp_buf,
                                   pswap_sb_g, aux, qbf_tmp, t1_tmp, t2_tmp))
                    S.emit_phase()
                with ExitStack() as stk:
                    en = stk.enter_context
                    wo_sb = en(nc.sbuf_tensor(f"wo_sb{b}", [128, 8 * E], BF16))
                    qch_buf = en(nc.sbuf_tensor(f"qch_buf{b}", [128, 2 * 8 * QC], BF16))
                    pt_buf = en(nc.sbuf_tensor(f"pt_buf{b}", [128, 2 * 2048], BF16))
                    rb_sb = en(nc.sbuf_tensor(f"rb_sb{b}", [64, 4 * 512], F32))
                    rb_scr = en(nc.sbuf_tensor(f"rb_scr{b}", [64, 2 * 512], F32))
                    ao_sb = en(nc.sbuf_tensor(f"ao_sb{b}", [128, 8 * 512], BF16))
                    xres_buf = en(nc.sbuf_tensor(f"xres_buf{b}", [128, 2 * E], F32))
                    rs1_buf = en(nc.sbuf_tensor(f"rs1_buf{b}", [128, 2 * E], F32))
                    S.new_phase()
                    attn_phase(b, (wo_sb, kr_sb, vaug_sb, qch_buf, pt_buf,
                                   rb_sb, rb_scr, ao_sb, xres_buf, rs1_buf,
                                   x2t_sb, aux))
                    S.emit_phase()
            bstk.close()
            with ExitStack() as stk:
                en = stk.enter_context
                w2_sb = en(nc.sbuf_tensor(f"w2_sb{b}", [128, 32 * E], BF16))
                h_sb = en(nc.sbuf_tensor(f"h_sb{b}", [128, 32 * 1024], BF16))
                w1_buf = en(nc.sbuf_tensor(f"w1_buf{b}", [128, 2 * 1024], BF16))
                b2b_sb = en(nc.sbuf_tensor(f"b2b_sb{b}", [128, E], F32))
                x2fr_buf = en(nc.sbuf_tensor(f"x2fr_buf{b}", [128, 2 * E], F32))
                t1y_buf = en(nc.sbuf_tensor(f"t1y_buf{b}", [128, 2 * E], F32))
                t2y_buf = en(nc.sbuf_tensor(f"t2y_buf{b}", [128, 2 * E], F32))
                outb_buf = en(nc.sbuf_tensor(f"outb_buf{b}", [128, 2 * E], F32))
                S.new_phase()
                yaps_g = [[psS[:, 0:512], psS[:, 512:1024]],
                          [psS[:, 1024:1536], psS[:, 1536:2048]],
                          [psC[0][:, :], psC[1][:, :]],
                          [psC[2][:, :], psC[3][:, :]]]
                ffn_phase(b, (w2_sb, x2t_sb, h_sb, w1_buf, b2b_sb, x2fr_buf,
                              t1y_buf, t2y_buf, outb_buf, aux))
                S.emit_phase()
            xstk.close()

    nc.finalize()
    return nc


def host_consts():
    inv = 1.0 / (10000.0 ** (np.arange(0, D, 2, dtype=np.float32) / D))
    ang = np.outer(np.arange(T, dtype=np.float32), inv)
    cos = np.cos(ang)
    sin = np.sin(ang)
    r = np.arange(128)
    i = (r % 64) // 2
    ctab = cos[:, i].T.astype(np.float32).copy()
    sgn = np.where(r % 2 == 0, -1.0, 1.0).astype(np.float32)
    stab = (sin[:, i].T * sgn[:, None]).astype(np.float32).copy()
    perm = np.arange(128) ^ 1
    pswap = np.zeros((128, 128), dtype=np.float32)
    pswap[perm, np.arange(128)] = 1.0
    return ctab, stab, pswap


_prog_cache = {}


def kernel(L0_attn_x, L0_attn_wq, L0_attn_w_down, L0_attn_wk_up, L0_attn_wv_up,
           L0_attn_wo, L0_ffn_router_w, L0_ffn_w1, L0_ffn_b1, L0_ffn_w2, L0_ffn_b2):
    x = np.ascontiguousarray(np.asarray(L0_attn_x, dtype=np.float32))
    ctab, stab, pswap = host_consts()
    common = {
        "wq": np.ascontiguousarray(np.asarray(L0_attn_wq, dtype=np.float32)),
        "wd": np.ascontiguousarray(np.asarray(L0_attn_w_down, dtype=np.float32)),
        "wk": np.ascontiguousarray(np.asarray(L0_attn_wk_up, dtype=np.float32)),
        "wv": np.ascontiguousarray(np.asarray(L0_attn_wv_up, dtype=np.float32)),
        "wo": np.ascontiguousarray(np.asarray(L0_attn_wo, dtype=np.float32)),
        "wr": np.ascontiguousarray(np.asarray(L0_ffn_router_w, dtype=np.float32)),
        "w1": np.ascontiguousarray(np.asarray(L0_ffn_w1, dtype=np.float32)),
        "w2": np.ascontiguousarray(np.asarray(L0_ffn_w2, dtype=np.float32)),
        "b1l": np.ascontiguousarray(np.asarray(L0_ffn_b1, dtype=np.float32).reshape(FF // 128, 128).T),
        "b2b": np.ascontiguousarray(np.broadcast_to(np.asarray(L0_ffn_b2, dtype=np.float32), (128, E))),
        "ctab": ctab, "stab": stab, "pswap": pswap,
    }
    if "nc" not in _prog_cache:
        _prog_cache["nc"] = build_program()
    nc = _prog_cache["nc"]
    in_maps = []
    for c in range(N_CORES):
        m = dict(common)
        m["x"] = np.ascontiguousarray(x[c * B_LOC:(c + 1) * B_LOC])
        in_maps.append(m)
    res = run_bass_kernel_spmd(nc, in_maps, core_ids=list(range(N_CORES)))
    return np.concatenate([r["out"] for r in res.results], axis=0)


# version 18
# speedup vs baseline: 1.1709x; 1.1709x over previous
import os
import numpy as np
import ml_dtypes
from contextlib import ExitStack
import concourse.bass as bass
import concourse.mybir as mybir
from concourse import dve_ops
from concourse.bass_utils import run_bass_kernel_spmd

F32 = mybir.dt.float32
BF16 = mybir.dt.bfloat16
AF = mybir.ActivationFunctionType
ALU = mybir.AluOpType

B_LOC = 2
T = 2048
E = 1024
H = 16
D = 64
LAT = 512
FF = 4096
QC = 512
NQC = T // QC
EPS = 1e-5
SET_LNEXP = 6
SET_GELU = 11
N_CORES = 8


class Buf:
    def __init__(self):
        self.ready = []
        self.users = []

    def begin_write(self):
        w = self.users + self.ready
        self.ready = []
        self.users = []
        return w


class Sched:
    def __init__(self, nc, semnames):
        self.nc = nc
        self.sems = {n: nc.alloc_semaphore(n) for n in semnames}
        self.cnt = {n: 0 for n in semnames}
        self.ops = None

    def new_phase(self):
        self.ops = []

    def add(self, engine, fn, waits=(), incs=()):
        norm = []
        for inc in incs:
            sem, n = inc if isinstance(inc, tuple) else (inc, 1)
            self.cnt[sem] += n
            norm.append((sem, n))
        wm = {}
        for w in waits:
            if w is None:
                continue
            sem, val = w
            if val > 0:
                wm[sem] = max(wm.get(sem, 0), val)
        self.ops.append((engine, tuple(sorted(wm.items())), tuple(norm), fn))
        return (norm[0][0], self.cnt[norm[0][0]]) if norm else None

    def emit_phase(self):
        ops = self.ops
        self.ops = None
        with self.nc.Block() as blk:
            decos = {"te": blk.tensor, "se": blk.scalar, "ve": blk.vector,
                     "gp": blk.gpsimd, "sp": blk.sync}
            for name in ("te", "se", "ve", "gp", "sp"):
                myops = [(w, i, f) for (e, w, i, f) in ops if e == name]
                if not myops:
                    continue

                def make_body(myops=myops):
                    def body(eng):
                        seen = {}
                        for waits, incs, fn in myops:
                            for sem, val in waits:
                                if seen.get(sem, 0) < val:
                                    eng.wait_ge(self.sems[sem], val)
                                    seen[sem] = val
                            if fn is None:
                                continue
                            inst = fn(eng)
                            for sem, n in incs:
                                inst.then_inc(self.sems[sem], n)
                    return body

                decos[name](make_body())


def build_program():
    nc = bass.Bass("TRN2", target_bir_lowering=False)

    x_in = nc.dram_tensor("x", [B_LOC, T, E], F32, kind="ExternalInput")
    wq_in = nc.dram_tensor("wq", [E, E], F32, kind="ExternalInput")
    wd_in = nc.dram_tensor("wd", [E, LAT], F32, kind="ExternalInput")
    wk_in = nc.dram_tensor("wk", [LAT, E], F32, kind="ExternalInput")
    wv_in = nc.dram_tensor("wv", [LAT, E], F32, kind="ExternalInput")
    wo_in = nc.dram_tensor("wo", [E, E], F32, kind="ExternalInput")
    wr_in = nc.dram_tensor("wr", [E, 8], F32, kind="ExternalInput")
    w1_in = nc.dram_tensor("w1", [E, FF], F32, kind="ExternalInput")
    w2_in = nc.dram_tensor("w2", [FF, E], F32, kind="ExternalInput")
    b1_in = nc.dram_tensor("b1l", [128, FF // 128], F32, kind="ExternalInput")
    b2_in = nc.dram_tensor("b2b", [128, E], F32, kind="ExternalInput")
    ct_in = nc.dram_tensor("ctab", [128, T], F32, kind="ExternalInput")
    st_in = nc.dram_tensor("stab", [128, T], F32, kind="ExternalInput")
    psw_in = nc.dram_tensor("pswap", [128, 128], F32, kind="ExternalInput")
    out_d = nc.dram_tensor("out", [B_LOC, T, E], F32, kind="ExternalOutput")

    xb_scr = [nc.dram_tensor(f"xb_scr{b}", [T, E], BF16, kind="Internal") for b in range(B_LOC)]
    q_scr = [nc.dram_tensor(f"q_scr{b}", [NQC, 8, 128, QC], BF16, kind="Internal") for b in range(B_LOC)]
    x2f_scr = [nc.dram_tensor(f"x2f_scr{b}", [T, E], F32, kind="Internal") for b in range(B_LOC)]
    x2b_scr = [nc.dram_tensor(f"x2b_scr{b}", [T, E], BF16, kind="Internal") for b in range(B_LOC)]

    psS = nc.alloc_psum_tensor("psS", [128, 2048], F32)
    psC = [nc.alloc_psum_tensor(f"psC{i}", [128, 512], F32) for i in range(4)]

    S = Sched(nc, ["dma_w", "dma_t", "dma_xg", "dma_xt", "dma_ct0", "dma_ct1",
                   "dma_st0", "dma_st1", "dma_qs0", "dma_qs1", "dma_qch0",
                   "dma_xr0", "dma_xr1", "dma_xf0", "dma_xf1", "dma_xb0",
                   "dma_xb1", "dma_w10", "dma_w11", "dma_x2t", "dma_fr0",
                   "dma_fr1", "dma_b1", "dma_sel", "dma_qch1", "dma_o0", "dma_o1", "te", "se", "ve", "gp"])
    bufs = {}

    def B(name):
        if name not in bufs:
            bufs[name] = Buf()
        return bufs[name]

    slots8 = [("psS0", psS, 0, 512), ("psS1", psS, 512, 512),
              ("psS2", psS, 1024, 512), ("psS3", psS, 1536, 512),
              ("psC0", psC[0], 0, 512), ("psC1", psC[1], 0, 512),
              ("psC2", psC[2], 0, 512), ("psC3", psC[3], 0, 512)]

    class Rot:
        def __init__(self, slots):
            self.slots = slots
            self.i = 0

        def next(self):
            s = self.slots[self.i % len(self.slots)]
            self.i += 1
            return s

    def mm_group(slot, pieces, extra_waits=(), reads=()):
        name, tensor, coloff, width = slot
        buf = B(name)
        war = buf.begin_write()
        ap = tensor[:, coloff:coloff + width]
        n = len(pieces)
        tok = None
        for j, (lhs, rhs, ws) in enumerate(pieces):
            waits = list(ws)
            if j == 0:
                waits += war + list(extra_waits)
            tok = S.add(
                "te",
                lambda e, ap=ap, lhs=lhs, rhs=rhs, st=(j == 0), sp=(j == n - 1):
                    e.matmul(ap, lhs, rhs, start=st, stop=sp),
                waits=waits,
                incs=("te",) if j == n - 1 else ())
        buf.ready = [tok]
        for rb in reads:
            rb.users.append(tok)
        return ap, tok, buf

    def dma(queue, out_ap, in_ap, waits, sem, buf=None):
        waits = list(waits)
        if queue == "gp":
            waits.append((sem, S.cnt[sem]))
        tok = S.add(queue, lambda e, o=out_ap, i=in_ap: e.dma_start(out=o, in_=i),
                    waits=waits, incs=[(sem, 16)])
        if buf is not None:
            buf.ready = [tok]
        return tok

    def load_set(set_id):
        if os.environ.get("KDBG_NOACT"):
            return
        S.add("se", lambda e, s=set_id: e.add_instruction(
            mybir.InstLoadActFuncSet(
                name=e.bass.get_next_instruction_name(),
                act_func_set_id=s, ins=[], outs=[])))

    def drain_dmas():
        S.add("sp", None, waits=[(n, S.cnt[n]) for n in S.sems if n.startswith("dma_")])

    xt_prefetch = set()

    def proj_phase(b, tens):
        (wq_sb, wd_sb, wk_sb, wv_sb, ct_sb, st_sb, kr_sb, vaug_sb,
         xt_buf, lat_buf, qsp_buf, pswap_sb, aux, qbf_tmp, t1_tmp,
         t2_tmp) = tens
        rot = Rot(slots8)

        for sbuf, dram, nkc in ((wq_sb, wq_in, 8),
                                (wk_sb, wk_in, 4), (wv_sb, wv_in, 4)):
            bb = B(sbuf.name)
            dma("gp", sbuf[:].rearrange("p (kc m) -> p kc m", kc=nkc),
                dram.rearrange("(kc p) m -> p kc m", p=128),
                bb.begin_write(), "dma_w", bb)
        load_set(SET_LNEXP)

        xbb = B(f"xb{b}")
        if not xbb.ready:
            dma("gp", xb_scr[b][:], x_in[b], xbb.begin_write(), "dma_xg", xbb)

        kr = B(kr_sb.name)
        kr_war = kr.begin_write()
        va = B(vaug_sb.name)
        va_war = va.begin_write()
        ones_tok = S.add("gp", lambda e: e.memset(
            vaug_sb[:].rearrange("p (tt h d) -> p tt h d", tt=16, h=H)[:, :, :, 64:128], 1.0),
            waits=va_war, incs=("gp",))
        va.ready.append(ones_tok)
        qsc = B(f"qsc{b}")
        qsc_war = qsc.begin_write()
        if os.environ.get("KDBG_STAGE") == "setup":
            drain_dmas()
            return

        for tc in range(NQC):
            skip_xt = (tc == 0 and b in xt_prefetch)
            ctb = B(f"ct{tc % 2}")
            toff = (tc % 2) * 512
            cttok = dma("sp", ct_sb[:, toff:toff + 512], ct_in[:, tc * 512:(tc + 1) * 512],
                        ctb.begin_write(), f"dma_ct{tc % 2}", ctb)
            stb = B(f"st{tc % 2}")
            sttok = dma("sp", st_sb[:, toff:toff + 512], st_in[:, tc * 512:(tc + 1) * 512],
                        stb.begin_write(), f"dma_st{tc % 2}", stb)
            xtb = B("xt")
            xoff = 0
            if not skip_xt:
                xt_war = xtb.begin_write()
                xt_toks = []
                for et in range(8):
                    tok = S.add("sp", lambda e, b=b, tc=tc, et=et, xoff=xoff: e.dma_start(
                        out=xt_buf[:, xoff + et * 512: xoff + et * 512 + 512],
                        in_=xb_scr[b][tc * 512:(tc + 1) * 512, et * 128:(et + 1) * 128],
                        transpose=True),
                        waits=(xt_war if et == 0 else []) + xbb.ready,
                        incs=[("dma_xt", 16)])
                    xt_toks.append(tok)
                xtb.ready = xt_toks

            def xt_rhs(kc, xoff=xoff):
                return xt_buf[:, xoff + kc * 512: xoff + kc * 512 + 512]

            latb = B(f"lat{tc % 2}")
            lat_war = latb.begin_write()
            lat_toks = []
            loff = (tc % 2) * 2048
            for lt in range(4):
                pieces = [(wd_sb[:, kc * LAT + lt * 128: kc * LAT + lt * 128 + 128],
                           xt_rhs(kc),
                           B(wd_sb.name).ready + xtb.ready) for kc in range(8)]
                ap, mtok, pbuf = mm_group(rot.next(), pieces, extra_waits=lat_war if lt == 0 else (),
                                          reads=[B(wd_sb.name), xtb])
                etok = S.add("se", lambda e, ap=ap, lt=lt, loff=loff: e.activation(
                    lat_buf[:, loff + lt * 512: loff + lt * 512 + 512], ap, AF.Copy),
                    waits=[mtok], incs=("se",))
                pbuf.users.append(etok)
                lat_toks.append(etok)
            latb.ready = lat_toks

            if os.environ.get("KDBG_STAGE") == "proj1":
                continue
            for which in ("q", "k"):
                for mc in range(8):
                    if which == "q":
                        pieces = [(wq_sb[:, kc * E + mc * 128: kc * E + mc * 128 + 128],
                                   xt_rhs(kc),
                                   B(wq_sb.name).ready + xtb.ready) for kc in range(8)]
                    else:
                        pieces = [(wk_sb[:, kc * E + mc * 128: kc * E + mc * 128 + 128],
                                   lat_buf[:, loff + kc * 512: loff + kc * 512 + 512],
                                   B(wk_sb.name).ready + latb.ready) for kc in range(4)]
                    ap, mtok, pbuf = mm_group(rot.next(), pieces,
                                              reads=[B(wq_sb.name), xtb] if which == "q" else [B(wk_sb.name), latb])
                    sl = mc % 2
                    qb = B(f"qbf{sl}")
                    qb_war = qb.begin_write()
                    qtok = S.add("se", lambda e, ap=ap, sl=sl: e.activation(
                        qbf_tmp[:, sl * 512:sl * 512 + 512], ap, AF.Copy),
                        waits=[mtok] + qb_war, incs=("se",))
                    qb.ready = [qtok]
                    pbuf.users.append(qtok)
                    if os.environ.get("KDBG_STAGE") == "proj2a":
                        continue

                    swslot = rot.next()
                    swname, swt, swoff, _ = swslot
                    swbuf = B(swname)
                    sw_war = swbuf.begin_write()
                    swap = swt[:, swoff:swoff + 512]
                    swtok = S.add("te", lambda e, swap=swap, sl=sl: e.matmul(
                        swap, pswap_sb[:], qbf_tmp[:, sl * 512:sl * 512 + 512],
                        start=True, stop=True),
                        waits=[qtok] + sw_war + B(pswap_sb.name).ready, incs=("te",))
                    swbuf.ready = [swtok]
                    qb.users.append(swtok)

                    if os.environ.get("KDBG_STAGE") == "proj2b":
                        continue
                    t1b = B(f"t1{sl}")
                    t1_war = t1b.begin_write()
                    t1tok = S.add("ve", lambda e, ap=ap, sl=sl, toff=toff: e.tensor_tensor(
                        t1_tmp[:, sl * 512:sl * 512 + 512], ap,
                        ct_sb[:, toff:toff + 512], ALU.mult),
                        waits=[mtok, qtok, cttok] + t1_war, incs=("ve",))
                    ctb.users.append(t1tok)
                    t1b.ready = [t1tok]
                    pbuf.users.append(t1tok)
                    if os.environ.get("KDBG_STAGE") == "proj2d":
                        t1b.users.append(t1tok)
                        continue

                    t2b = B(f"t2{sl}")
                    t2_war = t2b.begin_write()
                    t2tok = S.add("ve", lambda e, swap=swap, sl=sl, toff=toff: e.tensor_tensor(
                        t2_tmp[:, sl * 512:sl * 512 + 512], swap,
                        st_sb[:, toff:toff + 512], ALU.mult),
                        waits=[swtok, sttok] + t2_war, incs=("ve",))
                    stb.users.append(t2tok)
                    t2b.ready = [t2tok]
                    swbuf.users.append(t2tok)

                    if which == "k":
                        ktok = S.add("gp", lambda e, sl=sl, mc=mc, tc=tc: e.tensor_tensor(
                            kr_sb[:, mc * T + tc * 512: mc * T + tc * 512 + 512],
                            t1_tmp[:, sl * 512:sl * 512 + 512],
                            t2_tmp[:, sl * 512:sl * 512 + 512], ALU.add),
                            waits=[t1tok, t2tok] + (kr_war if (mc == 0 and tc == 0) else []),
                            incs=("gp",))
                        kr.ready.append(ktok)
                        t1b.users.append(ktok)
                        t2b.users.append(ktok)
                    else:
                        qsb = B(f"qsp{sl}")
                        qs_war = qsb.begin_write()
                        qrtok = S.add("gp", lambda e, sl=sl: e.tensor_tensor(
                            qsp_buf[:, sl * 512:sl * 512 + 512],
                            t1_tmp[:, sl * 512:sl * 512 + 512],
                            t2_tmp[:, sl * 512:sl * 512 + 512], ALU.add),
                            waits=[t1tok, t2tok] + qs_war, incs=("gp",))
                        t1b.users.append(qrtok)
                        t2b.users.append(qrtok)
                        qsb.ready = [qrtok]
                        if os.environ.get("KDBG_STAGE") == "proj2c":
                            qsb.users.append(qrtok)
                            continue
                        sptok = S.add("sp", lambda e, b=b, tc=tc, mc=mc, sl=sl: e.dma_start(
                            out=q_scr[b][tc, mc], in_=qsp_buf[:, sl * 512:sl * 512 + 512]),
                            waits=[qrtok] + (qsc_war if (mc == 0 and tc == 0) else []),
                            incs=[(f"dma_qs{sl}", 16)])
                        qsb.users.append(sptok)
                        qsc.ready.append(sptok)

            if os.environ.get("KDBG_STAGE") == "proj2":
                continue
            for tt4 in range(4):
                tt = tc * 4 + tt4
                for ec in range(2):
                    pieces = [(lat_buf[:, loff + lc * 512 + tt4 * 128: loff + lc * 512 + tt4 * 128 + 128],
                               wv_sb[:, lc * E + ec * 512: lc * E + ec * 512 + 512],
                               latb.ready + B(wv_sb.name).ready) for lc in range(4)]
                    ap, mtok, pbuf = mm_group(rot.next(), pieces, reads=[latb, B(wv_sb.name)])
                    vtok = S.add("ve", lambda e, ap=ap, tt=tt, ec=ec: e.tensor_copy(
                        vaug_sb[:].rearrange("p (tt h d) -> p tt h d", tt=16, h=H)
                        [:, tt, 8 * ec:8 * ec + 8, 0:64],
                        ap.rearrange("p (h d) -> p h d", h=8)),
                        waits=[mtok], incs=("ve",))
                    pbuf.users.append(vtok)
                    va.ready.append(vtok)

        drain_dmas()

    def emit_x2t(b, hf, x2t_sb):
        x2t = B("x2t")
        x2t_war = x2t.begin_write()
        toks = []
        for et in range(8):
            tok = S.add("sp", lambda e, b=b, hf=hf, et=et, x2t_sb=x2t_sb: e.dma_start(
                out=x2t_sb[:, et * 1024:(et + 1) * 1024],
                in_=x2b_scr[b][hf * 1024:(hf + 1) * 1024, et * 128:(et + 1) * 128],
                transpose=True),
                waits=(x2t_war if et == 0 else []) + B(f"x2bs{b}_h{hf}").ready,
                incs=[("dma_x2t", 16)])
            toks.append(tok)
        x2t.ready = toks
        return x2t

    def attn_phase(b, tens):
        (wo_sb, kr_sb, vaug_sb, qch_buf, pt_buf, rb_sb, rb_scr, ao_sb,
         xres_buf, rs1_buf, x2t_sb, aux) = tens
        if os.environ.get("KDBG_STAGE", "").startswith("proj"):
            drain_dmas()
            return
        load_set(SET_LNEXP)
        bb = B(wo_sb.name)
        dma("gp", wo_sb[:].rearrange("p (kc m) -> p kc m", kc=8),
            wo_in.rearrange("(kc p) m -> p kc m", p=128),
            bb.begin_write(), "dma_w", bb)
        kr = B(kr_sb.name)
        va = B(vaug_sb.name)
        qsc = B(f"qsc{b}")
        def load_qch(c):
            qch = B(f"qch{c % 2}")
            q_war = qch.begin_write()
            qoff = (c % 2) * 8 * QC
            dma("sp", qch_buf[:, qoff:qoff + 8 * QC].rearrange("p (kc n) -> p kc n", kc=8),
                q_scr[b][c].rearrange("kc p n -> p kc n"),
                q_war + qsc.ready, f"dma_qch{c % 2}", qch)
            return qch

        load_qch(0)
        for c in range(NQC):
            qch = B(f"qch{c % 2}")
            qoff = (c % 2) * 8 * QC
            if c + 1 < NQC:
                load_qch(c + 1)
            nkc = 4 * (c + 1)
            ao = B(ao_sb.name)
            ao_war = ao.begin_write()
            for hp in range(8):
                par = hp % 2
                avA_ap = psC[0 + 2 * par]
                avB_ap = psC[1 + 2 * par]
                avA = B(f"psC{0 + 2 * par}")
                avB = B(f"psC{1 + 2 * par}")
                avA_war = avA.begin_write()
                avB_war = avB.begin_write()
                av_last = None
                pt_toks = {}

                def emit_S(kc, hp=hp, c=c, qoff=qoff):
                    snames = ("psS0", "psS1") if (kc % 2 == 0) else ("psS2", "psS3")
                    ao_base = (kc % 2) * 1024
                    sb0, sb1 = B(snames[0]), B(snames[1])
                    s_war = sb0.begin_write() + sb1.begin_write()
                    mts = []
                    for half in range(2):
                        base = half * 64
                        colo = ao_base + half * 512
                        tok = S.add("te", lambda e, colo=colo, base=base, hp=hp, kc=kc, qoff=qoff:
                                    e.matmul(
                                        psS[:, colo:colo + 512],
                                        kr_sb[base:base + 64, hp * T + kc * 128: hp * T + kc * 128 + 128],
                                        qch_buf[base:base + 64, qoff + hp * QC: qoff + hp * QC + QC],
                                        start=True, stop=True, tile_position=(base, 0)),
                                    waits=(s_war + kr.ready + qch.ready) if half == 0 else [],
                                    incs=("te",))
                        mts.append(tok)
                    sb0.ready = [mts[0]]
                    sb1.ready = [mts[1]]
                    kr.users.append(mts[1])
                    qch.users.append(mts[1])
                    masked = (kc >= 4 * c + 2)
                    ptb = B(f"pt{kc % 4}")
                    pt_war = ptb.begin_write()
                    po = (kc % 4) * 1024
                    if not masked:
                        etok = S.add("se", lambda e, po=po, ao_base=ao_base: e.activation(
                            pt_buf[:, po:po + 1024], psS[:, ao_base:ao_base + 1024],
                            AF.Exp, scale=0.125),
                            waits=mts + pt_war, incs=("se",))
                        ptoks = [etok]
                    else:
                        ztok = S.add("ve", lambda e, po=po: e.memset(
                            pt_buf[:, po:po + 1024].rearrange("p (h n) -> p h n", h=2)[:, :, 0:256], 0.0),
                            waits=pt_war, incs=("ve",))
                        etok = S.add("se", lambda e, po=po, ao_base=ao_base: e.activation(
                            pt_buf[:, po:po + 1024].rearrange("p (h n) -> p h n", h=2)[:, :, 256:512],
                            psS[:, ao_base:ao_base + 1024].rearrange("p (h n) -> p h n", h=2)[:, :, 256:512],
                            AF.Exp, scale=0.125),
                            waits=mts + [ztok], incs=("se",))
                        ptoks = [etok, ztok]
                    ptb.ready = ptoks
                    sb0.users.append(etok)
                    sb1.users.append(etok)
                    pt_toks[kc] = ptoks

                def emit_AV(kc, hp=hp):
                    nonlocal av_last
                    po = (kc % 4) * 1024
                    ptb = B(f"pt{kc % 4}")
                    tok = None
                    for half in range(2):
                        av_ap = avA_ap if half == 0 else avB_ap
                        tok = S.add("te", lambda e, av_ap=av_ap, hp=hp, half=half, kc=kc, po=po, nkc=nkc:
                                    e.matmul(
                                        av_ap[:, :],
                                        vaug_sb[:, kc * (H * 128) + (2 * hp + half) * 128:
                                                kc * (H * 128) + (2 * hp + half) * 128 + 128],
                                        pt_buf[:, po + half * 512: po + half * 512 + 512],
                                        start=(kc == 0), stop=(kc == nkc - 1)),
                                    waits=pt_toks[kc] + va.ready +
                                          ((avA_war if half == 0 else avB_war) if kc == 0 else []),
                                    incs=("te",) if half == 1 else ())
                    ptb.users.append(tok)
                    va.users.append(tok)
                    if kc == nkc - 1:
                        av_last = tok

                emit_S(0)
                for kc in range(nkc):
                    if kc + 1 < nkc:
                        emit_S(kc + 1)
                    emit_AV(kc)
                avA.ready = [av_last]
                avB.ready = [av_last]

                rbs = B(f"rb{par}")
                rb_war = rbs.begin_write()
                rtoks = []
                for half in range(2):
                    av_ap = avA_ap if half == 0 else avB_ap
                    ro = (2 * par + half) * 512
                    if half == 0:
                        rtok = S.add("ve", lambda e, av_ap=av_ap, ro=ro:
                                     e.reciprocal(rb_sb[0:64, ro:ro + 512], av_ap[64:128, :]),
                                     waits=[av_last] + rb_war, incs=("ve",))
                    else:
                        scr = rb_scr[0:64, par * 512:par * 512 + 512]
                        scb = B(f"rbscr{par}")
                        scr_war = scb.begin_write()
                        ltok = S.add("se", lambda e, av_ap=av_ap, scr=scr:
                                     e.activation(scr, av_ap[64:128, :], AF.Ln),
                                     waits=[av_last] + scr_war, incs=("se",))
                        scb.ready = [ltok]
                        rtok = S.add("se", lambda e, ro=ro, scr=scr:
                                     e.activation(rb_sb[0:64, ro:ro + 512], scr, AF.Exp, scale=-1.0),
                                     waits=[ltok], incs=("se",))
                        scb.users.append(rtok)
                    rtoks.append(rtok)
                rbs.ready = rtoks
                avA.users.append(rtoks[0])
                avB.users.append(rtoks[1])
                for half in range(2):
                    avbuf = avA if half == 0 else avB
                    av_ap = avA_ap if half == 0 else avB_ap
                    atok = S.add("ve", lambda e, hp=hp, half=half, par=par, av_ap=av_ap: e.tensor_tensor(
                        ao_sb[half * 64:half * 64 + 64, hp * 512:hp * 512 + 512],
                        av_ap[0:64, :],
                        rb_sb[0:64, (2 * par + half) * 512:(2 * par + half) * 512 + 512], ALU.mult),
                        waits=[rtoks[half]] + (ao_war if (hp == 0 and half == 0) else []),
                        incs=("ve",))
                    avbuf.users.append(atok)
                    rbs.users.append(atok)
                    ao.ready.append(atok)
            for tb in range(4):
                trow = c * 512 + tb * 128
                sl = tb % 2
                xr = B(f"xres{sl}")
                xr_war = xr.begin_write()
                xtok = dma("sp", xres_buf[:, sl * E:(sl + 1) * E],
                           x_in[b, trow:trow + 128, :], xr_war, f"dma_xr{sl}", xr)
                rs = B(f"rs1{sl}")
                rs_war = rs.begin_write()
                rs.ready = []
                for ec in range(2):
                    slot = ("psC3", psC[3], 0, 512) if ec == 0 else ("psC2", psC[2], 0, 512)
                    pieces = [(ao_sb[:, kc * 512 + tb * 128: kc * 512 + tb * 128 + 128],
                               wo_sb[:, kc * E + ec * 512: kc * E + ec * 512 + 512],
                               ao.ready + B(wo_sb.name).ready) for kc in range(8)]
                    ap, mtok, pbuf = mm_group(slot, pieces, reads=[ao, B(wo_sb.name)])
                    rtok = S.add("ve", lambda e, ap=ap, sl=sl, ec=ec: e.tensor_tensor(
                        rs1_buf[:, sl * E + ec * 512: sl * E + ec * 512 + 512],
                        ap, xres_buf[:, sl * E + ec * 512: sl * E + ec * 512 + 512], ALU.add),
                        waits=[mtok, xtok] + (rs_war if ec == 0 else []), incs=("ve",))
                    pbuf.users.append(rtok)
                    xr.users.append(rtok)
                    rs.ready.append(rtok)
                rstd_ap, nmr_ap, st_toks = ln_stats(rs.ready, rs1_buf[:, sl * E:(sl + 1) * E], sl, aux)
                xr2_war = xr.begin_write()
                a1 = S.add("ve", lambda e, sl=sl, r=rstd_ap, nm=nmr_ap: e.tensor_scalar(
                    xres_buf[:, sl * E:(sl + 1) * E], rs1_buf[:, sl * E:(sl + 1) * E],
                    r, nm, ALU.mult, op1=ALU.add),
                    waits=st_toks + rs.ready + xr2_war, incs=("ve",))
                xr.ready = [a1]
                rs.users.append(a1)
                sp1 = S.add("sp", lambda e, b=b, trow=trow, sl=sl: e.dma_start(
                    out=x2f_scr[b][trow:trow + 128, :], in_=xres_buf[:, sl * E:(sl + 1) * E]),
                    waits=[a1], incs=[(f"dma_xf{sl}", 16)])
                xr.users.append(sp1)
                B(f"x2fs{b}").ready.append(sp1)
                B(f"x2fs{b}_h{trow // 1024}").ready.append(sp1)
            if c in (1, 3):
                hf_ = (c - 1) // 2
                xc = B(f"x2bs{b}_h{hf_}")
                xc_war = xc.begin_write()
                sptoks = B(f"x2fs{b}_h{hf_}").ready
                ctoks = []
                for k in range(8):
                    r0 = hf_ * 1024 + k * 128
                    ctok = S.add("gp", lambda e, b=b, r0=r0: e.dma_start(
                        out=x2b_scr[b][r0:r0 + 128, :],
                        in_=x2f_scr[b][r0:r0 + 128, :]),
                        waits=(list(xc_war) if k == 0 else []) + [sptoks[k]] +
                              [(f"dma_xb{hf_}", S.cnt[f"dma_xb{hf_}"])],
                        incs=[(f"dma_xb{hf_}", 16)])
                    ctoks.append(ctok)
                xc.ready = ctoks
            if c == 1:
                emit_x2t(b, 0, x2t_sb)
        drain_dmas()

    def ln_stats(src_toks, src_ap, sl, aux):
        so = sl * 24
        st = B(f"lnst{sl}")
        st_war = st.begin_write()
        t_b0 = S.add("ve", lambda e, src=src_ap, so=so: e.bn_stats(
            aux[:, so:so + 6], src[:, 0:512]),
            waits=src_toks + st_war, incs=("ve",))
        t_b1 = S.add("ve", lambda e, src=src_ap, so=so: e.bn_stats(
            aux[:, so + 6:so + 12], src[:, 512:1024]),
            waits=src_toks, incs=("ve",))
        t_ag = S.add("ve", lambda e, so=so: e.bn_aggr(aux[:, so + 12:so + 14], aux[:, so:so + 12]),
                     waits=[t_b0, t_b1], incs=("ve",))
        t_ln = S.add("se", lambda e, so=so: e.activation(
            aux[:, so + 14:so + 15], aux[:, so + 13:so + 14], AF.Ln, bias=aux[:, 255:256]),
            waits=[t_ag] + B("epsap").ready, incs=("se",))
        t_rs = S.add("se", lambda e, so=so: e.activation(
            aux[:, so + 15:so + 16], aux[:, so + 14:so + 15], AF.Exp, scale=-0.5),
            waits=[t_ln], incs=("se",))
        t_mr = S.add("ve", lambda e, so=so: e.tensor_tensor(
            aux[:, so + 16:so + 17], aux[:, so + 12:so + 13], aux[:, so + 15:so + 16], ALU.mult),
            waits=[t_ag, t_rs], incs=("ve",))
        t_nm = S.add("ve", lambda e, so=so: e.tensor_scalar(
            aux[:, so + 17:so + 18], aux[:, so + 16:so + 17], -1.0, None, ALU.mult),
            waits=[t_mr], incs=("ve",))
        st.ready = [t_nm, t_rs]
        return aux[:, so + 15:so + 16], aux[:, so + 17:so + 18], [t_nm, t_rs]

    def ffn_phase(b, tens):
        if os.environ.get("KDBG_STAGE") in ("proj", "attn"):
            drain_dmas()
            return
        (w2_sb, x2t_sb, h_sb, w1_buf, b2b_sb, x2fr_buf, t1y_buf, t2y_buf,
         outb_buf, aux) = tens
        w2b = B(w2_sb.name)
        b2b = B(b2b_sb.name)
        dma("sp", b2b_sb[:], b2_in[:], b2b.begin_write(), "dma_t", b2b)
        w2_chunks = []
        xb_chunks = []

        for hf in range(2):
            x2t = B("x2t")

            load_set(SET_GELU)
            hb = B(h_sb.name)
            h_war = hb.begin_write()
            hb.ready = []
            hslots = [("psS0", psS, 0, 512), ("psS1", psS, 512, 512),
                      ("psS2", psS, 1024, 512), ("psS3", psS, 1536, 512),
                      ("psC0", psC[0], 0, 512), ("psC1", psC[1], 0, 512),
                      ("psC2", psC[2], 0, 512), ("psC3", psC[3], 0, 512)]
            hrot = Rot(hslots)
            for ft in range(FF // 128):
                wsl = ft % 2
                w1b = B(f"w1{wsl}")
                w1_war = w1b.begin_write()
                wtok = S.add("gp", lambda e, ft=ft, wsl=wsl: e.dma_start(
                    out=w1_buf[:, wsl * 1024:(wsl + 1) * 1024]
                    .rearrange("p (kc m) -> p kc m", kc=8),
                    in_=w1_in[:, ft * 128:(ft + 1) * 128]
                    .rearrange("(kc p) m -> p kc m", p=128)),
                    waits=list(w1_war) + [(f"dma_w1{wsl}", S.cnt[f"dma_w1{wsl}"])],
                    incs=[(f"dma_w1{wsl}", 16)])
                w1b.ready = [wtok]
                if hf == 0 and 2 <= ft < 10:
                    k = ft - 2
                    war = w2b.begin_write() if k == 0 else []
                    ctok = S.add("gp", lambda e, k=k: e.dma_start(
                        out=w2_sb[:, k * 4096:(k + 1) * 4096]
                        .rearrange("p (fc m) -> p fc m", fc=4),
                        in_=w2_in[k * 512:(k + 1) * 512, :]
                        .rearrange("(fc p) m -> p fc m", p=128)),
                        waits=list(war) + [("dma_w", S.cnt["dma_w"])],
                        incs=[("dma_w", 16)])
                    w2_chunks.append(ctok)
                    if k == 7:
                        w2b.ready = w2_chunks
                if hf == 0 and 10 <= ft < 18 and b + 1 < B_LOC:
                    k = ft - 10
                    xbn = B(f"xb{b + 1}")
                    war = xbn.begin_write() if k == 0 else []
                    ctok = S.add("gp", lambda e, k=k, nb=b + 1: e.dma_start(
                        out=xb_scr[nb][k * 256:(k + 1) * 256, :],
                        in_=x_in[nb, k * 256:(k + 1) * 256, :]),
                        waits=list(war) + [("dma_xg", S.cnt["dma_xg"])],
                        incs=[("dma_xg", 16)])
                    xb_chunks.append(ctok)
                    if k == 7:
                        xbn.ready = xb_chunks
                        xt_prefetch.add(b + 1)
                slots2 = [hrot.next(), hrot.next()]
                bufs2 = [B(s[0]) for s in slots2]
                wars2 = [bb.begin_write() for bb in bufs2]
                aps2 = [s[1][:, s[2]:s[2] + s[3]] for s in slots2]
                mtoks2 = [None, None]
                for kc in range(8):
                    for tc2 in range(2):
                        w_ = list(([wtok] + x2t.ready))
                        if kc == 0:
                            w_ += wars2[tc2]
                            if ft == 0 and tc2 == 0:
                                w_ += list(h_war)
                        mtoks2[tc2] = S.add(
                            "te",
                            lambda e, ap=aps2[tc2], l=w1_buf[:, wsl * 1024 + kc * 128: wsl * 1024 + kc * 128 + 128],
                            r=x2t_sb[:, kc * 1024 + tc2 * 512: kc * 1024 + tc2 * 512 + 512],
                            st=(kc == 0), sp=(kc == 7): e.matmul(ap, l, r, start=st, stop=sp),
                            waits=w_, incs=("te",) if kc == 7 else ())
                for tc2 in range(2):
                    bufs2[tc2].ready = [mtoks2[tc2]]
                    w1b.users.append(mtoks2[tc2])
                    x2t.users.append(mtoks2[tc2])
                    gtok = S.add("se", lambda e, ap=aps2[tc2], ft=ft, tc2=tc2: e.activation(
                        h_sb[:, ft * 1024 + tc2 * 512: ft * 1024 + tc2 * 512 + 512], ap,
                        AF.Gelu_apprx_tanh, bias=b1_sb_global[:, ft:ft + 1]),
                        waits=[mtoks2[tc2]] + B(b1_sb_global.name).ready, incs=("se",))
                    bufs2[tc2].users.append(gtok)
                    hb.ready.append(gtok)

            load_set(SET_LNEXP)
            fac = B("fac")
            fac_war = fac.begin_write()
            fac_toks = []
            for tbl in range(8):
                rbslot = ("psC0", psC[0], 0, 512)
                pieces = [(x2t_sb[:, kc * 1024 + tbl * 128: kc * 1024 + tbl * 128 + 128],
                           wr_sb_global[:, kc * 8:(kc + 1) * 8],
                           x2t.ready + B(wr_sb_global.name).ready) for kc in range(8)]
                name, tensor, coloff, _ = rbslot
                pbufR = B(name)
                warR = pbufR.begin_write()
                apR = tensor[:, 0:8]
                mtokR = None
                for j, (lhs, rhs, ws) in enumerate(pieces):
                    mtokR = S.add("te", lambda e, apR=apR, lhs=lhs, rhs=rhs,
                                  st=(j == 0), sp=(j == 7): e.matmul(apR, lhs, rhs, start=st, stop=sp),
                                  waits=(list(ws) + warR) if j == 0 else list(ws),
                                  incs=("te",) if j == 7 else ())
                pbufR.ready = [mtokR]
                x2t.users.append(mtokR)
                B(wr_sb_global.name).users.append(mtokR)
                ro = 32 + tbl * 2
                lgc = 64 + (tbl % 2) * 32
                t0 = S.add("ve", lambda e, lgc=lgc: e.tensor_copy(aux[:, lgc:lgc + 8], psC[0][:, 0:8]),
                           waits=[mtokR] + B(f"rt{tbl % 2}").begin_write(), incs=("ve",))
                pbufR.users.append(t0)
                t1 = S.add("ve", lambda e, lgc=lgc: e.reduce_max(
                    aux[:, lgc + 8:lgc + 9], aux[:, lgc:lgc + 8], axis=mybir.AxisListType.X),
                    waits=[t0], incs=("ve",))
                t2 = S.add("ve", lambda e, lgc=lgc: e.tensor_scalar(
                    aux[:, lgc + 9:lgc + 17], aux[:, lgc:lgc + 8], aux[:, lgc + 8:lgc + 9],
                    -1e30, ALU.is_ge, op1=ALU.mult),
                    waits=[t1], incs=("ve",))
                t3 = S.add("ve", lambda e, lgc=lgc: e.tensor_tensor(
                    aux[:, lgc + 9:lgc + 17], aux[:, lgc:lgc + 8], aux[:, lgc + 9:lgc + 17], ALU.add),
                    waits=[t2], incs=("ve",))
                t4 = S.add("ve", lambda e, lgc=lgc: e.reduce_max(
                    aux[:, lgc + 17:lgc + 18], aux[:, lgc + 9:lgc + 17], axis=mybir.AxisListType.X),
                    waits=[t3], incs=("ve",))
                t5 = S.add("ve", lambda e, lgc=lgc: e.tensor_tensor(
                    aux[:, lgc + 18:lgc + 19], aux[:, lgc + 8:lgc + 9], aux[:, lgc + 17:lgc + 18],
                    ALU.subtract),
                    waits=[t4], incs=("ve",))
                t6 = S.add("se", lambda e, lgc=lgc: e.activation(
                    aux[:, lgc + 19:lgc + 20], aux[:, lgc + 18:lgc + 19], AF.Exp, scale=-1.0),
                    waits=[t5], incs=("se",))
                t7 = S.add("se", lambda e, lgc=lgc: e.activation(
                    aux[:, lgc + 20:lgc + 21], aux[:, lgc + 18:lgc + 19], AF.Exp),
                    waits=[t5], incs=("se",))
                t8 = S.add("ve", lambda e, lgc=lgc: e.tensor_scalar(
                    aux[:, lgc + 21:lgc + 22], aux[:, lgc + 19:lgc + 20], 1.0, None, ALU.add),
                    waits=[t6], incs=("ve",))
                t9 = S.add("ve", lambda e, lgc=lgc: e.reciprocal(
                    aux[:, lgc + 22:lgc + 23], aux[:, lgc + 21:lgc + 22]),
                    waits=[t8], incs=("ve",))
                t10 = S.add("ve", lambda e, lgc=lgc: e.tensor_scalar(
                    aux[:, lgc + 23:lgc + 24], aux[:, lgc + 20:lgc + 21], 1.0, None, ALU.add),
                    waits=[t7], incs=("ve",))
                t11 = S.add("ve", lambda e, lgc=lgc: e.reciprocal(
                    aux[:, lgc + 24:lgc + 25], aux[:, lgc + 23:lgc + 24]),
                    waits=[t10], incs=("ve",))
                t12 = S.add("ve", lambda e, lgc=lgc, tbl=tbl: e.tensor_tensor(
                    aux[:, 240 + tbl:241 + tbl], aux[:, lgc + 22:lgc + 23],
                    aux[:, lgc + 24:lgc + 25], ALU.add),
                    waits=[t9, t11] + (fac_war if tbl == 0 else []), incs=("ve",))
                B(f"rt{tbl % 2}").ready = [t12]
                fac_toks.append(t12)
            fac.ready = fac_toks
            if hf == 0:
                emit_x2t(b, 1, x2t_sb)
            elif b + 1 in xt_prefetch:
                xtb = B("xt")
                xt_war = xtb.begin_write()
                toks = []
                for et in range(8):
                    tok = S.add("sp", lambda e, nb=b + 1, et=et: e.dma_start(
                        out=xt_buf_g[:, et * 512: et * 512 + 512],
                        in_=xb_scr[nb][0:512, et * 128:(et + 1) * 128],
                        transpose=True),
                        waits=(xt_war if et == 0 else []) + B(f"xb{b + 1}").ready,
                        incs=[("dma_xt", 16)])
                    toks.append(tok)
                xtb.ready = toks

            for g in range(2):
                yslots = [[("psS0", psS, 0, 512), ("psS1", psS, 512, 512)],
                          [("psS2", psS, 1024, 512), ("psS3", psS, 1536, 512)],
                          [("psC0", psC[0], 0, 512), ("psC1", psC[1], 0, 512)],
                          [("psC2", psC[2], 0, 512), ("psC3", psC[3], 0, 512)]]
                ywar = []
                yaps = []
                for j in range(4):
                    aps = []
                    for ec in range(2):
                        nm, tsr, co, wd_ = yslots[j][ec]
                        bb = B(nm)
                        ywar.append((j, ec, bb, bb.begin_write()))
                        aps.append(tsr[:, co:co + wd_])
                    yaps.append(aps)
                ytoks = [[None, None] for _ in range(4)]
                nfc = FF // 128
                for fc in range(nfc):
                    for j in range(4):
                        tbl = g * 4 + j
                        for ec in range(2):
                            war = []
                            if fc == 0:
                                for (jj, eecc, bb, w) in ywar:
                                    if jj == j and eecc == ec:
                                        war = w
                            tok = S.add("te", lambda e, j=j, ec=ec, fc=fc, tbl=tbl:
                                        e.matmul(
                                            yaps_g[j][ec],
                                            h_sb[:, fc * 1024 + tbl * 128: fc * 1024 + tbl * 128 + 128],
                                            w2_sb[:, fc * E + ec * 512: fc * E + ec * 512 + 512],
                                            start=(fc == 0), stop=(fc == nfc - 1)),
                                        waits=(hb.ready + w2b.ready + war) if fc == 0 else [],
                                        incs=("te",) if fc == nfc - 1 else ())
                            if fc == nfc - 1:
                                ytoks[j][ec] = tok
                                hb.users.append(tok)
                                w2b.users.append(tok)
                for j in range(4):
                    tbl = g * 4 + j
                    trow = hf * 1024 + tbl * 128
                    sl = j % 2
                    xfr = B(f"x2fr{sl}")
                    xfr_war = xfr.begin_write()
                    xtok = dma("sp", x2fr_buf[:, sl * E:(sl + 1) * E],
                               x2f_scr[b][trow:trow + 128, :],
                               xfr_war + B(f"x2fs{b}").ready, f"dma_fr{sl}", xfr)
                    ty = B(f"t1y{sl}")
                    ty_war = ty.begin_write()
                    ty.ready = []
                    for ec in range(2):
                        ttok = S.add("ve", lambda e, j=j, ec=ec, sl=sl: e.tensor_tensor(
                            t1y_buf[:, sl * E + ec * 512: sl * E + ec * 512 + 512],
                            yaps_g[j][ec], b2b_sb[:, ec * 512:(ec + 1) * 512], ALU.add),
                            waits=[ytoks[j][ec]] + b2b.ready + (ty_war if ec == 0 else []),
                            incs=("ve",))
                        b2b.users.append(ttok)
                        for (jj, eecc, bb, w) in ywar:
                            if jj == j and eecc == ec:
                                bb.users.append(ttok)
                        ty.ready.append(ttok)
                    t2y = B(f"t2y{sl}")
                    t2y_war = t2y.begin_write()
                    ftok = S.add("ve", lambda e, sl=sl, tbl=tbl: e.tensor_scalar(
                        t2y_buf[:, sl * E:(sl + 1) * E], t1y_buf[:, sl * E:(sl + 1) * E],
                        aux[:, 240 + tbl:241 + tbl], None, ALU.mult),
                        waits=ty.ready + fac.ready + t2y_war, incs=("ve",))
                    ty.users.append(ftok)
                    fac.users.append(ftok)
                    t2y.ready = [ftok]
                    rs2 = B(f"rs2{sl}")
                    rs2_war = rs2.begin_write()
                    rtoks = []
                    for ec in range(2):
                        rtok = S.add("ve", lambda e, sl=sl, ec=ec: e.tensor_tensor(
                            t1y_buf[:, sl * E + ec * 512: sl * E + ec * 512 + 512],
                            t2y_buf[:, sl * E + ec * 512: sl * E + ec * 512 + 512],
                            x2fr_buf[:, sl * E + ec * 512: sl * E + ec * 512 + 512], ALU.add),
                            waits=[ftok, xtok] + (rs2_war + ty.begin_write() if ec == 0 else []),
                            incs=("ve",))
                        t2y.users.append(rtok)
                        xfr.users.append(rtok)
                        rtoks.append(rtok)
                    ty.ready = rtoks
                    rstd_ap, nmr_ap, st_toks = ln_stats(rtoks, t1y_buf[:, sl * E:(sl + 1) * E], sl, aux)
                    ob = B(f"outb{sl}")
                    ob_war = ob.begin_write()
                    otok = S.add("ve", lambda e, sl=sl, r=rstd_ap, nm=nmr_ap: e.tensor_scalar(
                        outb_buf[:, sl * E:(sl + 1) * E], t1y_buf[:, sl * E:(sl + 1) * E],
                        r, nm, ALU.mult, op1=ALU.add),
                        waits=st_toks + rtoks + ob_war, incs=("ve",))
                    ty.users.append(otok)
                    ob.ready = [otok]
                    stok = S.add("sp", lambda e, b=b, trow=trow, sl=sl: e.dma_start(
                        out=out_d[b, trow:trow + 128, :], in_=outb_buf[:, sl * E:(sl + 1) * E]),
                        waits=[otok], incs=[(f"dma_o{sl}", 16)])
                    ob.users.append(stok)
        drain_dmas()

    global b1_sb_global, wr_sb_global, yaps_g
    with nc.sbuf_tensor("b1_sb", [128, FF // 128], F32) as b1_sb, \
         nc.sbuf_tensor("wr_sb", [128, 64], BF16) as wr_sb, \
         nc.sbuf_tensor("aux", [128, 512], F32) as aux, \
         nc.sbuf_tensor("pswap_sb", [128, 128], BF16) as pswap_sb_g, \
         nc.sbuf_tensor("wd_sb", [128, 8 * LAT], BF16) as wd_sb_g, \
         nc.sbuf_tensor("xt_buf", [128, 8 * 512], BF16, is_transpose_dest=True) as xt_buf_g:
        b1_sb_global = b1_sb
        wr_sb_global = wr_sb
        S.new_phase()
        bb = B(b1_sb.name)
        dma("sp", b1_sb[:], b1_in[:], bb.begin_write(), "dma_b1", bb)
        bb = B(wr_sb.name)
        dma("gp", wr_sb[:].rearrange("p (kc m) -> p kc m", kc=8),
            wr_in.rearrange("(kc p) m -> p kc m", p=128),
            bb.begin_write(), "dma_w", bb)
        epsb = B("epsap")
        etok = S.add("ve", lambda e: e.memset(aux[:, 255:256], EPS), incs=("ve",))
        epsb.ready = [etok]
        bb = B(wd_sb_g.name)
        dma("gp", wd_sb_g[:].rearrange("p (kc m) -> p kc m", kc=8),
            wd_in.rearrange("(kc p) m -> p kc m", p=128), bb.begin_write(), "dma_w", bb)
        bb = B(pswap_sb_g.name)
        dma("gp", pswap_sb_g[:], psw_in[:], bb.begin_write(), "dma_w", bb)
        xbb = B("xb0")
        dma("gp", xb_scr[0][:], x_in[0], xbb.begin_write(), "dma_xg", xbb)
        S.emit_phase()

        for b in range(B_LOC):
            xstk = ExitStack()
            bstk = ExitStack()
            x2t_sb = xstk.enter_context(
                nc.sbuf_tensor(f"x2t_sb{b}", [128, 8 * 1024], BF16, is_transpose_dest=True))
            if True:
                ben = bstk.enter_context
                kr_sb = ben(nc.sbuf_tensor(f"kr_sb{b}", [128, 8 * T], BF16))
                vaug_sb = ben(nc.sbuf_tensor(f"vaug_sb{b}", [128, 16 * H * 128], BF16))
                with ExitStack() as stk:
                    en = stk.enter_context
                    wq_sb = en(nc.sbuf_tensor(f"wq_sb{b}", [128, 8 * E], BF16))
                    wd_sb = wd_sb_g
                    wk_sb = en(nc.sbuf_tensor(f"wk_sb{b}", [128, 4 * E], BF16))
                    wv_sb = en(nc.sbuf_tensor(f"wv_sb{b}", [128, 4 * E], BF16))
                    ct_sb = en(nc.sbuf_tensor(f"ct_sb{b}", [128, 2 * 512], F32))
                    st_sb = en(nc.sbuf_tensor(f"st_sb{b}", [128, 2 * 512], F32))
                    xt_buf = xt_buf_g
                    lat_buf = en(nc.sbuf_tensor(f"lat_buf{b}", [128, 2 * 4 * 512], BF16))
                    qsp_buf = en(nc.sbuf_tensor(f"qsp_buf{b}", [128, 2 * 512], BF16))
                    qbf_tmp = en(nc.sbuf_tensor(f"qbf_tmp{b}", [128, 2 * 512], BF16))
                    t1_tmp = en(nc.sbuf_tensor(f"t1_tmp{b}", [128, 2 * 512], F32))
                    t2_tmp = en(nc.sbuf_tensor(f"t2_tmp{b}", [128, 2 * 512], F32))
                    S.new_phase()
                    proj_phase(b, (wq_sb, wd_sb, wk_sb, wv_sb, ct_sb, st_sb,
                                   kr_sb, vaug_sb, xt_buf, lat_buf, qsp_buf,
                                   pswap_sb_g, aux, qbf_tmp, t1_tmp, t2_tmp))
                    S.emit_phase()
                with ExitStack() as stk:
                    en = stk.enter_context
                    wo_sb = en(nc.sbuf_tensor(f"wo_sb{b}", [128, 8 * E], BF16))
                    qch_buf = en(nc.sbuf_tensor(f"qch_buf{b}", [128, 2 * 8 * QC], BF16))
                    pt_buf = en(nc.sbuf_tensor(f"pt_buf{b}", [128, 2 * 2048], BF16))
                    rb_sb = en(nc.sbuf_tensor(f"rb_sb{b}", [64, 4 * 512], F32))
                    rb_scr = en(nc.sbuf_tensor(f"rb_scr{b}", [64, 2 * 512], F32))
                    ao_sb = en(nc.sbuf_tensor(f"ao_sb{b}", [128, 8 * 512], BF16))
                    xres_buf = en(nc.sbuf_tensor(f"xres_buf{b}", [128, 2 * E], F32))
                    rs1_buf = en(nc.sbuf_tensor(f"rs1_buf{b}", [128, 2 * E], F32))
                    S.new_phase()
                    attn_phase(b, (wo_sb, kr_sb, vaug_sb, qch_buf, pt_buf,
                                   rb_sb, rb_scr, ao_sb, xres_buf, rs1_buf,
                                   x2t_sb, aux))
                    S.emit_phase()
            bstk.close()
            with ExitStack() as stk:
                en = stk.enter_context
                w2_sb = en(nc.sbuf_tensor(f"w2_sb{b}", [128, 32 * E], BF16))
                h_sb = en(nc.sbuf_tensor(f"h_sb{b}", [128, 32 * 1024], BF16))
                w1_buf = en(nc.sbuf_tensor(f"w1_buf{b}", [128, 2 * 1024], BF16))
                b2b_sb = en(nc.sbuf_tensor(f"b2b_sb{b}", [128, E], F32))
                x2fr_buf = en(nc.sbuf_tensor(f"x2fr_buf{b}", [128, 2 * E], F32))
                t1y_buf = en(nc.sbuf_tensor(f"t1y_buf{b}", [128, 2 * E], F32))
                t2y_buf = en(nc.sbuf_tensor(f"t2y_buf{b}", [128, 2 * E], F32))
                outb_buf = en(nc.sbuf_tensor(f"outb_buf{b}", [128, 2 * E], F32))
                S.new_phase()
                yaps_g = [[psS[:, 0:512], psS[:, 512:1024]],
                          [psS[:, 1024:1536], psS[:, 1536:2048]],
                          [psC[0][:, :], psC[1][:, :]],
                          [psC[2][:, :], psC[3][:, :]]]
                ffn_phase(b, (w2_sb, x2t_sb, h_sb, w1_buf, b2b_sb, x2fr_buf,
                              t1y_buf, t2y_buf, outb_buf, aux))
                S.emit_phase()
            xstk.close()

    nc.finalize()
    return nc


def host_consts():
    inv = 1.0 / (10000.0 ** (np.arange(0, D, 2, dtype=np.float32) / D))
    ang = np.outer(np.arange(T, dtype=np.float32), inv)
    cos = np.cos(ang)
    sin = np.sin(ang)
    r = np.arange(128)
    i = (r % 64) // 2
    ctab = cos[:, i].T.astype(np.float32).copy()
    sgn = np.where(r % 2 == 0, -1.0, 1.0).astype(np.float32)
    stab = (sin[:, i].T * sgn[:, None]).astype(np.float32).copy()
    perm = np.arange(128) ^ 1
    pswap = np.zeros((128, 128), dtype=np.float32)
    pswap[perm, np.arange(128)] = 1.0
    return ctab, stab, pswap


_prog_cache = {}


def kernel(L0_attn_x, L0_attn_wq, L0_attn_w_down, L0_attn_wk_up, L0_attn_wv_up,
           L0_attn_wo, L0_ffn_router_w, L0_ffn_w1, L0_ffn_b1, L0_ffn_w2, L0_ffn_b2):
    x = np.ascontiguousarray(np.asarray(L0_attn_x, dtype=np.float32))
    ctab, stab, pswap = host_consts()
    common = {
        "wq": np.ascontiguousarray(np.asarray(L0_attn_wq, dtype=np.float32)),
        "wd": np.ascontiguousarray(np.asarray(L0_attn_w_down, dtype=np.float32)),
        "wk": np.ascontiguousarray(np.asarray(L0_attn_wk_up, dtype=np.float32)),
        "wv": np.ascontiguousarray(np.asarray(L0_attn_wv_up, dtype=np.float32)),
        "wo": np.ascontiguousarray(np.asarray(L0_attn_wo, dtype=np.float32)),
        "wr": np.ascontiguousarray(np.asarray(L0_ffn_router_w, dtype=np.float32)),
        "w1": np.ascontiguousarray(np.asarray(L0_ffn_w1, dtype=np.float32)),
        "w2": np.ascontiguousarray(np.asarray(L0_ffn_w2, dtype=np.float32)),
        "b1l": np.ascontiguousarray(np.asarray(L0_ffn_b1, dtype=np.float32).reshape(FF // 128, 128).T),
        "b2b": np.ascontiguousarray(np.broadcast_to(np.asarray(L0_ffn_b2, dtype=np.float32), (128, E))),
        "ctab": ctab, "stab": stab, "pswap": pswap,
    }
    if "nc" not in _prog_cache:
        _prog_cache["nc"] = build_program()
    nc = _prog_cache["nc"]
    in_maps = []
    for c in range(N_CORES):
        m = dict(common)
        m["x"] = np.ascontiguousarray(x[c * B_LOC:(c + 1) * B_LOC])
        in_maps.append(m)
    res = run_bass_kernel_spmd(nc, in_maps, core_ids=list(range(N_CORES)))
    return np.concatenate([r["out"] for r in res.results], axis=0)


# version 19
# speedup vs baseline: 1.1815x; 1.0090x over previous
import os
import numpy as np
import ml_dtypes
from contextlib import ExitStack
import concourse.bass as bass
import concourse.mybir as mybir
from concourse import dve_ops
from concourse.bass_utils import run_bass_kernel_spmd

F32 = mybir.dt.float32
BF16 = mybir.dt.bfloat16
AF = mybir.ActivationFunctionType
ALU = mybir.AluOpType

B_LOC = 2
T = 2048
E = 1024
H = 16
D = 64
LAT = 512
FF = 4096
QC = 512
NQC = T // QC
EPS = 1e-5
SET_LNEXP = 6
SET_GELU = 11
N_CORES = 8


class Buf:
    def __init__(self):
        self.ready = []
        self.users = []

    def begin_write(self):
        w = self.users + self.ready
        self.ready = []
        self.users = []
        return w


class Sched:
    def __init__(self, nc, semnames):
        self.nc = nc
        self.sems = {n: nc.alloc_semaphore(n) for n in semnames}
        self.cnt = {n: 0 for n in semnames}
        self.ops = None

    def new_phase(self):
        self.ops = []

    def add(self, engine, fn, waits=(), incs=()):
        norm = []
        for inc in incs:
            sem, n = inc if isinstance(inc, tuple) else (inc, 1)
            self.cnt[sem] += n
            norm.append((sem, n))
        wm = {}
        for w in waits:
            if w is None:
                continue
            sem, val = w
            if val > 0:
                wm[sem] = max(wm.get(sem, 0), val)
        self.ops.append((engine, tuple(sorted(wm.items())), tuple(norm), fn))
        return (norm[0][0], self.cnt[norm[0][0]]) if norm else None

    def emit_phase(self):
        ops = self.ops
        self.ops = None
        with self.nc.Block() as blk:
            decos = {"te": blk.tensor, "se": blk.scalar, "ve": blk.vector,
                     "gp": blk.gpsimd, "sp": blk.sync}
            for name in ("te", "se", "ve", "gp", "sp"):
                myops = [(w, i, f) for (e, w, i, f) in ops if e == name]
                if not myops:
                    continue

                def make_body(myops=myops):
                    def body(eng):
                        seen = {}
                        for waits, incs, fn in myops:
                            for sem, val in waits:
                                if seen.get(sem, 0) < val:
                                    eng.wait_ge(self.sems[sem], val)
                                    seen[sem] = val
                            if fn is None:
                                continue
                            inst = fn(eng)
                            for sem, n in incs:
                                inst.then_inc(self.sems[sem], n)
                    return body

                decos[name](make_body())


def build_program():
    nc = bass.Bass("TRN2", target_bir_lowering=False)

    x_in = nc.dram_tensor("x", [B_LOC, T, E], F32, kind="ExternalInput")
    wq_in = nc.dram_tensor("wq", [E, E], F32, kind="ExternalInput")
    wd_in = nc.dram_tensor("wd", [E, LAT], F32, kind="ExternalInput")
    wk_in = nc.dram_tensor("wk", [LAT, E], F32, kind="ExternalInput")
    wv_in = nc.dram_tensor("wv", [LAT, E], F32, kind="ExternalInput")
    wo_in = nc.dram_tensor("wo", [E, E], F32, kind="ExternalInput")
    wr_in = nc.dram_tensor("wr", [E, 8], F32, kind="ExternalInput")
    w1_in = nc.dram_tensor("w1", [E, FF], F32, kind="ExternalInput")
    w2_in = nc.dram_tensor("w2", [FF, E], F32, kind="ExternalInput")
    b1_in = nc.dram_tensor("b1l", [128, FF // 128], F32, kind="ExternalInput")
    b2_in = nc.dram_tensor("b2b", [128, E], F32, kind="ExternalInput")
    ct_in = nc.dram_tensor("ctab", [128, T], F32, kind="ExternalInput")
    st_in = nc.dram_tensor("stab", [128, T], F32, kind="ExternalInput")
    psw_in = nc.dram_tensor("pswap", [128, 128], F32, kind="ExternalInput")
    out_d = nc.dram_tensor("out", [B_LOC, T, E], F32, kind="ExternalOutput")

    xb_scr = [nc.dram_tensor(f"xb_scr{b}", [T, E], BF16, kind="Internal") for b in range(B_LOC)]
    q_scr = [nc.dram_tensor(f"q_scr{b}", [NQC, 8, 128, QC], BF16, kind="Internal") for b in range(B_LOC)]
    x2f_scr = [nc.dram_tensor(f"x2f_scr{b}", [T, E], F32, kind="Internal") for b in range(B_LOC)]
    x2b_scr = [nc.dram_tensor(f"x2b_scr{b}", [T, E], BF16, kind="Internal") for b in range(B_LOC)]

    psS = nc.alloc_psum_tensor("psS", [128, 2048], F32)
    psC = [nc.alloc_psum_tensor(f"psC{i}", [128, 512], F32) for i in range(4)]

    S = Sched(nc, ["dma_w", "dma_t", "dma_xg", "dma_xt", "dma_ct0", "dma_ct1",
                   "dma_st0", "dma_st1", "dma_qs0", "dma_qs1", "dma_qch0",
                   "dma_xr0", "dma_xr1", "dma_xf0", "dma_xf1", "dma_xb0",
                   "dma_xb1", "dma_w10", "dma_w11", "dma_x2t", "dma_fr0",
                   "dma_fr1", "dma_b1", "dma_sel", "dma_qch1", "dma_o0", "dma_o1", "te", "se", "ve", "gp"])
    bufs = {}

    def B(name):
        if name not in bufs:
            bufs[name] = Buf()
        return bufs[name]

    slots8 = [("psS0", psS, 0, 512), ("psS1", psS, 512, 512),
              ("psS2", psS, 1024, 512), ("psS3", psS, 1536, 512),
              ("psC0", psC[0], 0, 512), ("psC1", psC[1], 0, 512),
              ("psC2", psC[2], 0, 512), ("psC3", psC[3], 0, 512)]

    class Rot:
        def __init__(self, slots):
            self.slots = slots
            self.i = 0

        def next(self):
            s = self.slots[self.i % len(self.slots)]
            self.i += 1
            return s

    def mm_group(slot, pieces, extra_waits=(), reads=()):
        name, tensor, coloff, width = slot
        buf = B(name)
        war = buf.begin_write()
        ap = tensor[:, coloff:coloff + width]
        n = len(pieces)
        tok = None
        for j, (lhs, rhs, ws) in enumerate(pieces):
            waits = list(ws)
            if j == 0:
                waits += war + list(extra_waits)
            tok = S.add(
                "te",
                lambda e, ap=ap, lhs=lhs, rhs=rhs, st=(j == 0), sp=(j == n - 1):
                    e.matmul(ap, lhs, rhs, start=st, stop=sp),
                waits=waits,
                incs=("te",) if j == n - 1 else ())
        buf.ready = [tok]
        for rb in reads:
            rb.users.append(tok)
        return ap, tok, buf

    def dma(queue, out_ap, in_ap, waits, sem, buf=None):
        waits = list(waits)
        if queue == "gp":
            waits.append((sem, S.cnt[sem]))
        tok = S.add(queue, lambda e, o=out_ap, i=in_ap: e.dma_start(out=o, in_=i),
                    waits=waits, incs=[(sem, 16)])
        if buf is not None:
            buf.ready = [tok]
        return tok

    def load_set(set_id):
        if os.environ.get("KDBG_NOACT"):
            return
        S.add("se", lambda e, s=set_id: e.add_instruction(
            mybir.InstLoadActFuncSet(
                name=e.bass.get_next_instruction_name(),
                act_func_set_id=s, ins=[], outs=[])))

    def drain_dmas():
        S.add("sp", None, waits=[(n, S.cnt[n]) for n in S.sems if n.startswith("dma_")])

    xt_prefetch = set()

    def proj_phase(b, tens):
        (wq_sb, wd_sb, wk_sb, wv_sb, ct_sb, st_sb, kr_sb, vaug_sb,
         xt_buf, lat_buf, qsp_buf, pswap_sb, aux, qbf_tmp, t1_tmp,
         t2_tmp) = tens
        rot = Rot(slots8)

        for sbuf, dram, nkc in ((wq_sb, wq_in, 8),
                                (wk_sb, wk_in, 4), (wv_sb, wv_in, 4)):
            bb = B(sbuf.name)
            dma("gp", sbuf[:].rearrange("p (kc m) -> p kc m", kc=nkc),
                dram.rearrange("(kc p) m -> p kc m", p=128),
                bb.begin_write(), "dma_w", bb)
        load_set(SET_LNEXP)

        xbb = B(f"xb{b}")
        if not xbb.ready:
            dma("gp", xb_scr[b][:], x_in[b], xbb.begin_write(), "dma_xg", xbb)

        kr = B(kr_sb.name)
        kr_war = kr.begin_write()
        va = B(vaug_sb.name)
        va_war = va.begin_write()
        ones_tok = S.add("gp", lambda e: e.memset(
            vaug_sb[:].rearrange("p (tt h d) -> p tt h d", tt=16, h=H)[:, :, :, 64:128], 1.0),
            waits=va_war, incs=("gp",))
        va.ready.append(ones_tok)
        qsc = B(f"qsc{b}")
        qsc_war = qsc.begin_write()
        if os.environ.get("KDBG_STAGE") == "setup":
            drain_dmas()
            return

        for tc in range(NQC):
            skip_xt = (tc == 0 and b in xt_prefetch)
            ctb = B(f"ct{tc % 2}")
            toff = (tc % 2) * 512
            cttok = dma("sp", ct_sb[:, toff:toff + 512], ct_in[:, tc * 512:(tc + 1) * 512],
                        ctb.begin_write(), f"dma_ct{tc % 2}", ctb)
            stb = B(f"st{tc % 2}")
            sttok = dma("sp", st_sb[:, toff:toff + 512], st_in[:, tc * 512:(tc + 1) * 512],
                        stb.begin_write(), f"dma_st{tc % 2}", stb)
            xtb = B("xt")
            xoff = 0
            if not skip_xt:
                xt_war = xtb.begin_write()
                xt_toks = []
                for et in range(8):
                    tok = S.add("sp", lambda e, b=b, tc=tc, et=et, xoff=xoff: e.dma_start(
                        out=xt_buf[:, xoff + et * 512: xoff + et * 512 + 512],
                        in_=xb_scr[b][tc * 512:(tc + 1) * 512, et * 128:(et + 1) * 128],
                        transpose=True),
                        waits=(xt_war if et == 0 else []) + xbb.ready,
                        incs=[("dma_xt", 16)])
                    xt_toks.append(tok)
                xtb.ready = xt_toks

            def xt_rhs(kc, xoff=xoff):
                return xt_buf[:, xoff + kc * 512: xoff + kc * 512 + 512]

            latb = B(f"lat{tc % 2}")
            lat_war = latb.begin_write()
            lat_toks = []
            loff = (tc % 2) * 2048
            for lt in range(4):
                pieces = [(wd_sb[:, kc * LAT + lt * 128: kc * LAT + lt * 128 + 128],
                           xt_rhs(kc),
                           B(wd_sb.name).ready + xtb.ready) for kc in range(8)]
                ap, mtok, pbuf = mm_group(rot.next(), pieces, extra_waits=lat_war if lt == 0 else (),
                                          reads=[B(wd_sb.name), xtb])
                etok = S.add("se", lambda e, ap=ap, lt=lt, loff=loff: e.activation(
                    lat_buf[:, loff + lt * 512: loff + lt * 512 + 512], ap, AF.Copy),
                    waits=[mtok], incs=("se",))
                pbuf.users.append(etok)
                lat_toks.append(etok)
            latb.ready = lat_toks

            if os.environ.get("KDBG_STAGE") == "proj1":
                continue
            for which in ("q", "k"):
                for mc in range(8):
                    if which == "q":
                        pieces = [(wq_sb[:, kc * E + mc * 128: kc * E + mc * 128 + 128],
                                   xt_rhs(kc),
                                   B(wq_sb.name).ready + xtb.ready) for kc in range(8)]
                    else:
                        pieces = [(wk_sb[:, kc * E + mc * 128: kc * E + mc * 128 + 128],
                                   lat_buf[:, loff + kc * 512: loff + kc * 512 + 512],
                                   B(wk_sb.name).ready + latb.ready) for kc in range(4)]
                    ap, mtok, pbuf = mm_group(rot.next(), pieces,
                                              reads=[B(wq_sb.name), xtb] if which == "q" else [B(wk_sb.name), latb])
                    sl = mc % 2
                    qb = B(f"qbf{sl}")
                    qb_war = qb.begin_write()
                    qtok = S.add("se", lambda e, ap=ap, sl=sl: e.activation(
                        qbf_tmp[:, sl * 512:sl * 512 + 512], ap, AF.Copy),
                        waits=[mtok] + qb_war, incs=("se",))
                    qb.ready = [qtok]
                    pbuf.users.append(qtok)
                    if os.environ.get("KDBG_STAGE") == "proj2a":
                        continue

                    swslot = rot.next()
                    swname, swt, swoff, _ = swslot
                    swbuf = B(swname)
                    sw_war = swbuf.begin_write()
                    swap = swt[:, swoff:swoff + 512]
                    swtok = S.add("te", lambda e, swap=swap, sl=sl: e.matmul(
                        swap, pswap_sb[:], qbf_tmp[:, sl * 512:sl * 512 + 512],
                        start=True, stop=True),
                        waits=[qtok] + sw_war + B(pswap_sb.name).ready, incs=("te",))
                    swbuf.ready = [swtok]
                    qb.users.append(swtok)

                    if os.environ.get("KDBG_STAGE") == "proj2b":
                        continue
                    t1b = B(f"t1{sl}")
                    t1_war = t1b.begin_write()
                    t1tok = S.add("ve", lambda e, ap=ap, sl=sl, toff=toff: e.tensor_tensor(
                        t1_tmp[:, sl * 512:sl * 512 + 512], ap,
                        ct_sb[:, toff:toff + 512], ALU.mult),
                        waits=[mtok, qtok, cttok] + t1_war, incs=("ve",))
                    ctb.users.append(t1tok)
                    t1b.ready = [t1tok]
                    pbuf.users.append(t1tok)
                    if os.environ.get("KDBG_STAGE") == "proj2d":
                        t1b.users.append(t1tok)
                        continue

                    t2b = B(f"t2{sl}")
                    t2_war = t2b.begin_write()
                    t2tok = S.add("ve", lambda e, swap=swap, sl=sl, toff=toff: e.tensor_tensor(
                        t2_tmp[:, sl * 512:sl * 512 + 512], swap,
                        st_sb[:, toff:toff + 512], ALU.mult),
                        waits=[swtok, sttok] + t2_war, incs=("ve",))
                    stb.users.append(t2tok)
                    t2b.ready = [t2tok]
                    swbuf.users.append(t2tok)

                    if which == "k":
                        ktok = S.add("gp", lambda e, sl=sl, mc=mc, tc=tc: e.tensor_tensor(
                            kr_sb[:, mc * T + tc * 512: mc * T + tc * 512 + 512],
                            t1_tmp[:, sl * 512:sl * 512 + 512],
                            t2_tmp[:, sl * 512:sl * 512 + 512], ALU.add),
                            waits=[t1tok, t2tok] + (kr_war if (mc == 0 and tc == 0) else []),
                            incs=("gp",))
                        kr.ready.append(ktok)
                        t1b.users.append(ktok)
                        t2b.users.append(ktok)
                    else:
                        qsb = B(f"qsp{sl}")
                        qs_war = qsb.begin_write()
                        qrtok = S.add("gp", lambda e, sl=sl: e.tensor_tensor(
                            qsp_buf[:, sl * 512:sl * 512 + 512],
                            t1_tmp[:, sl * 512:sl * 512 + 512],
                            t2_tmp[:, sl * 512:sl * 512 + 512], ALU.add),
                            waits=[t1tok, t2tok] + qs_war, incs=("gp",))
                        t1b.users.append(qrtok)
                        t2b.users.append(qrtok)
                        qsb.ready = [qrtok]
                        if os.environ.get("KDBG_STAGE") == "proj2c":
                            qsb.users.append(qrtok)
                            continue
                        sptok = S.add("sp", lambda e, b=b, tc=tc, mc=mc, sl=sl: e.dma_start(
                            out=q_scr[b][tc, mc], in_=qsp_buf[:, sl * 512:sl * 512 + 512]),
                            waits=[qrtok] + (qsc_war if (mc == 0 and tc == 0) else []),
                            incs=[(f"dma_qs{sl}", 16)])
                        qsb.users.append(sptok)
                        qsc.ready.append(sptok)

            if os.environ.get("KDBG_STAGE") == "proj2":
                continue
            for tt4 in range(4):
                tt = tc * 4 + tt4
                for ec in range(2):
                    pieces = [(lat_buf[:, loff + lc * 512 + tt4 * 128: loff + lc * 512 + tt4 * 128 + 128],
                               wv_sb[:, lc * E + ec * 512: lc * E + ec * 512 + 512],
                               latb.ready + B(wv_sb.name).ready) for lc in range(4)]
                    ap, mtok, pbuf = mm_group(rot.next(), pieces, reads=[latb, B(wv_sb.name)])
                    vtok = S.add("ve", lambda e, ap=ap, tt=tt, ec=ec: e.tensor_copy(
                        vaug_sb[:].rearrange("p (tt h d) -> p tt h d", tt=16, h=H)
                        [:, tt, 8 * ec:8 * ec + 8, 0:64],
                        ap.rearrange("p (h d) -> p h d", h=8)),
                        waits=[mtok], incs=("ve",))
                    pbuf.users.append(vtok)
                    va.ready.append(vtok)

        drain_dmas()

    def emit_x2t(b, hf, x2t_sb):
        x2t = B("x2t")
        x2t_war = x2t.begin_write()
        toks = []
        for et in range(8):
            tok = S.add("sp", lambda e, b=b, hf=hf, et=et, x2t_sb=x2t_sb: e.dma_start(
                out=x2t_sb[:, et * 1024:(et + 1) * 1024],
                in_=x2b_scr[b][hf * 1024:(hf + 1) * 1024, et * 128:(et + 1) * 128],
                transpose=True),
                waits=(x2t_war if et == 0 else []) + B(f"x2bs{b}_h{hf}").ready,
                incs=[("dma_x2t", 16)])
            toks.append(tok)
        x2t.ready = toks
        return x2t

    def attn_phase(b, tens):
        (wo_sb, kr_sb, vaug_sb, qch_buf, pt_buf, rb_sb, rb_scr, ao_sb,
         xres_buf, rs1_buf, x2t_sb, aux) = tens
        if os.environ.get("KDBG_STAGE", "").startswith("proj"):
            drain_dmas()
            return
        load_set(SET_LNEXP)
        bb = B(wo_sb.name)
        dma("gp", wo_sb[:].rearrange("p (kc m) -> p kc m", kc=8),
            wo_in.rearrange("(kc p) m -> p kc m", p=128),
            bb.begin_write(), "dma_w", bb)
        kr = B(kr_sb.name)
        va = B(vaug_sb.name)
        qsc = B(f"qsc{b}")
        def load_qch(c):
            qch = B(f"qch{c % 2}")
            q_war = qch.begin_write()
            qoff = (c % 2) * 8 * QC
            dma("sp", qch_buf[:, qoff:qoff + 8 * QC].rearrange("p (kc n) -> p kc n", kc=8),
                q_scr[b][c].rearrange("kc p n -> p kc n"),
                q_war + qsc.ready, f"dma_qch{c % 2}", qch)
            return qch

        load_qch(0)
        for c in range(NQC):
            qch = B(f"qch{c % 2}")
            qoff = (c % 2) * 8 * QC
            if c + 1 < NQC:
                load_qch(c + 1)
            nkc = 4 * (c + 1)
            ao = B(ao_sb.name)
            ao_war = ao.begin_write()
            for hp in range(8):
                par = hp % 2
                avA_ap = psC[0 + 2 * par]
                avB_ap = psC[1 + 2 * par]
                avA = B(f"psC{0 + 2 * par}")
                avB = B(f"psC{1 + 2 * par}")
                avA_war = avA.begin_write()
                avB_war = avB.begin_write()
                av_last = None
                pt_toks = {}

                def emit_S(kc, hp=hp, c=c, qoff=qoff):
                    snames = ("psS0", "psS1") if (kc % 2 == 0) else ("psS2", "psS3")
                    ao_base = (kc % 2) * 1024
                    sb0, sb1 = B(snames[0]), B(snames[1])
                    s_war = sb0.begin_write() + sb1.begin_write()
                    mts = []
                    for half in range(2):
                        base = half * 64
                        colo = ao_base + half * 512
                        tok = S.add("te", lambda e, colo=colo, base=base, hp=hp, kc=kc, qoff=qoff:
                                    e.matmul(
                                        psS[:, colo:colo + 512],
                                        kr_sb[base:base + 64, hp * T + kc * 128: hp * T + kc * 128 + 128],
                                        qch_buf[base:base + 64, qoff + hp * QC: qoff + hp * QC + QC],
                                        start=True, stop=True, tile_position=(base, 0)),
                                    waits=(s_war + kr.ready + qch.ready) if half == 0 else [],
                                    incs=("te",))
                        mts.append(tok)
                    sb0.ready = [mts[0]]
                    sb1.ready = [mts[1]]
                    kr.users.append(mts[1])
                    qch.users.append(mts[1])
                    masked = (kc >= 4 * c + 2)
                    ptb = B(f"pt{kc % 4}")
                    pt_war = ptb.begin_write()
                    po = (kc % 4) * 1024
                    if not masked:
                        etok = S.add("se", lambda e, po=po, ao_base=ao_base: e.activation(
                            pt_buf[:, po:po + 1024], psS[:, ao_base:ao_base + 1024],
                            AF.Exp, scale=0.125),
                            waits=mts + pt_war, incs=("se",))
                        ptoks = [etok]
                    else:
                        ztok = S.add("ve", lambda e, po=po: e.memset(
                            pt_buf[:, po:po + 1024].rearrange("p (h n) -> p h n", h=2)[:, :, 0:256], 0.0),
                            waits=pt_war, incs=("ve",))
                        etok = S.add("se", lambda e, po=po, ao_base=ao_base: e.activation(
                            pt_buf[:, po:po + 1024].rearrange("p (h n) -> p h n", h=2)[:, :, 256:512],
                            psS[:, ao_base:ao_base + 1024].rearrange("p (h n) -> p h n", h=2)[:, :, 256:512],
                            AF.Exp, scale=0.125),
                            waits=mts + [ztok], incs=("se",))
                        ptoks = [etok, ztok]
                    ptb.ready = ptoks
                    sb0.users.append(etok)
                    sb1.users.append(etok)
                    pt_toks[kc] = ptoks

                def emit_AV(kc, hp=hp):
                    nonlocal av_last
                    po = (kc % 4) * 1024
                    ptb = B(f"pt{kc % 4}")
                    tok = None
                    for half in range(2):
                        av_ap = avA_ap if half == 0 else avB_ap
                        tok = S.add("te", lambda e, av_ap=av_ap, hp=hp, half=half, kc=kc, po=po, nkc=nkc:
                                    e.matmul(
                                        av_ap[:, :],
                                        vaug_sb[:, kc * (H * 128) + (2 * hp + half) * 128:
                                                kc * (H * 128) + (2 * hp + half) * 128 + 128],
                                        pt_buf[:, po + half * 512: po + half * 512 + 512],
                                        start=(kc == 0), stop=(kc == nkc - 1)),
                                    waits=pt_toks[kc] + va.ready +
                                          ((avA_war if half == 0 else avB_war) if kc == 0 else []),
                                    incs=("te",) if half == 1 else ())
                    ptb.users.append(tok)
                    va.users.append(tok)
                    if kc == nkc - 1:
                        av_last = tok

                emit_S(0)
                for kc in range(nkc):
                    if kc + 1 < nkc:
                        emit_S(kc + 1)
                    emit_AV(kc)
                avA.ready = [av_last]
                avB.ready = [av_last]

                rbs = B(f"rb{par}")
                rb_war = rbs.begin_write()
                rtoks = []
                for half in range(2):
                    av_ap = avA_ap if half == 0 else avB_ap
                    ro = (2 * par + half) * 512
                    if half == 0:
                        rtok = S.add("ve", lambda e, av_ap=av_ap, ro=ro:
                                     e.reciprocal(rb_sb[0:64, ro:ro + 512], av_ap[64:128, :]),
                                     waits=[av_last] + rb_war, incs=("ve",))
                    else:
                        scr = rb_scr[0:64, par * 512:par * 512 + 512]
                        scb = B(f"rbscr{par}")
                        scr_war = scb.begin_write()
                        ltok = S.add("se", lambda e, av_ap=av_ap, scr=scr:
                                     e.activation(scr, av_ap[64:128, :], AF.Ln),
                                     waits=[av_last] + scr_war, incs=("se",))
                        scb.ready = [ltok]
                        rtok = S.add("se", lambda e, ro=ro, scr=scr:
                                     e.activation(rb_sb[0:64, ro:ro + 512], scr, AF.Exp, scale=-1.0),
                                     waits=[ltok], incs=("se",))
                        scb.users.append(rtok)
                    rtoks.append(rtok)
                rbs.ready = rtoks
                avA.users.append(rtoks[0])
                avB.users.append(rtoks[1])
                for half in range(2):
                    avbuf = avA if half == 0 else avB
                    av_ap = avA_ap if half == 0 else avB_ap
                    atok = S.add("ve", lambda e, hp=hp, half=half, par=par, av_ap=av_ap: e.tensor_tensor(
                        ao_sb[half * 64:half * 64 + 64, hp * 512:hp * 512 + 512],
                        av_ap[0:64, :],
                        rb_sb[0:64, (2 * par + half) * 512:(2 * par + half) * 512 + 512], ALU.mult),
                        waits=[rtoks[half]] + (ao_war if (hp == 0 and half == 0) else []),
                        incs=("ve",))
                    avbuf.users.append(atok)
                    rbs.users.append(atok)
                    ao.ready.append(atok)
            for tb in range(4):
                trow = c * 512 + tb * 128
                sl = tb % 2
                xr = B(f"xres{sl}")
                xr_war = xr.begin_write()
                xtok = dma("sp", xres_buf[:, sl * E:(sl + 1) * E],
                           x_in[b, trow:trow + 128, :], xr_war, f"dma_xr{sl}", xr)
                rs = B(f"rs1{sl}")
                rs_war = rs.begin_write()
                rs.ready = []
                for ec in range(2):
                    slot = ("psC3", psC[3], 0, 512) if ec == 0 else ("psC2", psC[2], 0, 512)
                    pieces = [(ao_sb[:, kc * 512 + tb * 128: kc * 512 + tb * 128 + 128],
                               wo_sb[:, kc * E + ec * 512: kc * E + ec * 512 + 512],
                               ao.ready + B(wo_sb.name).ready) for kc in range(8)]
                    ap, mtok, pbuf = mm_group(slot, pieces, reads=[ao, B(wo_sb.name)])
                    rtok = S.add("ve", lambda e, ap=ap, sl=sl, ec=ec: e.tensor_tensor(
                        rs1_buf[:, sl * E + ec * 512: sl * E + ec * 512 + 512],
                        ap, xres_buf[:, sl * E + ec * 512: sl * E + ec * 512 + 512], ALU.add),
                        waits=[mtok, xtok] + (rs_war if ec == 0 else []), incs=("ve",))
                    pbuf.users.append(rtok)
                    xr.users.append(rtok)
                    rs.ready.append(rtok)
                rstd_ap, nmr_ap, st_toks = ln_stats(rs.ready, rs1_buf[:, sl * E:(sl + 1) * E], sl, aux)
                xr2_war = xr.begin_write()
                a1 = S.add("ve", lambda e, sl=sl, r=rstd_ap, nm=nmr_ap: e.tensor_scalar(
                    xres_buf[:, sl * E:(sl + 1) * E], rs1_buf[:, sl * E:(sl + 1) * E],
                    r, nm, ALU.mult, op1=ALU.add),
                    waits=st_toks + rs.ready + xr2_war, incs=("ve",))
                xr.ready = [a1]
                rs.users.append(a1)
                sp1 = S.add("sp", lambda e, b=b, trow=trow, sl=sl: e.dma_start(
                    out=x2f_scr[b][trow:trow + 128, :], in_=xres_buf[:, sl * E:(sl + 1) * E]),
                    waits=[a1], incs=[(f"dma_xf{sl}", 16)])
                xr.users.append(sp1)
                B(f"x2fs{b}").ready.append(sp1)
                B(f"x2fs{b}_h{trow // 1024}").ready.append(sp1)
            if c in (1, 3):
                hf_ = (c - 1) // 2
                xc = B(f"x2bs{b}_h{hf_}")
                xc_war = xc.begin_write()
                sptoks = B(f"x2fs{b}_h{hf_}").ready
                ctoks = []
                for k in range(8):
                    r0 = hf_ * 1024 + k * 128
                    ctok = S.add("gp", lambda e, b=b, r0=r0: e.dma_start(
                        out=x2b_scr[b][r0:r0 + 128, :],
                        in_=x2f_scr[b][r0:r0 + 128, :]),
                        waits=(list(xc_war) if k == 0 else []) + [sptoks[k]] +
                              [(f"dma_xb{hf_}", S.cnt[f"dma_xb{hf_}"])],
                        incs=[(f"dma_xb{hf_}", 16)])
                    ctoks.append(ctok)
                xc.ready = ctoks
            if c == 1:
                emit_x2t(b, 0, x2t_sb)
        drain_dmas()

    def ln_stats(src_toks, src_ap, sl, aux):
        so = sl * 24
        st = B(f"lnst{sl}")
        st_war = st.begin_write()
        t_b0 = S.add("ve", lambda e, src=src_ap, so=so: e.bn_stats(
            aux[:, so:so + 6], src[:, 0:512]),
            waits=src_toks + st_war, incs=("ve",))
        t_b1 = S.add("ve", lambda e, src=src_ap, so=so: e.bn_stats(
            aux[:, so + 6:so + 12], src[:, 512:1024]),
            waits=src_toks, incs=("ve",))
        t_ag = S.add("ve", lambda e, so=so: e.bn_aggr(aux[:, so + 12:so + 14], aux[:, so:so + 12]),
                     waits=[t_b0, t_b1], incs=("ve",))
        t_ln = S.add("se", lambda e, so=so: e.activation(
            aux[:, so + 14:so + 15], aux[:, so + 13:so + 14], AF.Ln, bias=aux[:, 255:256]),
            waits=[t_ag] + B("epsap").ready, incs=("se",))
        t_rs = S.add("se", lambda e, so=so: e.activation(
            aux[:, so + 15:so + 16], aux[:, so + 14:so + 15], AF.Exp, scale=-0.5),
            waits=[t_ln], incs=("se",))
        t_mr = S.add("ve", lambda e, so=so: e.tensor_tensor(
            aux[:, so + 16:so + 17], aux[:, so + 12:so + 13], aux[:, so + 15:so + 16], ALU.mult),
            waits=[t_ag, t_rs], incs=("ve",))
        t_nm = S.add("ve", lambda e, so=so: e.tensor_scalar(
            aux[:, so + 17:so + 18], aux[:, so + 16:so + 17], -1.0, None, ALU.mult),
            waits=[t_mr], incs=("ve",))
        st.ready = [t_nm, t_rs]
        return aux[:, so + 15:so + 16], aux[:, so + 17:so + 18], [t_nm, t_rs]

    def ffn_phase(b, tens):
        if os.environ.get("KDBG_STAGE") in ("proj", "attn"):
            drain_dmas()
            return
        (w2_sb, x2t_sb, h_sb, w1_buf, b2b_sb, x2fr_buf, t1y_buf, t2y_buf,
         outb_buf, aux) = tens
        w2b = B(w2_sb.name)
        b2b = B(b2b_sb.name)
        dma("sp", b2b_sb[:], b2_in[:], b2b.begin_write(), "dma_t", b2b)
        w2_chunks = []
        xb_chunks = []

        for hf in range(2):
            x2t = B("x2t")

            load_set(SET_GELU)
            hb = B(h_sb.name)
            h_war = hb.begin_write()
            hb.ready = []
            hslots = [("psS0", psS, 0, 512), ("psS1", psS, 512, 512),
                      ("psS2", psS, 1024, 512), ("psS3", psS, 1536, 512),
                      ("psC0", psC[0], 0, 512), ("psC1", psC[1], 0, 512),
                      ("psC2", psC[2], 0, 512), ("psC3", psC[3], 0, 512)]
            hrot = Rot(hslots)
            for ft in range(FF // 128):
                wsl = ft % 2
                w1b = B(f"w1{wsl}")
                w1_war = w1b.begin_write()
                wtok = S.add("gp", lambda e, ft=ft, wsl=wsl: e.dma_start(
                    out=w1_buf[:, wsl * 1024:(wsl + 1) * 1024]
                    .rearrange("p (kc m) -> p kc m", kc=8),
                    in_=w1_in[:, ft * 128:(ft + 1) * 128]
                    .rearrange("(kc p) m -> p kc m", p=128)),
                    waits=list(w1_war) + [(f"dma_w1{wsl}", S.cnt[f"dma_w1{wsl}"])],
                    incs=[(f"dma_w1{wsl}", 16)])
                w1b.ready = [wtok]
                if hf == 0 and 2 <= ft < 10:
                    k = ft - 2
                    war = w2b.begin_write() if k == 0 else []
                    ctok = S.add("gp", lambda e, k=k: e.dma_start(
                        out=w2_sb[:, k * 4096:(k + 1) * 4096]
                        .rearrange("p (fc m) -> p fc m", fc=4),
                        in_=w2_in[k * 512:(k + 1) * 512, :]
                        .rearrange("(fc p) m -> p fc m", p=128)),
                        waits=list(war) + [("dma_w", S.cnt["dma_w"])],
                        incs=[("dma_w", 16)])
                    w2_chunks.append(ctok)
                    if k == 7:
                        w2b.ready = w2_chunks
                if hf == 0 and 10 <= ft < 18 and b + 1 < B_LOC:
                    k = ft - 10
                    xbn = B(f"xb{b + 1}")
                    war = xbn.begin_write() if k == 0 else []
                    ctok = S.add("gp", lambda e, k=k, nb=b + 1: e.dma_start(
                        out=xb_scr[nb][k * 256:(k + 1) * 256, :],
                        in_=x_in[nb, k * 256:(k + 1) * 256, :]),
                        waits=list(war) + [("dma_xg", S.cnt["dma_xg"])],
                        incs=[("dma_xg", 16)])
                    xb_chunks.append(ctok)
                    if k == 7:
                        xbn.ready = xb_chunks
                        xt_prefetch.add(b + 1)
                slots2 = [hrot.next(), hrot.next()]
                bufs2 = [B(s[0]) for s in slots2]
                wars2 = [bb.begin_write() for bb in bufs2]
                aps2 = [s[1][:, s[2]:s[2] + s[3]] for s in slots2]
                mtoks2 = [None, None]
                for kc in range(8):
                    for tc2 in range(2):
                        w_ = list(([wtok] + x2t.ready))
                        if kc == 0:
                            w_ += wars2[tc2]
                            if ft == 0 and tc2 == 0:
                                w_ += list(h_war)
                        mtoks2[tc2] = S.add(
                            "te",
                            lambda e, ap=aps2[tc2], l=w1_buf[:, wsl * 1024 + kc * 128: wsl * 1024 + kc * 128 + 128],
                            r=x2t_sb[:, kc * 1024 + tc2 * 512: kc * 1024 + tc2 * 512 + 512],
                            st=(kc == 0), sp=(kc == 7): e.matmul(ap, l, r, start=st, stop=sp),
                            waits=w_, incs=("te",) if kc == 7 else ())
                for tc2 in range(2):
                    bufs2[tc2].ready = [mtoks2[tc2]]
                    w1b.users.append(mtoks2[tc2])
                    x2t.users.append(mtoks2[tc2])
                    gtok = S.add("se", lambda e, ap=aps2[tc2], ft=ft, tc2=tc2: e.activation(
                        h_sb[:, ft * 1024 + tc2 * 512: ft * 1024 + tc2 * 512 + 512], ap,
                        AF.Gelu_apprx_tanh, bias=b1_sb_global[:, ft:ft + 1]),
                        waits=[mtoks2[tc2]] + B(b1_sb_global.name).ready, incs=("se",))
                    bufs2[tc2].users.append(gtok)
                    hb.ready.append(gtok)

            load_set(SET_LNEXP)
            fac = B("fac")
            fac_war = fac.begin_write()
            fac_toks = []
            for tbl in range(8):
                rbslot = ("psC0", psC[0], 0, 512)
                pieces = [(x2t_sb[:, kc * 1024 + tbl * 128: kc * 1024 + tbl * 128 + 128],
                           wr_sb_global[:, kc * 8:(kc + 1) * 8],
                           x2t.ready + B(wr_sb_global.name).ready) for kc in range(8)]
                name, tensor, coloff, _ = rbslot
                pbufR = B(name)
                warR = pbufR.begin_write()
                apR = tensor[:, 0:8]
                mtokR = None
                for j, (lhs, rhs, ws) in enumerate(pieces):
                    mtokR = S.add("te", lambda e, apR=apR, lhs=lhs, rhs=rhs,
                                  st=(j == 0), sp=(j == 7): e.matmul(apR, lhs, rhs, start=st, stop=sp),
                                  waits=(list(ws) + warR) if j == 0 else list(ws),
                                  incs=("te",) if j == 7 else ())
                pbufR.ready = [mtokR]
                x2t.users.append(mtokR)
                B(wr_sb_global.name).users.append(mtokR)
                ro = 32 + tbl * 2
                lgc = 64 + (tbl % 2) * 32
                t0 = S.add("ve", lambda e, lgc=lgc: e.tensor_copy(aux[:, lgc:lgc + 8], psC[0][:, 0:8]),
                           waits=[mtokR] + B(f"rt{tbl % 2}").begin_write(), incs=("ve",))
                pbufR.users.append(t0)
                t1 = S.add("ve", lambda e, lgc=lgc: e.reduce_max(
                    aux[:, lgc + 8:lgc + 9], aux[:, lgc:lgc + 8], axis=mybir.AxisListType.X),
                    waits=[t0], incs=("ve",))
                t2 = S.add("ve", lambda e, lgc=lgc: e.tensor_scalar(
                    aux[:, lgc + 9:lgc + 17], aux[:, lgc:lgc + 8], aux[:, lgc + 8:lgc + 9],
                    -1e30, ALU.is_ge, op1=ALU.mult),
                    waits=[t1], incs=("ve",))
                t3 = S.add("ve", lambda e, lgc=lgc: e.tensor_tensor(
                    aux[:, lgc + 9:lgc + 17], aux[:, lgc:lgc + 8], aux[:, lgc + 9:lgc + 17], ALU.add),
                    waits=[t2], incs=("ve",))
                t4 = S.add("ve", lambda e, lgc=lgc: e.reduce_max(
                    aux[:, lgc + 17:lgc + 18], aux[:, lgc + 9:lgc + 17], axis=mybir.AxisListType.X),
                    waits=[t3], incs=("ve",))
                t5 = S.add("ve", lambda e, lgc=lgc: e.tensor_tensor(
                    aux[:, lgc + 18:lgc + 19], aux[:, lgc + 8:lgc + 9], aux[:, lgc + 17:lgc + 18],
                    ALU.subtract),
                    waits=[t4], incs=("ve",))
                t6 = S.add("se", lambda e, lgc=lgc: e.activation(
                    aux[:, lgc + 19:lgc + 20], aux[:, lgc + 18:lgc + 19], AF.Exp, scale=-1.0),
                    waits=[t5], incs=("se",))
                t7 = S.add("se", lambda e, lgc=lgc: e.activation(
                    aux[:, lgc + 20:lgc + 21], aux[:, lgc + 18:lgc + 19], AF.Exp),
                    waits=[t5], incs=("se",))
                t8 = S.add("ve", lambda e, lgc=lgc: e.tensor_scalar(
                    aux[:, lgc + 21:lgc + 22], aux[:, lgc + 19:lgc + 20], 1.0, None, ALU.add),
                    waits=[t6], incs=("ve",))
                t9 = S.add("ve", lambda e, lgc=lgc: e.reciprocal(
                    aux[:, lgc + 22:lgc + 23], aux[:, lgc + 21:lgc + 22]),
                    waits=[t8], incs=("ve",))
                t10 = S.add("ve", lambda e, lgc=lgc: e.tensor_scalar(
                    aux[:, lgc + 23:lgc + 24], aux[:, lgc + 20:lgc + 21], 1.0, None, ALU.add),
                    waits=[t7], incs=("ve",))
                t11 = S.add("ve", lambda e, lgc=lgc: e.reciprocal(
                    aux[:, lgc + 24:lgc + 25], aux[:, lgc + 23:lgc + 24]),
                    waits=[t10], incs=("ve",))
                t12 = S.add("ve", lambda e, lgc=lgc, tbl=tbl: e.tensor_tensor(
                    aux[:, 240 + tbl:241 + tbl], aux[:, lgc + 22:lgc + 23],
                    aux[:, lgc + 24:lgc + 25], ALU.add),
                    waits=[t9, t11] + (fac_war if tbl == 0 else []), incs=("ve",))
                B(f"rt{tbl % 2}").ready = [t12]
                fac_toks.append(t12)
            fac.ready = fac_toks
            if hf == 0:
                emit_x2t(b, 1, x2t_sb)
            elif b + 1 in xt_prefetch:
                xtb = B("xt")
                xt_war = xtb.begin_write()
                toks = []
                for et in range(8):
                    tok = S.add("sp", lambda e, nb=b + 1, et=et: e.dma_start(
                        out=xt_buf_g[:, et * 512: et * 512 + 512],
                        in_=xb_scr[nb][0:512, et * 128:(et + 1) * 128],
                        transpose=True),
                        waits=(xt_war if et == 0 else []) + B(f"xb{b + 1}").ready,
                        incs=[("dma_xt", 16)])
                    toks.append(tok)
                xtb.ready = toks

            for g in range(2):
                yslots = [[("psS0", psS, 0, 512), ("psS1", psS, 512, 512)],
                          [("psS2", psS, 1024, 512), ("psS3", psS, 1536, 512)],
                          [("psC0", psC[0], 0, 512), ("psC1", psC[1], 0, 512)],
                          [("psC2", psC[2], 0, 512), ("psC3", psC[3], 0, 512)]]
                ywar = []
                yaps = []
                for j in range(4):
                    aps = []
                    for ec in range(2):
                        nm, tsr, co, wd_ = yslots[j][ec]
                        bb = B(nm)
                        ywar.append((j, ec, bb, bb.begin_write()))
                        aps.append(tsr[:, co:co + wd_])
                    yaps.append(aps)
                ytoks = [[None, None] for _ in range(4)]
                nfc = FF // 128
                for fc in range(nfc):
                    for j in range(4):
                        tbl = g * 4 + j
                        for ec in range(2):
                            war = []
                            if fc == 0:
                                for (jj, eecc, bb, w) in ywar:
                                    if jj == j and eecc == ec:
                                        war = w
                            tok = S.add("te", lambda e, j=j, ec=ec, fc=fc, tbl=tbl:
                                        e.matmul(
                                            yaps_g[j][ec],
                                            h_sb[:, fc * 1024 + tbl * 128: fc * 1024 + tbl * 128 + 128],
                                            w2_sb[:, fc * E + ec * 512: fc * E + ec * 512 + 512],
                                            start=(fc == 0), stop=(fc == nfc - 1)),
                                        waits=(hb.ready + w2b.ready + war) if fc == 0 else [],
                                        incs=("te",) if fc == nfc - 1 else ())
                            if fc == nfc - 1:
                                ytoks[j][ec] = tok
                                hb.users.append(tok)
                                w2b.users.append(tok)
                xtoks = {}
                for j in (0, 1):
                    sl = j % 2
                    trow = hf * 1024 + (g * 4 + j) * 128
                    xfr = B(f"x2fr{sl}")
                    xfr_war = xfr.begin_write()
                    xtoks[j] = dma("sp", x2fr_buf[:, sl * E:(sl + 1) * E],
                                   x2f_scr[b][trow:trow + 128, :],
                                   xfr_war + B(f"x2fs{b}").ready, f"dma_fr{sl}", xfr)
                for j in range(4):
                    tbl = g * 4 + j
                    trow = hf * 1024 + tbl * 128
                    sl = j % 2
                    if j == 2:
                        for jj in (2, 3):
                            sll = jj % 2
                            trw = hf * 1024 + (g * 4 + jj) * 128
                            xfr2 = B(f"x2fr{sll}")
                            xfr2_war = xfr2.begin_write()
                            xtoks[jj] = dma("sp", x2fr_buf[:, sll * E:(sll + 1) * E],
                                            x2f_scr[b][trw:trw + 128, :],
                                            xfr2_war + B(f"x2fs{b}").ready,
                                            f"dma_fr{sll}", xfr2)
                    xfr = B(f"x2fr{sl}")
                    xtok = xtoks[j]
                    ty = B(f"t1y{sl}")
                    ty_war = ty.begin_write()
                    ty.ready = []
                    for ec in range(2):
                        ttok = S.add("ve", lambda e, j=j, ec=ec, sl=sl: e.tensor_tensor(
                            t1y_buf[:, sl * E + ec * 512: sl * E + ec * 512 + 512],
                            yaps_g[j][ec], b2b_sb[:, ec * 512:(ec + 1) * 512], ALU.add),
                            waits=[ytoks[j][ec]] + b2b.ready + (ty_war if ec == 0 else []),
                            incs=("ve",))
                        b2b.users.append(ttok)
                        for (jj, eecc, bb, w) in ywar:
                            if jj == j and eecc == ec:
                                bb.users.append(ttok)
                        ty.ready.append(ttok)
                    t2y = B(f"t2y{sl}")
                    t2y_war = t2y.begin_write()
                    ftok = S.add("ve", lambda e, sl=sl, tbl=tbl: e.tensor_scalar(
                        t2y_buf[:, sl * E:(sl + 1) * E], t1y_buf[:, sl * E:(sl + 1) * E],
                        aux[:, 240 + tbl:241 + tbl], None, ALU.mult),
                        waits=ty.ready + fac.ready + t2y_war, incs=("ve",))
                    ty.users.append(ftok)
                    fac.users.append(ftok)
                    t2y.ready = [ftok]
                    rs2 = B(f"rs2{sl}")
                    rs2_war = rs2.begin_write()
                    rtoks = []
                    for ec in range(2):
                        rtok = S.add("ve", lambda e, sl=sl, ec=ec: e.tensor_tensor(
                            t1y_buf[:, sl * E + ec * 512: sl * E + ec * 512 + 512],
                            t2y_buf[:, sl * E + ec * 512: sl * E + ec * 512 + 512],
                            x2fr_buf[:, sl * E + ec * 512: sl * E + ec * 512 + 512], ALU.add),
                            waits=[ftok, xtok] + (rs2_war + ty.begin_write() if ec == 0 else []),
                            incs=("ve",))
                        t2y.users.append(rtok)
                        xfr.users.append(rtok)
                        rtoks.append(rtok)
                    ty.ready = rtoks
                    rstd_ap, nmr_ap, st_toks = ln_stats(rtoks, t1y_buf[:, sl * E:(sl + 1) * E], sl, aux)
                    ob = B(f"outb{sl}")
                    ob_war = ob.begin_write()
                    otok = S.add("ve", lambda e, sl=sl, r=rstd_ap, nm=nmr_ap: e.tensor_scalar(
                        outb_buf[:, sl * E:(sl + 1) * E], t1y_buf[:, sl * E:(sl + 1) * E],
                        r, nm, ALU.mult, op1=ALU.add),
                        waits=st_toks + rtoks + ob_war, incs=("ve",))
                    ty.users.append(otok)
                    ob.ready = [otok]
                    stok = S.add("sp", lambda e, b=b, trow=trow, sl=sl: e.dma_start(
                        out=out_d[b, trow:trow + 128, :], in_=outb_buf[:, sl * E:(sl + 1) * E]),
                        waits=[otok], incs=[(f"dma_o{sl}", 16)])
                    ob.users.append(stok)
        drain_dmas()

    global b1_sb_global, wr_sb_global, yaps_g
    with nc.sbuf_tensor("b1_sb", [128, FF // 128], F32) as b1_sb, \
         nc.sbuf_tensor("wr_sb", [128, 64], BF16) as wr_sb, \
         nc.sbuf_tensor("aux", [128, 512], F32) as aux, \
         nc.sbuf_tensor("pswap_sb", [128, 128], BF16) as pswap_sb_g, \
         nc.sbuf_tensor("wd_sb", [128, 8 * LAT], BF16) as wd_sb_g, \
         nc.sbuf_tensor("xt_buf", [128, 8 * 512], BF16, is_transpose_dest=True) as xt_buf_g:
        b1_sb_global = b1_sb
        wr_sb_global = wr_sb
        S.new_phase()
        bb = B(b1_sb.name)
        dma("sp", b1_sb[:], b1_in[:], bb.begin_write(), "dma_b1", bb)
        bb = B(wr_sb.name)
        dma("gp", wr_sb[:].rearrange("p (kc m) -> p kc m", kc=8),
            wr_in.rearrange("(kc p) m -> p kc m", p=128),
            bb.begin_write(), "dma_w", bb)
        epsb = B("epsap")
        etok = S.add("ve", lambda e: e.memset(aux[:, 255:256], EPS), incs=("ve",))
        epsb.ready = [etok]
        bb = B(wd_sb_g.name)
        dma("gp", wd_sb_g[:].rearrange("p (kc m) -> p kc m", kc=8),
            wd_in.rearrange("(kc p) m -> p kc m", p=128), bb.begin_write(), "dma_w", bb)
        bb = B(pswap_sb_g.name)
        dma("gp", pswap_sb_g[:], psw_in[:], bb.begin_write(), "dma_w", bb)
        xbb = B("xb0")
        dma("gp", xb_scr[0][:], x_in[0], xbb.begin_write(), "dma_xg", xbb)
        S.emit_phase()

        for b in range(B_LOC):
            xstk = ExitStack()
            bstk = ExitStack()
            x2t_sb = xstk.enter_context(
                nc.sbuf_tensor(f"x2t_sb{b}", [128, 8 * 1024], BF16, is_transpose_dest=True))
            if True:
                ben = bstk.enter_context
                kr_sb = ben(nc.sbuf_tensor(f"kr_sb{b}", [128, 8 * T], BF16))
                vaug_sb = ben(nc.sbuf_tensor(f"vaug_sb{b}", [128, 16 * H * 128], BF16))
                with ExitStack() as stk:
                    en = stk.enter_context
                    wq_sb = en(nc.sbuf_tensor(f"wq_sb{b}", [128, 8 * E], BF16))
                    wd_sb = wd_sb_g
                    wk_sb = en(nc.sbuf_tensor(f"wk_sb{b}", [128, 4 * E], BF16))
                    wv_sb = en(nc.sbuf_tensor(f"wv_sb{b}", [128, 4 * E], BF16))
                    ct_sb = en(nc.sbuf_tensor(f"ct_sb{b}", [128, 2 * 512], F32))
                    st_sb = en(nc.sbuf_tensor(f"st_sb{b}", [128, 2 * 512], F32))
                    xt_buf = xt_buf_g
                    lat_buf = en(nc.sbuf_tensor(f"lat_buf{b}", [128, 2 * 4 * 512], BF16))
                    qsp_buf = en(nc.sbuf_tensor(f"qsp_buf{b}", [128, 2 * 512], BF16))
                    qbf_tmp = en(nc.sbuf_tensor(f"qbf_tmp{b}", [128, 2 * 512], BF16))
                    t1_tmp = en(nc.sbuf_tensor(f"t1_tmp{b}", [128, 2 * 512], F32))
                    t2_tmp = en(nc.sbuf_tensor(f"t2_tmp{b}", [128, 2 * 512], F32))
                    S.new_phase()
                    proj_phase(b, (wq_sb, wd_sb, wk_sb, wv_sb, ct_sb, st_sb,
                                   kr_sb, vaug_sb, xt_buf, lat_buf, qsp_buf,
                                   pswap_sb_g, aux, qbf_tmp, t1_tmp, t2_tmp))
                    S.emit_phase()
                with ExitStack() as stk:
                    en = stk.enter_context
                    wo_sb = en(nc.sbuf_tensor(f"wo_sb{b}", [128, 8 * E], BF16))
                    qch_buf = en(nc.sbuf_tensor(f"qch_buf{b}", [128, 2 * 8 * QC], BF16))
                    pt_buf = en(nc.sbuf_tensor(f"pt_buf{b}", [128, 2 * 2048], BF16))
                    rb_sb = en(nc.sbuf_tensor(f"rb_sb{b}", [64, 4 * 512], F32))
                    rb_scr = en(nc.sbuf_tensor(f"rb_scr{b}", [64, 2 * 512], F32))
                    ao_sb = en(nc.sbuf_tensor(f"ao_sb{b}", [128, 8 * 512], BF16))
                    xres_buf = en(nc.sbuf_tensor(f"xres_buf{b}", [128, 2 * E], F32))
                    rs1_buf = en(nc.sbuf_tensor(f"rs1_buf{b}", [128, 2 * E], F32))
                    S.new_phase()
                    attn_phase(b, (wo_sb, kr_sb, vaug_sb, qch_buf, pt_buf,
                                   rb_sb, rb_scr, ao_sb, xres_buf, rs1_buf,
                                   x2t_sb, aux))
                    S.emit_phase()
            bstk.close()
            with ExitStack() as stk:
                en = stk.enter_context
                w2_sb = en(nc.sbuf_tensor(f"w2_sb{b}", [128, 32 * E], BF16))
                h_sb = en(nc.sbuf_tensor(f"h_sb{b}", [128, 32 * 1024], BF16))
                w1_buf = en(nc.sbuf_tensor(f"w1_buf{b}", [128, 2 * 1024], BF16))
                b2b_sb = en(nc.sbuf_tensor(f"b2b_sb{b}", [128, E], F32))
                x2fr_buf = en(nc.sbuf_tensor(f"x2fr_buf{b}", [128, 2 * E], F32))
                t1y_buf = en(nc.sbuf_tensor(f"t1y_buf{b}", [128, 2 * E], F32))
                t2y_buf = en(nc.sbuf_tensor(f"t2y_buf{b}", [128, 2 * E], F32))
                outb_buf = en(nc.sbuf_tensor(f"outb_buf{b}", [128, 2 * E], F32))
                S.new_phase()
                yaps_g = [[psS[:, 0:512], psS[:, 512:1024]],
                          [psS[:, 1024:1536], psS[:, 1536:2048]],
                          [psC[0][:, :], psC[1][:, :]],
                          [psC[2][:, :], psC[3][:, :]]]
                ffn_phase(b, (w2_sb, x2t_sb, h_sb, w1_buf, b2b_sb, x2fr_buf,
                              t1y_buf, t2y_buf, outb_buf, aux))
                S.emit_phase()
            xstk.close()

    nc.finalize()
    return nc


def host_consts():
    inv = 1.0 / (10000.0 ** (np.arange(0, D, 2, dtype=np.float32) / D))
    ang = np.outer(np.arange(T, dtype=np.float32), inv)
    cos = np.cos(ang)
    sin = np.sin(ang)
    r = np.arange(128)
    i = (r % 64) // 2
    ctab = cos[:, i].T.astype(np.float32).copy()
    sgn = np.where(r % 2 == 0, -1.0, 1.0).astype(np.float32)
    stab = (sin[:, i].T * sgn[:, None]).astype(np.float32).copy()
    perm = np.arange(128) ^ 1
    pswap = np.zeros((128, 128), dtype=np.float32)
    pswap[perm, np.arange(128)] = 1.0
    return ctab, stab, pswap


_prog_cache = {}


def kernel(L0_attn_x, L0_attn_wq, L0_attn_w_down, L0_attn_wk_up, L0_attn_wv_up,
           L0_attn_wo, L0_ffn_router_w, L0_ffn_w1, L0_ffn_b1, L0_ffn_w2, L0_ffn_b2):
    x = np.ascontiguousarray(np.asarray(L0_attn_x, dtype=np.float32))
    ctab, stab, pswap = host_consts()
    common = {
        "wq": np.ascontiguousarray(np.asarray(L0_attn_wq, dtype=np.float32)),
        "wd": np.ascontiguousarray(np.asarray(L0_attn_w_down, dtype=np.float32)),
        "wk": np.ascontiguousarray(np.asarray(L0_attn_wk_up, dtype=np.float32)),
        "wv": np.ascontiguousarray(np.asarray(L0_attn_wv_up, dtype=np.float32)),
        "wo": np.ascontiguousarray(np.asarray(L0_attn_wo, dtype=np.float32)),
        "wr": np.ascontiguousarray(np.asarray(L0_ffn_router_w, dtype=np.float32)),
        "w1": np.ascontiguousarray(np.asarray(L0_ffn_w1, dtype=np.float32)),
        "w2": np.ascontiguousarray(np.asarray(L0_ffn_w2, dtype=np.float32)),
        "b1l": np.ascontiguousarray(np.asarray(L0_ffn_b1, dtype=np.float32).reshape(FF // 128, 128).T),
        "b2b": np.ascontiguousarray(np.broadcast_to(np.asarray(L0_ffn_b2, dtype=np.float32), (128, E))),
        "ctab": ctab, "stab": stab, "pswap": pswap,
    }
    if "nc" not in _prog_cache:
        _prog_cache["nc"] = build_program()
    nc = _prog_cache["nc"]
    in_maps = []
    for c in range(N_CORES):
        m = dict(common)
        m["x"] = np.ascontiguousarray(x[c * B_LOC:(c + 1) * B_LOC])
        in_maps.append(m)
    res = run_bass_kernel_spmd(nc, in_maps, core_ids=list(range(N_CORES)))
    return np.concatenate([r["out"] for r in res.results], axis=0)


# version 20
# speedup vs baseline: 1.2201x; 1.0326x over previous
import os
import numpy as np
import ml_dtypes
from contextlib import ExitStack
import concourse.bass as bass
import concourse.mybir as mybir
from concourse import dve_ops
from concourse.bass_utils import run_bass_kernel_spmd

F32 = mybir.dt.float32
BF16 = mybir.dt.bfloat16
AF = mybir.ActivationFunctionType
ALU = mybir.AluOpType

B_LOC = 2
T = 2048
E = 1024
H = 16
D = 64
LAT = 512
FF = 4096
QC = 512
NQC = T // QC
EPS = 1e-5
SET_LNEXP = 6
SET_GELU = 11
N_CORES = 8


class Buf:
    def __init__(self):
        self.ready = []
        self.users = []

    def begin_write(self):
        w = self.users + self.ready
        self.ready = []
        self.users = []
        return w


class Sched:
    def __init__(self, nc, semnames):
        self.nc = nc
        self.sems = {n: nc.alloc_semaphore(n) for n in semnames}
        self.cnt = {n: 0 for n in semnames}
        self.ops = None

    def new_phase(self):
        self.ops = []

    def add(self, engine, fn, waits=(), incs=()):
        norm = []
        for inc in incs:
            sem, n = inc if isinstance(inc, tuple) else (inc, 1)
            self.cnt[sem] += n
            norm.append((sem, n))
        wm = {}
        for w in waits:
            if w is None:
                continue
            sem, val = w
            if val > 0:
                wm[sem] = max(wm.get(sem, 0), val)
        self.ops.append((engine, tuple(sorted(wm.items())), tuple(norm), fn))
        return (norm[0][0], self.cnt[norm[0][0]]) if norm else None

    def emit_phase(self):
        ops = self.ops
        self.ops = None
        with self.nc.Block() as blk:
            decos = {"te": blk.tensor, "se": blk.scalar, "ve": blk.vector,
                     "gp": blk.gpsimd, "sp": blk.sync}
            for name in ("te", "se", "ve", "gp", "sp"):
                myops = [(w, i, f) for (e, w, i, f) in ops if e == name]
                if not myops:
                    continue

                def make_body(myops=myops):
                    def body(eng):
                        seen = {}
                        for waits, incs, fn in myops:
                            for sem, val in waits:
                                if seen.get(sem, 0) < val:
                                    eng.wait_ge(self.sems[sem], val)
                                    seen[sem] = val
                            if fn is None:
                                continue
                            inst = fn(eng)
                            for sem, n in incs:
                                inst.then_inc(self.sems[sem], n)
                    return body

                decos[name](make_body())


def build_program():
    nc = bass.Bass("TRN2", target_bir_lowering=False)

    x_in = nc.dram_tensor("x", [B_LOC, T, E], F32, kind="ExternalInput")
    wq_in = nc.dram_tensor("wq", [E, E], F32, kind="ExternalInput")
    wd_in = nc.dram_tensor("wd", [E, LAT], F32, kind="ExternalInput")
    wk_in = nc.dram_tensor("wk", [LAT, E], F32, kind="ExternalInput")
    wv_in = nc.dram_tensor("wv", [LAT, E], F32, kind="ExternalInput")
    wo_in = nc.dram_tensor("wo", [E, E], F32, kind="ExternalInput")
    wr_in = nc.dram_tensor("wr", [E, 8], F32, kind="ExternalInput")
    w1_in = nc.dram_tensor("w1", [E, FF], F32, kind="ExternalInput")
    w2_in = nc.dram_tensor("w2", [FF, E], F32, kind="ExternalInput")
    b1_in = nc.dram_tensor("b1l", [128, FF // 128], F32, kind="ExternalInput")
    b2_in = nc.dram_tensor("b2b", [128, E], F32, kind="ExternalInput")
    ct_in = nc.dram_tensor("ctab", [128, T], F32, kind="ExternalInput")
    st_in = nc.dram_tensor("stab", [128, T], F32, kind="ExternalInput")
    psw_in = nc.dram_tensor("pswap", [128, 128], F32, kind="ExternalInput")
    out_d = nc.dram_tensor("out", [B_LOC, T, E], F32, kind="ExternalOutput")

    xb_scr = [nc.dram_tensor(f"xb_scr{b}", [T, E], BF16, kind="Internal") for b in range(B_LOC)]
    q_scr = [nc.dram_tensor(f"q_scr{b}", [NQC, 8, 128, QC], BF16, kind="Internal") for b in range(B_LOC)]
    x2f_scr = [nc.dram_tensor(f"x2f_scr{b}", [T, E], F32, kind="Internal") for b in range(B_LOC)]
    x2b_scr = [nc.dram_tensor(f"x2b_scr{b}", [T, E], BF16, kind="Internal") for b in range(B_LOC)]

    psS = nc.alloc_psum_tensor("psS", [128, 2048], F32)
    psC = [nc.alloc_psum_tensor(f"psC{i}", [128, 512], F32) for i in range(4)]

    S = Sched(nc, ["dma_w", "dma_t", "dma_xg", "dma_xt", "dma_ct0", "dma_ct1",
                   "dma_st0", "dma_st1", "dma_qs0", "dma_qs1", "dma_qch0",
                   "dma_xr0", "dma_xr1", "dma_xf0", "dma_xf1", "dma_xb0",
                   "dma_xb1", "dma_w10", "dma_w11", "dma_x2t", "dma_fr0",
                   "dma_fr1", "dma_b1", "dma_sel", "dma_qch1", "dma_o0", "dma_o1", "te", "se", "ve", "gp"])
    bufs = {}

    def B(name):
        if name not in bufs:
            bufs[name] = Buf()
        return bufs[name]

    slots8 = [("psS0", psS, 0, 512), ("psS1", psS, 512, 512),
              ("psS2", psS, 1024, 512), ("psS3", psS, 1536, 512),
              ("psC0", psC[0], 0, 512), ("psC1", psC[1], 0, 512),
              ("psC2", psC[2], 0, 512), ("psC3", psC[3], 0, 512)]

    class Rot:
        def __init__(self, slots):
            self.slots = slots
            self.i = 0

        def next(self):
            s = self.slots[self.i % len(self.slots)]
            self.i += 1
            return s

    def mm_group(slot, pieces, extra_waits=(), reads=()):
        name, tensor, coloff, width = slot
        buf = B(name)
        war = buf.begin_write()
        ap = tensor[:, coloff:coloff + width]
        n = len(pieces)
        tok = None
        for j, (lhs, rhs, ws) in enumerate(pieces):
            waits = list(ws)
            if j == 0:
                waits += war + list(extra_waits)
            tok = S.add(
                "te",
                lambda e, ap=ap, lhs=lhs, rhs=rhs, st=(j == 0), sp=(j == n - 1):
                    e.matmul(ap, lhs, rhs, start=st, stop=sp),
                waits=waits,
                incs=("te",) if j == n - 1 else ())
        buf.ready = [tok]
        for rb in reads:
            rb.users.append(tok)
        return ap, tok, buf

    def dma(queue, out_ap, in_ap, waits, sem, buf=None):
        waits = list(waits)
        if queue == "gp":
            waits.append((sem, S.cnt[sem]))
        tok = S.add(queue, lambda e, o=out_ap, i=in_ap: e.dma_start(out=o, in_=i),
                    waits=waits, incs=[(sem, 16)])
        if buf is not None:
            buf.ready = [tok]
        return tok

    def load_set(set_id):
        if os.environ.get("KDBG_NOACT"):
            return
        S.add("se", lambda e, s=set_id: e.add_instruction(
            mybir.InstLoadActFuncSet(
                name=e.bass.get_next_instruction_name(),
                act_func_set_id=s, ins=[], outs=[])))

    def drain_dmas():
        S.add("sp", None, waits=[(n, S.cnt[n]) for n in S.sems if n.startswith("dma_")])

    xt_prefetch = set()

    def proj_phase(b, tens):
        (wq_sb, wd_sb, wk_sb, wv_sb, ct_sb, st_sb, kr_sb, vaug_sb,
         xt_buf, lat_buf, qsp_buf, pswap_sb, aux, qbf_tmp, t1_tmp,
         t2_tmp) = tens
        rot = Rot(slots8)

        for sbuf, dram, nkc in ((wq_sb, wq_in, 8),
                                (wk_sb, wk_in, 4), (wv_sb, wv_in, 4)):
            bb = B(sbuf.name)
            dma("gp", sbuf[:].rearrange("p (kc m) -> p kc m", kc=nkc),
                dram.rearrange("(kc p) m -> p kc m", p=128),
                bb.begin_write(), "dma_w", bb)
        load_set(SET_LNEXP)

        xbb = B(f"xb{b}")
        if not xbb.ready:
            dma("gp", xb_scr[b][:], x_in[b], xbb.begin_write(), "dma_xg", xbb)

        kr = B(kr_sb.name)
        kr_war = kr.begin_write()
        va = B(vaug_sb.name)
        va_war = va.begin_write()
        ones_tok = S.add("gp", lambda e: e.memset(
            vaug_sb[:].rearrange("p (tt h d) -> p tt h d", tt=16, h=H)[:, :, :, 64:128], 1.0),
            waits=va_war, incs=("gp",))
        va.ready.append(ones_tok)
        qsc = B(f"qsc{b}")
        qsc_war = qsc.begin_write()
        if os.environ.get("KDBG_STAGE") == "setup":
            drain_dmas()
            return

        for tc in range(NQC):
            skip_xt = (tc == 0 and b in xt_prefetch)
            ctb = B(f"ct{tc % 2}")
            toff = (tc % 2) * 512
            cttok = dma("sp", ct_sb[:, toff:toff + 512], ct_in[:, tc * 512:(tc + 1) * 512],
                        ctb.begin_write(), f"dma_ct{tc % 2}", ctb)
            stb = B(f"st{tc % 2}")
            sttok = dma("sp", st_sb[:, toff:toff + 512], st_in[:, tc * 512:(tc + 1) * 512],
                        stb.begin_write(), f"dma_st{tc % 2}", stb)
            xtb = B("xt")
            xoff = 0
            if not skip_xt:
                xt_war = xtb.begin_write()
                xt_toks = []
                for et in range(8):
                    tok = S.add("sp", lambda e, b=b, tc=tc, et=et, xoff=xoff: e.dma_start(
                        out=xt_buf[:, xoff + et * 512: xoff + et * 512 + 512],
                        in_=xb_scr[b][tc * 512:(tc + 1) * 512, et * 128:(et + 1) * 128],
                        transpose=True),
                        waits=(xt_war if et == 0 else []) + xbb.ready,
                        incs=[("dma_xt", 16)])
                    xt_toks.append(tok)
                xtb.ready = xt_toks

            def xt_rhs(kc, xoff=xoff):
                return xt_buf[:, xoff + kc * 512: xoff + kc * 512 + 512]

            latb = B(f"lat{tc % 2}")
            lat_war = latb.begin_write()
            lat_toks = []
            loff = (tc % 2) * 2048
            for lt in range(4):
                pieces = [(wd_sb[:, kc * LAT + lt * 128: kc * LAT + lt * 128 + 128],
                           xt_rhs(kc),
                           B(wd_sb.name).ready + xtb.ready) for kc in range(8)]
                ap, mtok, pbuf = mm_group(rot.next(), pieces, extra_waits=lat_war if lt == 0 else (),
                                          reads=[B(wd_sb.name), xtb])
                etok = S.add("se", lambda e, ap=ap, lt=lt, loff=loff: e.activation(
                    lat_buf[:, loff + lt * 512: loff + lt * 512 + 512], ap, AF.Copy),
                    waits=[mtok], incs=("se",))
                pbuf.users.append(etok)
                lat_toks.append(etok)
            latb.ready = lat_toks

            if os.environ.get("KDBG_STAGE") == "proj1":
                continue
            for which in ("q", "k"):
                for mc in range(8):
                    if which == "q":
                        pieces = [(wq_sb[:, kc * E + mc * 128: kc * E + mc * 128 + 128],
                                   xt_rhs(kc),
                                   B(wq_sb.name).ready + xtb.ready) for kc in range(8)]
                    else:
                        pieces = [(wk_sb[:, kc * E + mc * 128: kc * E + mc * 128 + 128],
                                   lat_buf[:, loff + kc * 512: loff + kc * 512 + 512],
                                   B(wk_sb.name).ready + latb.ready) for kc in range(4)]
                    ap, mtok, pbuf = mm_group(rot.next(), pieces,
                                              reads=[B(wq_sb.name), xtb] if which == "q" else [B(wk_sb.name), latb])
                    sl = mc % 2
                    qb = B(f"qbf{sl}")
                    qb_war = qb.begin_write()
                    qtok = S.add("se", lambda e, ap=ap, sl=sl: e.activation(
                        qbf_tmp[:, sl * 512:sl * 512 + 512], ap, AF.Copy),
                        waits=[mtok] + qb_war, incs=("se",))
                    qb.ready = [qtok]
                    pbuf.users.append(qtok)
                    if os.environ.get("KDBG_STAGE") == "proj2a":
                        continue

                    swslot = rot.next()
                    swname, swt, swoff, _ = swslot
                    swbuf = B(swname)
                    sw_war = swbuf.begin_write()
                    swap = swt[:, swoff:swoff + 512]
                    swtok = S.add("te", lambda e, swap=swap, sl=sl: e.matmul(
                        swap, pswap_sb[:], qbf_tmp[:, sl * 512:sl * 512 + 512],
                        start=True, stop=True),
                        waits=[qtok] + sw_war + B(pswap_sb.name).ready, incs=("te",))
                    swbuf.ready = [swtok]
                    qb.users.append(swtok)

                    if os.environ.get("KDBG_STAGE") == "proj2b":
                        continue
                    t1b = B(f"t1{sl}")
                    t1_war = t1b.begin_write()
                    t1tok = S.add("ve", lambda e, ap=ap, sl=sl, toff=toff: e.tensor_tensor(
                        t1_tmp[:, sl * 512:sl * 512 + 512], ap,
                        ct_sb[:, toff:toff + 512], ALU.mult),
                        waits=[mtok, qtok, cttok] + t1_war, incs=("ve",))
                    ctb.users.append(t1tok)
                    t1b.ready = [t1tok]
                    pbuf.users.append(t1tok)
                    if os.environ.get("KDBG_STAGE") == "proj2d":
                        t1b.users.append(t1tok)
                        continue

                    t2b = B(f"t2{sl}")
                    t2_war = t2b.begin_write()
                    t2tok = S.add("ve", lambda e, swap=swap, sl=sl, toff=toff: e.tensor_tensor(
                        t2_tmp[:, sl * 512:sl * 512 + 512], swap,
                        st_sb[:, toff:toff + 512], ALU.mult),
                        waits=[swtok, sttok] + t2_war, incs=("ve",))
                    stb.users.append(t2tok)
                    t2b.ready = [t2tok]
                    swbuf.users.append(t2tok)

                    if which == "k":
                        ktok = S.add("gp", lambda e, sl=sl, mc=mc, tc=tc: e.tensor_tensor(
                            kr_sb[:, mc * T + tc * 512: mc * T + tc * 512 + 512],
                            t1_tmp[:, sl * 512:sl * 512 + 512],
                            t2_tmp[:, sl * 512:sl * 512 + 512], ALU.add),
                            waits=[t1tok, t2tok] + (kr_war if (mc == 0 and tc == 0) else []),
                            incs=("gp",))
                        kr.ready.append(ktok)
                        t1b.users.append(ktok)
                        t2b.users.append(ktok)
                    else:
                        qsb = B(f"qsp{sl}")
                        qs_war = qsb.begin_write()
                        qrtok = S.add("gp", lambda e, sl=sl: e.tensor_tensor(
                            qsp_buf[:, sl * 512:sl * 512 + 512],
                            t1_tmp[:, sl * 512:sl * 512 + 512],
                            t2_tmp[:, sl * 512:sl * 512 + 512], ALU.add),
                            waits=[t1tok, t2tok] + qs_war, incs=("gp",))
                        t1b.users.append(qrtok)
                        t2b.users.append(qrtok)
                        qsb.ready = [qrtok]
                        if os.environ.get("KDBG_STAGE") == "proj2c":
                            qsb.users.append(qrtok)
                            continue
                        sptok = S.add("sp", lambda e, b=b, tc=tc, mc=mc, sl=sl: e.dma_start(
                            out=q_scr[b][tc, mc], in_=qsp_buf[:, sl * 512:sl * 512 + 512]),
                            waits=[qrtok] + (qsc_war if (mc == 0 and tc == 0) else []),
                            incs=[(f"dma_qs{sl}", 16)])
                        qsb.users.append(sptok)
                        qsc.ready.append(sptok)

            if os.environ.get("KDBG_STAGE") == "proj2":
                continue
            for tt4 in range(4):
                tt = tc * 4 + tt4
                for ec in range(2):
                    pieces = [(lat_buf[:, loff + lc * 512 + tt4 * 128: loff + lc * 512 + tt4 * 128 + 128],
                               wv_sb[:, lc * E + ec * 512: lc * E + ec * 512 + 512],
                               latb.ready + B(wv_sb.name).ready) for lc in range(4)]
                    ap, mtok, pbuf = mm_group(rot.next(), pieces, reads=[latb, B(wv_sb.name)])
                    vtok = S.add("ve", lambda e, ap=ap, tt=tt, ec=ec: e.tensor_copy(
                        vaug_sb[:].rearrange("p (tt h d) -> p tt h d", tt=16, h=H)
                        [:, tt, 8 * ec:8 * ec + 8, 0:64],
                        ap.rearrange("p (h d) -> p h d", h=8)),
                        waits=[mtok], incs=("ve",))
                    pbuf.users.append(vtok)
                    va.ready.append(vtok)

        drain_dmas()

    def emit_x2t(b, hf, x2t_sb):
        x2t = B("x2t")
        x2t_war = x2t.begin_write()
        toks = []
        for et in range(8):
            tok = S.add("sp", lambda e, b=b, hf=hf, et=et, x2t_sb=x2t_sb: e.dma_start(
                out=x2t_sb[:, et * 1024:(et + 1) * 1024],
                in_=x2b_scr[b][hf * 1024:(hf + 1) * 1024, et * 128:(et + 1) * 128],
                transpose=True),
                waits=(x2t_war if et == 0 else []) + B(f"x2bs{b}_h{hf}").ready,
                incs=[("dma_x2t", 16)])
            toks.append(tok)
        x2t.ready = toks
        return x2t

    def attn_phase(b, tens):
        (wo_sb, kr_sb, vaug_sb, qch_buf, pt_buf, rb_sb, rb_scr, ao_sb,
         xres_buf, rs1_buf, x2t_sb, aux) = tens
        if os.environ.get("KDBG_STAGE", "").startswith("proj"):
            drain_dmas()
            return
        load_set(SET_LNEXP)
        bb = B(wo_sb.name)
        dma("gp", wo_sb[:].rearrange("p (kc m) -> p kc m", kc=8),
            wo_in.rearrange("(kc p) m -> p kc m", p=128),
            bb.begin_write(), "dma_w", bb)
        kr = B(kr_sb.name)
        va = B(vaug_sb.name)
        qsc = B(f"qsc{b}")
        def load_qch(c):
            qch = B(f"qch{c % 2}")
            q_war = qch.begin_write()
            qoff = (c % 2) * 8 * QC
            dma("sp", qch_buf[:, qoff:qoff + 8 * QC].rearrange("p (kc n) -> p kc n", kc=8),
                q_scr[b][c].rearrange("kc p n -> p kc n"),
                q_war + qsc.ready, f"dma_qch{c % 2}", qch)
            return qch

        load_qch(0)
        for c in range(NQC):
            qch = B(f"qch{c % 2}")
            qoff = (c % 2) * 8 * QC
            if c + 1 < NQC:
                load_qch(c + 1)
            nkc = 4 * (c + 1)
            ao = B(ao_sb.name)
            ao_war = ao.begin_write()
            for hp in range(8):
                par = hp % 2
                avA_ap = psC[0 + 2 * par]
                avB_ap = psC[1 + 2 * par]
                avA = B(f"psC{0 + 2 * par}")
                avB = B(f"psC{1 + 2 * par}")
                avA_war = avA.begin_write()
                avB_war = avB.begin_write()
                av_last = None
                pt_toks = {}

                def emit_S(kc, hp=hp, c=c, qoff=qoff):
                    snames = ("psS0", "psS1") if (kc % 2 == 0) else ("psS2", "psS3")
                    ao_base = (kc % 2) * 1024
                    sb0, sb1 = B(snames[0]), B(snames[1])
                    s_war = sb0.begin_write() + sb1.begin_write()
                    mts = []
                    for half in range(2):
                        base = half * 64
                        colo = ao_base + half * 512
                        tok = S.add("te", lambda e, colo=colo, base=base, hp=hp, kc=kc, qoff=qoff:
                                    e.matmul(
                                        psS[:, colo:colo + 512],
                                        kr_sb[base:base + 64, hp * T + kc * 128: hp * T + kc * 128 + 128],
                                        qch_buf[base:base + 64, qoff + hp * QC: qoff + hp * QC + QC],
                                        start=True, stop=True, tile_position=(base, 0)),
                                    waits=(s_war + kr.ready + qch.ready) if half == 0 else [],
                                    incs=("te",))
                        mts.append(tok)
                    sb0.ready = [mts[0]]
                    sb1.ready = [mts[1]]
                    kr.users.append(mts[1])
                    qch.users.append(mts[1])
                    masked = (kc >= 4 * c + 2)
                    ptb = B(f"pt{kc % 4}")
                    pt_war = ptb.begin_write()
                    po = (kc % 4) * 1024
                    if not masked:
                        etok = S.add("se", lambda e, po=po, ao_base=ao_base: e.activation(
                            pt_buf[:, po:po + 1024], psS[:, ao_base:ao_base + 1024],
                            AF.Exp, scale=0.125),
                            waits=mts + pt_war, incs=("se",))
                        ptoks = [etok]
                    else:
                        ztok = S.add("ve", lambda e, po=po: e.memset(
                            pt_buf[:, po:po + 1024].rearrange("p (h n) -> p h n", h=2)[:, :, 0:256], 0.0),
                            waits=pt_war, incs=("ve",))
                        etok = S.add("se", lambda e, po=po, ao_base=ao_base: e.activation(
                            pt_buf[:, po:po + 1024].rearrange("p (h n) -> p h n", h=2)[:, :, 256:512],
                            psS[:, ao_base:ao_base + 1024].rearrange("p (h n) -> p h n", h=2)[:, :, 256:512],
                            AF.Exp, scale=0.125),
                            waits=mts + [ztok], incs=("se",))
                        ptoks = [etok, ztok]
                    ptb.ready = ptoks
                    sb0.users.append(etok)
                    sb1.users.append(etok)
                    pt_toks[kc] = ptoks

                def emit_AV(kc, hp=hp):
                    nonlocal av_last
                    po = (kc % 4) * 1024
                    ptb = B(f"pt{kc % 4}")
                    tok = None
                    for half in range(2):
                        av_ap = avA_ap if half == 0 else avB_ap
                        tok = S.add("te", lambda e, av_ap=av_ap, hp=hp, half=half, kc=kc, po=po, nkc=nkc:
                                    e.matmul(
                                        av_ap[:, :],
                                        vaug_sb[:, kc * (H * 128) + (2 * hp + half) * 128:
                                                kc * (H * 128) + (2 * hp + half) * 128 + 128],
                                        pt_buf[:, po + half * 512: po + half * 512 + 512],
                                        start=(kc == 0), stop=(kc == nkc - 1)),
                                    waits=pt_toks[kc] + va.ready +
                                          ((avA_war if half == 0 else avB_war) if kc == 0 else []),
                                    incs=("te",) if half == 1 else ())
                    ptb.users.append(tok)
                    va.users.append(tok)
                    if kc == nkc - 1:
                        av_last = tok

                emit_S(0)
                for kc in range(nkc):
                    if kc + 1 < nkc:
                        emit_S(kc + 1)
                    emit_AV(kc)
                avA.ready = [av_last]
                avB.ready = [av_last]

                rbs = B(f"rb{par}")
                rb_war = rbs.begin_write()
                rtoks = []
                for half in range(2):
                    av_ap = avA_ap if half == 0 else avB_ap
                    ro = (2 * par + half) * 512
                    if half == 0:
                        rtok = S.add("ve", lambda e, av_ap=av_ap, ro=ro:
                                     e.reciprocal(rb_sb[0:64, ro:ro + 512], av_ap[64:128, :]),
                                     waits=[av_last] + rb_war, incs=("ve",))
                    else:
                        scr = rb_scr[0:64, par * 512:par * 512 + 512]
                        scb = B(f"rbscr{par}")
                        scr_war = scb.begin_write()
                        ltok = S.add("se", lambda e, av_ap=av_ap, scr=scr:
                                     e.activation(scr, av_ap[64:128, :], AF.Ln),
                                     waits=[av_last] + scr_war, incs=("se",))
                        scb.ready = [ltok]
                        rtok = S.add("se", lambda e, ro=ro, scr=scr:
                                     e.activation(rb_sb[0:64, ro:ro + 512], scr, AF.Exp, scale=-1.0),
                                     waits=[ltok], incs=("se",))
                        scb.users.append(rtok)
                    rtoks.append(rtok)
                rbs.ready = rtoks
                avA.users.append(rtoks[0])
                avB.users.append(rtoks[1])
                for half in range(2):
                    avbuf = avA if half == 0 else avB
                    av_ap = avA_ap if half == 0 else avB_ap
                    atok = S.add("ve", lambda e, hp=hp, half=half, par=par, av_ap=av_ap: e.tensor_tensor(
                        ao_sb[half * 64:half * 64 + 64, hp * 512:hp * 512 + 512],
                        av_ap[0:64, :],
                        rb_sb[0:64, (2 * par + half) * 512:(2 * par + half) * 512 + 512], ALU.mult),
                        waits=[rtoks[half]] + (ao_war if (hp == 0 and half == 0) else []),
                        incs=("ve",))
                    avbuf.users.append(atok)
                    rbs.users.append(atok)
                    ao.ready.append(atok)
            for tb in range(4):
                trow = c * 512 + tb * 128
                sl = tb % 2
                xr = B(f"xres{sl}")
                xr_war = xr.begin_write()
                xtok = dma("sp", xres_buf[:, sl * E:(sl + 1) * E],
                           x_in[b, trow:trow + 128, :], xr_war, f"dma_xr{sl}", xr)
                rs = B(f"rs1{sl}")
                rs_war = rs.begin_write()
                rs.ready = []
                for ec in range(2):
                    slot = ("psC3", psC[3], 0, 512) if ec == 0 else ("psC2", psC[2], 0, 512)
                    pieces = [(ao_sb[:, kc * 512 + tb * 128: kc * 512 + tb * 128 + 128],
                               wo_sb[:, kc * E + ec * 512: kc * E + ec * 512 + 512],
                               ao.ready + B(wo_sb.name).ready) for kc in range(8)]
                    ap, mtok, pbuf = mm_group(slot, pieces, reads=[ao, B(wo_sb.name)])
                    rtok = S.add("ve", lambda e, ap=ap, sl=sl, ec=ec: e.tensor_tensor(
                        rs1_buf[:, sl * E + ec * 512: sl * E + ec * 512 + 512],
                        ap, xres_buf[:, sl * E + ec * 512: sl * E + ec * 512 + 512], ALU.add),
                        waits=[mtok, xtok] + (rs_war if ec == 0 else []), incs=("ve",))
                    pbuf.users.append(rtok)
                    xr.users.append(rtok)
                    rs.ready.append(rtok)
                rstd_ap, nmr_ap, st_toks = ln_stats(rs.ready, rs1_buf[:, sl * E:(sl + 1) * E], sl, aux)
                xr2_war = xr.begin_write()
                a1 = S.add("ve", lambda e, sl=sl, r=rstd_ap, nm=nmr_ap: e.tensor_scalar(
                    xres_buf[:, sl * E:(sl + 1) * E], rs1_buf[:, sl * E:(sl + 1) * E],
                    r, nm, ALU.mult, op1=ALU.add),
                    waits=st_toks + rs.ready + xr2_war, incs=("ve",))
                xr.ready = [a1]
                rs.users.append(a1)
                sp1 = S.add("sp", lambda e, b=b, trow=trow, sl=sl: e.dma_start(
                    out=x2f_scr[b][trow:trow + 128, :], in_=xres_buf[:, sl * E:(sl + 1) * E]),
                    waits=[a1], incs=[(f"dma_xf{sl}", 16)])
                xr.users.append(sp1)
                B(f"x2fs{b}").ready.append(sp1)
                B(f"x2fs{b}_h{trow // 1024}").ready.append(sp1)
            if c in (1, 3):
                hf_ = (c - 1) // 2
                xc = B(f"x2bs{b}_h{hf_}")
                xc_war = xc.begin_write()
                sptoks = B(f"x2fs{b}_h{hf_}").ready
                ctoks = []
                for k in range(8):
                    r0 = hf_ * 1024 + k * 128
                    ctok = S.add("gp", lambda e, b=b, r0=r0: e.dma_start(
                        out=x2b_scr[b][r0:r0 + 128, :],
                        in_=x2f_scr[b][r0:r0 + 128, :]),
                        waits=(list(xc_war) if k == 0 else []) + [sptoks[k]] +
                              [(f"dma_xb{hf_}", S.cnt[f"dma_xb{hf_}"])],
                        incs=[(f"dma_xb{hf_}", 16)])
                    ctoks.append(ctok)
                xc.ready = ctoks
            if c == 1:
                emit_x2t(b, 0, x2t_sb)
        drain_dmas()

    def ln_stats(src_toks, src_ap, sl, aux):
        so = sl * 24
        st = B(f"lnst{sl}")
        st_war = st.begin_write()
        t_b0 = S.add("ve", lambda e, src=src_ap, so=so: e.bn_stats(
            aux[:, so:so + 6], src[:, 0:512]),
            waits=src_toks + st_war, incs=("ve",))
        t_b1 = S.add("ve", lambda e, src=src_ap, so=so: e.bn_stats(
            aux[:, so + 6:so + 12], src[:, 512:1024]),
            waits=src_toks, incs=("ve",))
        t_ag = S.add("ve", lambda e, so=so: e.bn_aggr(aux[:, so + 12:so + 14], aux[:, so:so + 12]),
                     waits=[t_b0, t_b1], incs=("ve",))
        t_ln = S.add("se", lambda e, so=so: e.activation(
            aux[:, so + 14:so + 15], aux[:, so + 13:so + 14], AF.Ln, bias=aux[:, 255:256]),
            waits=[t_ag] + B("epsap").ready, incs=("se",))
        t_rs = S.add("se", lambda e, so=so: e.activation(
            aux[:, so + 15:so + 16], aux[:, so + 14:so + 15], AF.Exp, scale=-0.5),
            waits=[t_ln], incs=("se",))
        t_mr = S.add("ve", lambda e, so=so: e.tensor_tensor(
            aux[:, so + 16:so + 17], aux[:, so + 12:so + 13], aux[:, so + 15:so + 16], ALU.mult),
            waits=[t_ag, t_rs], incs=("ve",))
        t_nm = S.add("ve", lambda e, so=so: e.tensor_scalar(
            aux[:, so + 17:so + 18], aux[:, so + 16:so + 17], -1.0, None, ALU.mult),
            waits=[t_mr], incs=("ve",))
        st.ready = [t_nm, t_rs]
        return aux[:, so + 15:so + 16], aux[:, so + 17:so + 18], [t_nm, t_rs]

    def ffn_phase(b, tens):
        if os.environ.get("KDBG_STAGE") in ("proj", "attn"):
            drain_dmas()
            return
        (w2_sb, x2t_sb, h_sb, w1_buf, b2b_sb, x2fr_buf, t1y_buf,
         outb_buf, aux) = tens
        w2b = B(w2_sb.name)
        b2b = B(b2b_sb.name)
        dma("sp", b2b_sb[:], b2_in[:], b2b.begin_write(), "dma_t", b2b)
        w2_chunks = []
        xb_chunks = []

        for hf in range(2):
            x2t = B("x2t")

            load_set(SET_GELU)
            hb = B(h_sb.name)
            h_war = hb.begin_write()
            hb.ready = []
            hslots = [("psS0", psS, 0, 512), ("psS1", psS, 512, 512),
                      ("psS2", psS, 1024, 512), ("psS3", psS, 1536, 512),
                      ("psC0", psC[0], 0, 512), ("psC1", psC[1], 0, 512),
                      ("psC2", psC[2], 0, 512), ("psC3", psC[3], 0, 512)]
            hrot = Rot(hslots)
            for ft in range(FF // 128):
                wsl = ft % 2
                w1b = B(f"w1{wsl}")
                w1_war = w1b.begin_write()
                wtok = S.add("gp", lambda e, ft=ft, wsl=wsl: e.dma_start(
                    out=w1_buf[:, wsl * 1024:(wsl + 1) * 1024]
                    .rearrange("p (kc m) -> p kc m", kc=8),
                    in_=w1_in[:, ft * 128:(ft + 1) * 128]
                    .rearrange("(kc p) m -> p kc m", p=128)),
                    waits=list(w1_war) + [(f"dma_w1{wsl}", S.cnt[f"dma_w1{wsl}"])],
                    incs=[(f"dma_w1{wsl}", 16)])
                w1b.ready = [wtok]
                if hf == 0 and 2 <= ft < 10:
                    k = ft - 2
                    war = w2b.begin_write() if k == 0 else []
                    ctok = S.add("gp", lambda e, k=k: e.dma_start(
                        out=w2_sb[:, k * 4096:(k + 1) * 4096]
                        .rearrange("p (fc m) -> p fc m", fc=4),
                        in_=w2_in[k * 512:(k + 1) * 512, :]
                        .rearrange("(fc p) m -> p fc m", p=128)),
                        waits=list(war) + [("dma_w", S.cnt["dma_w"])],
                        incs=[("dma_w", 16)])
                    w2_chunks.append(ctok)
                    if k == 7:
                        w2b.ready = w2_chunks
                if hf == 0 and 10 <= ft < 18 and b + 1 < B_LOC:
                    k = ft - 10
                    xbn = B(f"xb{b + 1}")
                    war = xbn.begin_write() if k == 0 else []
                    ctok = S.add("gp", lambda e, k=k, nb=b + 1: e.dma_start(
                        out=xb_scr[nb][k * 256:(k + 1) * 256, :],
                        in_=x_in[nb, k * 256:(k + 1) * 256, :]),
                        waits=list(war) + [("dma_xg", S.cnt["dma_xg"])],
                        incs=[("dma_xg", 16)])
                    xb_chunks.append(ctok)
                    if k == 7:
                        xbn.ready = xb_chunks
                        xt_prefetch.add(b + 1)
                slots2 = [hrot.next(), hrot.next()]
                bufs2 = [B(s[0]) for s in slots2]
                wars2 = [bb.begin_write() for bb in bufs2]
                aps2 = [s[1][:, s[2]:s[2] + s[3]] for s in slots2]
                mtoks2 = [None, None]
                for kc in range(8):
                    for tc2 in range(2):
                        w_ = list(([wtok] + x2t.ready))
                        if kc == 0:
                            w_ += wars2[tc2]
                            if ft == 0 and tc2 == 0:
                                w_ += list(h_war)
                        mtoks2[tc2] = S.add(
                            "te",
                            lambda e, ap=aps2[tc2], l=w1_buf[:, wsl * 1024 + kc * 128: wsl * 1024 + kc * 128 + 128],
                            r=x2t_sb[:, kc * 1024 + tc2 * 512: kc * 1024 + tc2 * 512 + 512],
                            st=(kc == 0), sp=(kc == 7): e.matmul(ap, l, r, start=st, stop=sp),
                            waits=w_, incs=("te",) if kc == 7 else ())
                for tc2 in range(2):
                    bufs2[tc2].ready = [mtoks2[tc2]]
                    w1b.users.append(mtoks2[tc2])
                    x2t.users.append(mtoks2[tc2])
                    gtok = S.add("se", lambda e, ap=aps2[tc2], ft=ft, tc2=tc2: e.activation(
                        h_sb[:, ft * 1024 + tc2 * 512: ft * 1024 + tc2 * 512 + 512], ap,
                        AF.Gelu_apprx_tanh, bias=b1_sb_global[:, ft:ft + 1]),
                        waits=[mtoks2[tc2]] + B(b1_sb_global.name).ready, incs=("se",))
                    bufs2[tc2].users.append(gtok)
                    hb.ready.append(gtok)

            load_set(SET_LNEXP)
            fac = B("fac")
            fac_war = fac.begin_write()
            fac_toks = []
            for tbl in range(8):
                rbslot = ("psC0", psC[0], 0, 512)
                pieces = [(x2t_sb[:, kc * 1024 + tbl * 128: kc * 1024 + tbl * 128 + 128],
                           wr_sb_global[:, kc * 8:(kc + 1) * 8],
                           x2t.ready + B(wr_sb_global.name).ready) for kc in range(8)]
                name, tensor, coloff, _ = rbslot
                pbufR = B(name)
                warR = pbufR.begin_write()
                apR = tensor[:, 0:8]
                mtokR = None
                for j, (lhs, rhs, ws) in enumerate(pieces):
                    mtokR = S.add("te", lambda e, apR=apR, lhs=lhs, rhs=rhs,
                                  st=(j == 0), sp=(j == 7): e.matmul(apR, lhs, rhs, start=st, stop=sp),
                                  waits=(list(ws) + warR) if j == 0 else list(ws),
                                  incs=("te",) if j == 7 else ())
                pbufR.ready = [mtokR]
                x2t.users.append(mtokR)
                B(wr_sb_global.name).users.append(mtokR)
                ro = 32 + tbl * 2
                lgc = 64 + (tbl % 2) * 32
                t0 = S.add("ve", lambda e, lgc=lgc: e.tensor_copy(aux[:, lgc:lgc + 8], psC[0][:, 0:8]),
                           waits=[mtokR] + B(f"rt{tbl % 2}").begin_write(), incs=("ve",))
                pbufR.users.append(t0)
                t1 = S.add("ve", lambda e, lgc=lgc: e.reduce_max(
                    aux[:, lgc + 8:lgc + 9], aux[:, lgc:lgc + 8], axis=mybir.AxisListType.X),
                    waits=[t0], incs=("ve",))
                t2 = S.add("ve", lambda e, lgc=lgc: e.tensor_scalar(
                    aux[:, lgc + 9:lgc + 17], aux[:, lgc:lgc + 8], aux[:, lgc + 8:lgc + 9],
                    -1e30, ALU.is_ge, op1=ALU.mult),
                    waits=[t1], incs=("ve",))
                t3 = S.add("ve", lambda e, lgc=lgc: e.tensor_tensor(
                    aux[:, lgc + 9:lgc + 17], aux[:, lgc:lgc + 8], aux[:, lgc + 9:lgc + 17], ALU.add),
                    waits=[t2], incs=("ve",))
                t4 = S.add("ve", lambda e, lgc=lgc: e.reduce_max(
                    aux[:, lgc + 17:lgc + 18], aux[:, lgc + 9:lgc + 17], axis=mybir.AxisListType.X),
                    waits=[t3], incs=("ve",))
                t5 = S.add("ve", lambda e, lgc=lgc: e.tensor_tensor(
                    aux[:, lgc + 18:lgc + 19], aux[:, lgc + 8:lgc + 9], aux[:, lgc + 17:lgc + 18],
                    ALU.subtract),
                    waits=[t4], incs=("ve",))
                t6 = S.add("se", lambda e, lgc=lgc: e.activation(
                    aux[:, lgc + 19:lgc + 20], aux[:, lgc + 18:lgc + 19], AF.Exp, scale=-1.0),
                    waits=[t5], incs=("se",))
                t7 = S.add("se", lambda e, lgc=lgc: e.activation(
                    aux[:, lgc + 20:lgc + 21], aux[:, lgc + 18:lgc + 19], AF.Exp),
                    waits=[t5], incs=("se",))
                t8 = S.add("ve", lambda e, lgc=lgc: e.tensor_scalar(
                    aux[:, lgc + 21:lgc + 22], aux[:, lgc + 19:lgc + 20], 1.0, None, ALU.add),
                    waits=[t6], incs=("ve",))
                t9 = S.add("ve", lambda e, lgc=lgc: e.reciprocal(
                    aux[:, lgc + 22:lgc + 23], aux[:, lgc + 21:lgc + 22]),
                    waits=[t8], incs=("ve",))
                t10 = S.add("ve", lambda e, lgc=lgc: e.tensor_scalar(
                    aux[:, lgc + 23:lgc + 24], aux[:, lgc + 20:lgc + 21], 1.0, None, ALU.add),
                    waits=[t7], incs=("ve",))
                t11 = S.add("ve", lambda e, lgc=lgc: e.reciprocal(
                    aux[:, lgc + 24:lgc + 25], aux[:, lgc + 23:lgc + 24]),
                    waits=[t10], incs=("ve",))
                t12 = S.add("ve", lambda e, lgc=lgc, tbl=tbl: e.tensor_tensor(
                    aux[:, 240 + tbl:241 + tbl], aux[:, lgc + 22:lgc + 23],
                    aux[:, lgc + 24:lgc + 25], ALU.add),
                    waits=[t9, t11] + (fac_war if tbl == 0 else []), incs=("ve",))
                B(f"rt{tbl % 2}").ready = [t12]
                fac_toks.append(t12)
            fac.ready = fac_toks
            if hf == 0:
                emit_x2t(b, 1, x2t_sb)
            elif b + 1 in xt_prefetch:
                xtb = B("xt")
                xt_war = xtb.begin_write()
                toks = []
                for et in range(8):
                    tok = S.add("sp", lambda e, nb=b + 1, et=et: e.dma_start(
                        out=xt_buf_g[:, et * 512: et * 512 + 512],
                        in_=xb_scr[nb][0:512, et * 128:(et + 1) * 128],
                        transpose=True),
                        waits=(xt_war if et == 0 else []) + B(f"xb{b + 1}").ready,
                        incs=[("dma_xt", 16)])
                    toks.append(tok)
                xtb.ready = toks

            for g in range(2):
                yslots = [[("psS0", psS, 0, 512), ("psS1", psS, 512, 512)],
                          [("psS2", psS, 1024, 512), ("psS3", psS, 1536, 512)],
                          [("psC0", psC[0], 0, 512), ("psC1", psC[1], 0, 512)],
                          [("psC2", psC[2], 0, 512), ("psC3", psC[3], 0, 512)]]
                ywar = []
                yaps = []
                for j in range(4):
                    aps = []
                    for ec in range(2):
                        nm, tsr, co, wd_ = yslots[j][ec]
                        bb = B(nm)
                        ywar.append((j, ec, bb, bb.begin_write()))
                        aps.append(tsr[:, co:co + wd_])
                    yaps.append(aps)
                ytoks = [[None, None] for _ in range(4)]
                nfc = FF // 128
                for fc in range(nfc):
                    for j in range(4):
                        tbl = g * 4 + j
                        for ec in range(2):
                            war = []
                            if fc == 0:
                                for (jj, eecc, bb, w) in ywar:
                                    if jj == j and eecc == ec:
                                        war = w
                            tok = S.add("te", lambda e, j=j, ec=ec, fc=fc, tbl=tbl:
                                        e.matmul(
                                            yaps_g[j][ec],
                                            h_sb[:, fc * 1024 + tbl * 128: fc * 1024 + tbl * 128 + 128],
                                            w2_sb[:, fc * E + ec * 512: fc * E + ec * 512 + 512],
                                            start=(fc == 0), stop=(fc == nfc - 1)),
                                        waits=(hb.ready + w2b.ready + war) if fc == 0 else [],
                                        incs=("te",) if fc == nfc - 1 else ())
                            if fc == nfc - 1:
                                ytoks[j][ec] = tok
                                hb.users.append(tok)
                                w2b.users.append(tok)
                xtoks = {}
                for j in (0, 1):
                    sl = j % 2
                    trow = hf * 1024 + (g * 4 + j) * 128
                    xfr = B(f"x2fr{sl}")
                    xfr_war = xfr.begin_write()
                    xtoks[j] = dma("sp", x2fr_buf[:, sl * E:(sl + 1) * E],
                                   x2f_scr[b][trow:trow + 128, :],
                                   xfr_war + B(f"x2fs{b}").ready, f"dma_fr{sl}", xfr)
                for j in range(4):
                    ty = B(f"t1y{j}")
                    ty_war = ty.begin_write()
                    ty.ready = []
                    for ec in range(2):
                        ttok = S.add("ve", lambda e, j=j, ec=ec: e.tensor_tensor(
                            t1y_buf[:, j * E + ec * 512: j * E + ec * 512 + 512],
                            yaps_g[j][ec], b2b_sb[:, ec * 512:(ec + 1) * 512], ALU.add),
                            waits=[ytoks[j][ec]] + b2b.ready + (ty_war if ec == 0 else []),
                            incs=("ve",))
                        b2b.users.append(ttok)
                        for (jj, eecc, bb, w) in ywar:
                            if jj == j and eecc == ec:
                                bb.users.append(ttok)
                        ty.ready.append(ttok)
                for j in range(4):
                    tbl = g * 4 + j
                    trow = hf * 1024 + tbl * 128
                    sl = j % 2
                    if j == 2:
                        for jj in (2, 3):
                            sll = jj % 2
                            trw = hf * 1024 + (g * 4 + jj) * 128
                            xfr2 = B(f"x2fr{sll}")
                            xfr2_war = xfr2.begin_write()
                            xtoks[jj] = dma("sp", x2fr_buf[:, sll * E:(sll + 1) * E],
                                            x2f_scr[b][trw:trw + 128, :],
                                            xfr2_war + B(f"x2fs{b}").ready,
                                            f"dma_fr{sll}", xfr2)
                    xfr = B(f"x2fr{sl}")
                    xtok = xtoks[j]
                    ty = B(f"t1y{j}")
                    ftok = S.add("ve", lambda e, j=j, tbl=tbl: e.tensor_scalar(
                        t1y_buf[:, j * E:(j + 1) * E], t1y_buf[:, j * E:(j + 1) * E],
                        aux[:, 240 + tbl:241 + tbl], None, ALU.mult),
                        waits=ty.ready + fac.ready, incs=("ve",))
                    ty.users.append(ftok)
                    fac.users.append(ftok)
                    rtoks = []
                    for ec in range(2):
                        rtok = S.add("ve", lambda e, j=j, sl=sl, ec=ec: e.tensor_tensor(
                            t1y_buf[:, j * E + ec * 512: j * E + ec * 512 + 512],
                            t1y_buf[:, j * E + ec * 512: j * E + ec * 512 + 512],
                            x2fr_buf[:, sl * E + ec * 512: sl * E + ec * 512 + 512], ALU.add),
                            waits=[ftok, xtok], incs=("ve",))
                        xfr.users.append(rtok)
                        rtoks.append(rtok)
                    ty.ready = rtoks
                    rstd_ap, nmr_ap, st_toks = ln_stats(rtoks, t1y_buf[:, j * E:(j + 1) * E], sl, aux)
                    ob = B(f"outb{sl}")
                    ob_war = ob.begin_write()
                    otok = S.add("ve", lambda e, j=j, sl=sl, r=rstd_ap, nm=nmr_ap: e.tensor_scalar(
                        outb_buf[:, sl * E:(sl + 1) * E], t1y_buf[:, j * E:(j + 1) * E],
                        r, nm, ALU.mult, op1=ALU.add),
                        waits=st_toks + rtoks + ob_war, incs=("ve",))
                    ty.users.append(otok)
                    ob.ready = [otok]
                    stok = S.add("sp", lambda e, b=b, trow=trow, sl=sl: e.dma_start(
                        out=out_d[b, trow:trow + 128, :], in_=outb_buf[:, sl * E:(sl + 1) * E]),
                        waits=[otok], incs=[(f"dma_o{sl}", 16)])
                    ob.users.append(stok)
        drain_dmas()

    global b1_sb_global, wr_sb_global, yaps_g
    with nc.sbuf_tensor("b1_sb", [128, FF // 128], F32) as b1_sb, \
         nc.sbuf_tensor("wr_sb", [128, 64], BF16) as wr_sb, \
         nc.sbuf_tensor("aux", [128, 512], F32) as aux, \
         nc.sbuf_tensor("pswap_sb", [128, 128], BF16) as pswap_sb_g, \
         nc.sbuf_tensor("wd_sb", [128, 8 * LAT], BF16) as wd_sb_g, \
         nc.sbuf_tensor("xt_buf", [128, 8 * 512], BF16, is_transpose_dest=True) as xt_buf_g:
        b1_sb_global = b1_sb
        wr_sb_global = wr_sb
        S.new_phase()
        bb = B(b1_sb.name)
        dma("sp", b1_sb[:], b1_in[:], bb.begin_write(), "dma_b1", bb)
        bb = B(wr_sb.name)
        dma("gp", wr_sb[:].rearrange("p (kc m) -> p kc m", kc=8),
            wr_in.rearrange("(kc p) m -> p kc m", p=128),
            bb.begin_write(), "dma_w", bb)
        epsb = B("epsap")
        etok = S.add("ve", lambda e: e.memset(aux[:, 255:256], EPS), incs=("ve",))
        epsb.ready = [etok]
        bb = B(wd_sb_g.name)
        dma("gp", wd_sb_g[:].rearrange("p (kc m) -> p kc m", kc=8),
            wd_in.rearrange("(kc p) m -> p kc m", p=128), bb.begin_write(), "dma_w", bb)
        bb = B(pswap_sb_g.name)
        dma("gp", pswap_sb_g[:], psw_in[:], bb.begin_write(), "dma_w", bb)
        xbb = B("xb0")
        dma("gp", xb_scr[0][:], x_in[0], xbb.begin_write(), "dma_xg", xbb)
        S.emit_phase()

        for b in range(B_LOC):
            xstk = ExitStack()
            bstk = ExitStack()
            x2t_sb = xstk.enter_context(
                nc.sbuf_tensor(f"x2t_sb{b}", [128, 8 * 1024], BF16, is_transpose_dest=True))
            if True:
                ben = bstk.enter_context
                kr_sb = ben(nc.sbuf_tensor(f"kr_sb{b}", [128, 8 * T], BF16))
                vaug_sb = ben(nc.sbuf_tensor(f"vaug_sb{b}", [128, 16 * H * 128], BF16))
                with ExitStack() as stk:
                    en = stk.enter_context
                    wq_sb = en(nc.sbuf_tensor(f"wq_sb{b}", [128, 8 * E], BF16))
                    wd_sb = wd_sb_g
                    wk_sb = en(nc.sbuf_tensor(f"wk_sb{b}", [128, 4 * E], BF16))
                    wv_sb = en(nc.sbuf_tensor(f"wv_sb{b}", [128, 4 * E], BF16))
                    ct_sb = en(nc.sbuf_tensor(f"ct_sb{b}", [128, 2 * 512], F32))
                    st_sb = en(nc.sbuf_tensor(f"st_sb{b}", [128, 2 * 512], F32))
                    xt_buf = xt_buf_g
                    lat_buf = en(nc.sbuf_tensor(f"lat_buf{b}", [128, 2 * 4 * 512], BF16))
                    qsp_buf = en(nc.sbuf_tensor(f"qsp_buf{b}", [128, 2 * 512], BF16))
                    qbf_tmp = en(nc.sbuf_tensor(f"qbf_tmp{b}", [128, 2 * 512], BF16))
                    t1_tmp = en(nc.sbuf_tensor(f"t1_tmp{b}", [128, 2 * 512], F32))
                    t2_tmp = en(nc.sbuf_tensor(f"t2_tmp{b}", [128, 2 * 512], F32))
                    S.new_phase()
                    proj_phase(b, (wq_sb, wd_sb, wk_sb, wv_sb, ct_sb, st_sb,
                                   kr_sb, vaug_sb, xt_buf, lat_buf, qsp_buf,
                                   pswap_sb_g, aux, qbf_tmp, t1_tmp, t2_tmp))
                    S.emit_phase()
                with ExitStack() as stk:
                    en = stk.enter_context
                    wo_sb = en(nc.sbuf_tensor(f"wo_sb{b}", [128, 8 * E], BF16))
                    qch_buf = en(nc.sbuf_tensor(f"qch_buf{b}", [128, 2 * 8 * QC], BF16))
                    pt_buf = en(nc.sbuf_tensor(f"pt_buf{b}", [128, 2 * 2048], BF16))
                    rb_sb = en(nc.sbuf_tensor(f"rb_sb{b}", [64, 4 * 512], F32))
                    rb_scr = en(nc.sbuf_tensor(f"rb_scr{b}", [64, 2 * 512], F32))
                    ao_sb = en(nc.sbuf_tensor(f"ao_sb{b}", [128, 8 * 512], BF16))
                    xres_buf = en(nc.sbuf_tensor(f"xres_buf{b}", [128, 2 * E], F32))
                    rs1_buf = en(nc.sbuf_tensor(f"rs1_buf{b}", [128, 2 * E], F32))
                    S.new_phase()
                    attn_phase(b, (wo_sb, kr_sb, vaug_sb, qch_buf, pt_buf,
                                   rb_sb, rb_scr, ao_sb, xres_buf, rs1_buf,
                                   x2t_sb, aux))
                    S.emit_phase()
            bstk.close()
            with ExitStack() as stk:
                en = stk.enter_context
                w2_sb = en(nc.sbuf_tensor(f"w2_sb{b}", [128, 32 * E], BF16))
                h_sb = en(nc.sbuf_tensor(f"h_sb{b}", [128, 32 * 1024], BF16))
                w1_buf = en(nc.sbuf_tensor(f"w1_buf{b}", [128, 2 * 1024], BF16))
                b2b_sb = en(nc.sbuf_tensor(f"b2b_sb{b}", [128, E], F32))
                x2fr_buf = en(nc.sbuf_tensor(f"x2fr_buf{b}", [128, 2 * E], F32))
                t1y_buf = en(nc.sbuf_tensor(f"t1y_buf{b}", [128, 4 * E], F32))
                outb_buf = en(nc.sbuf_tensor(f"outb_buf{b}", [128, 2 * E], F32))
                S.new_phase()
                yaps_g = [[psS[:, 0:512], psS[:, 512:1024]],
                          [psS[:, 1024:1536], psS[:, 1536:2048]],
                          [psC[0][:, :], psC[1][:, :]],
                          [psC[2][:, :], psC[3][:, :]]]
                ffn_phase(b, (w2_sb, x2t_sb, h_sb, w1_buf, b2b_sb, x2fr_buf,
                              t1y_buf, outb_buf, aux))
                S.emit_phase()
            xstk.close()

    nc.finalize()
    return nc


def host_consts():
    inv = 1.0 / (10000.0 ** (np.arange(0, D, 2, dtype=np.float32) / D))
    ang = np.outer(np.arange(T, dtype=np.float32), inv)
    cos = np.cos(ang)
    sin = np.sin(ang)
    r = np.arange(128)
    i = (r % 64) // 2
    ctab = cos[:, i].T.astype(np.float32).copy()
    sgn = np.where(r % 2 == 0, -1.0, 1.0).astype(np.float32)
    stab = (sin[:, i].T * sgn[:, None]).astype(np.float32).copy()
    perm = np.arange(128) ^ 1
    pswap = np.zeros((128, 128), dtype=np.float32)
    pswap[perm, np.arange(128)] = 1.0
    return ctab, stab, pswap


_prog_cache = {}


def kernel(L0_attn_x, L0_attn_wq, L0_attn_w_down, L0_attn_wk_up, L0_attn_wv_up,
           L0_attn_wo, L0_ffn_router_w, L0_ffn_w1, L0_ffn_b1, L0_ffn_w2, L0_ffn_b2):
    x = np.ascontiguousarray(np.asarray(L0_attn_x, dtype=np.float32))
    ctab, stab, pswap = host_consts()
    common = {
        "wq": np.ascontiguousarray(np.asarray(L0_attn_wq, dtype=np.float32)),
        "wd": np.ascontiguousarray(np.asarray(L0_attn_w_down, dtype=np.float32)),
        "wk": np.ascontiguousarray(np.asarray(L0_attn_wk_up, dtype=np.float32)),
        "wv": np.ascontiguousarray(np.asarray(L0_attn_wv_up, dtype=np.float32)),
        "wo": np.ascontiguousarray(np.asarray(L0_attn_wo, dtype=np.float32)),
        "wr": np.ascontiguousarray(np.asarray(L0_ffn_router_w, dtype=np.float32)),
        "w1": np.ascontiguousarray(np.asarray(L0_ffn_w1, dtype=np.float32)),
        "w2": np.ascontiguousarray(np.asarray(L0_ffn_w2, dtype=np.float32)),
        "b1l": np.ascontiguousarray(np.asarray(L0_ffn_b1, dtype=np.float32).reshape(FF // 128, 128).T),
        "b2b": np.ascontiguousarray(np.broadcast_to(np.asarray(L0_ffn_b2, dtype=np.float32), (128, E))),
        "ctab": ctab, "stab": stab, "pswap": pswap,
    }
    if "nc" not in _prog_cache:
        _prog_cache["nc"] = build_program()
    nc = _prog_cache["nc"]
    in_maps = []
    for c in range(N_CORES):
        m = dict(common)
        m["x"] = np.ascontiguousarray(x[c * B_LOC:(c + 1) * B_LOC])
        in_maps.append(m)
    res = run_bass_kernel_spmd(nc, in_maps, core_ids=list(range(N_CORES)))
    return np.concatenate([r["out"] for r in res.results], axis=0)
